# version 12
# baseline (speedup 1.0000x reference)
import math
from contextlib import ExitStack

import numpy as np
import ml_dtypes
import concourse.bass as bass
import concourse.mybir as mybir
from concourse.bass_utils import run_bass_kernel_spmd
from concourse.alu_op_type import AluOpType as ALU

F32 = mybir.dt.float32
BF16 = mybir.dt.bfloat16
I32 = mybir.dt.int32
AF = mybir.ActivationFunctionType
AXX = mybir.AxisListType.X

S = 4096
DM = 2048
HD = 8192
CH = 512
NCH = S // CH
NDM = DM // 128
NH = HD // 128
ND = DM // 128
NQH = 16
NKVH = 4
DH = 128
KV = 512
PI = float(np.pi)
INVSQ = 1.0 / math.sqrt(DH)
EPS_RMS = 1e-6
EPS_LN = 1e-5
FWHT_SCALE = 1.0 / math.sqrt(HD)

ENGS = ["sync", "gpsimd", "vector", "scalar", "tensor"]


class Sem:
    def __init__(self, handle, name):
        self.h = handle
        self.name = name
        self.n = 0


class Rec:

    def __init__(self):
        self.ops = {e: [] for e in ENGS}

    def o(self, eng, fn, waits=(), inc=None, incn=1):
        w = tuple((s, v) for (s, v) in waits if s is not None and v > 0)
        self.ops[eng].append((w, fn, inc, incn))
        if inc is not None:
            inc.n += incn
            return inc.n
        return None

    def emit(self, block):
        decs = {"sync": block.sync, "gpsimd": block.gpsimd,
                "vector": block.vector, "scalar": block.scalar,
                "tensor": block.tensor}
        for eng in ENGS:
            oplist = self.ops[eng]
            if not oplist:
                continue

            @decs[eng]
            def _(e, oplist=oplist):
                seen = {}
                for (waits, fn, inc, incn) in oplist:
                    for (s, v) in waits:
                        if seen.get(s.name, 0) < v:
                            e.wait_ge(s.h, v)
                            seen[s.name] = v
                    inst = fn(e)
                    if inc is not None:
                        inst.then_inc(inc.h, incn)


def build_program():
    nc = bass.Bass("TRN2", target_bir_lowering=False)

    xT = nc.dram_tensor("xT", [DM, S], F32, kind="ExternalInput").ap()
    wq = nc.dram_tensor("wq", [NQH, 128, DM], F32, kind="ExternalInput").ap()
    wkh = nc.dram_tensor("wkh", [NKVH, 128, DM], F32, kind="ExternalInput").ap()
    wvh = nc.dram_tensor("wvh", [128, NDM, KV], F32, kind="ExternalInput").ap()
    wo = nc.dram_tensor("wo", [NDM, 128, DM], F32, kind="ExternalInput").ap()
    w1 = nc.dram_tensor("w1", [NH, 128, DM], F32, kind="ExternalInput").ap()
    w3 = nc.dram_tensor("w3", [NH, 128, DM], F32, kind="ExternalInput").ap()
    w2sl = nc.dram_tensor("w2sl", [ND, 128, NH * 128], F32, kind="ExternalInput").ap()
    bqh = nc.dram_tensor("bqh", [128, NQH], F32, kind="ExternalInput").ap()
    bkh = nc.dram_tensor("bkh", [128, NKVH], F32, kind="ExternalInput").ap()
    bvr = nc.dram_tensor("bvr", [1, KV], F32, kind="ExternalInput").ap()
    lnwh = nc.dram_tensor("lnwh", [128, NDM], F32, kind="ExternalInput").ap()

    wqs = nc.dram_tensor("wqs", [NQH, 128, DM], BF16, kind="Internal").ap()
    wos = nc.dram_tensor("wos", [NDM, 128, DM], BF16, kind="Internal").ap()
    w1s = nc.dram_tensor("w1s", [NH, 128, DM], BF16, kind="Internal").ap()
    w3s = nc.dram_tensor("w3s", [NH, 128, DM], BF16, kind="Internal").ap()
    w2p = nc.dram_tensor("w2p", [ND, 128, HD], BF16, kind="Internal").ap()
    kscr = nc.dram_tensor("kscr", [NKVH, 128, S], BF16, kind="Internal").ap()
    vscr = nc.dram_tensor("vscr", [NKVH, S // 128, 128, 128], BF16,
                          kind="Internal").ap()
    x1scr = nc.dram_tensor("x1scr", [DM, S], F32, kind="Internal").ap()
    outT = nc.dram_tensor("outT", [DM, S], F32, kind="ExternalOutput").ap()

    rec = Rec()
    R = rec.o
    stk = ExitStack()

    with stk:
        sems = {}

        def sem(name):
            s = Sem(stk.enter_context(nc.semaphore(name)), name)
            sems[name] = s
            return s

        sp_ring = [sem(f"sp{i}") for i in range(4)]
        sp = sp_ring[0]
        sa = sem("sa")
        sv = sem("sv")
        sq = sem("sq")
        sd_x = sem("sd_x")
        s_kvst = sem("s_kvst")
        s_spl = sem("s_spl")
        s_out = [sem("s_out0"), sem("s_out1")]
        sd_pre = sem("sd_pre")
        sg_res = sem("sg_res")
        sg_pre = [sem("sg_pre0"), sem("sg_pre1")]
        s_wff = sem("s_wff")
        s_qo = [sem("s_qo0"), sem("s_qo1")]
        s_13h = [[sem("s_13a0"), sem("s_13a1")],
                 [sem("s_13b0"), sem("s_13b1")]]
        s_w2 = [sem("s_w2a"), sem("s_w2b")]
        s_kv = [sem("s_kv0"), sem("s_kv1")]
        s_x1 = [sem("s_x1a"), sem("s_x1b")]
        s_y2w = [sem("s_y2w0"), sem("s_y2w1")]
        s_y2r = [sem("s_y2r0"), sem("s_y2r1")]
        s_cst = [sem("s_cst0"), sem("s_cst1")]

        def sb(name, shape, dt):
            return stk.enter_context(nc.sbuf_tensor(name, shape, dt))

        cos_t = sb("cos_t", [128, S], F32)
        sin_t = sb("sin_t", [128, S], F32)
        masks = sb("masks", [128, 4 * CH], BF16)
        wk_sb = sb("wk_sb", [128, NKVH * DM], BF16)
        wv_sb = sb("wv_sb", [128, NDM * KV], BF16)
        bq_sb = sb("bq_sb", [128, NQH], F32)
        bk_sb = sb("bk_sb", [128, NKVH], F32)
        bv_bc = sb("bv_bc", [128, KV], BF16)
        lnw_sb = sb("lnw_sb", [128, NDM], F32)
        ones_bf = sb("ones_bf", [128, 1], BF16)
        epsr = sb("epsr", [128, 1], F32)
        epsl = sb("epsl", [128, 1], F32)
        ones_f = sb("ones_f", [1, 128], F32)
        h128 = sb("h128", [128, 128], BF16)
        h128n = sb("h128n", [128, 128], BF16)
        kmax2 = sb("kmax2", [1, NKVH], F32)
        irow = sb("irow", [1, 64], F32)
        bias_h = sb("bias_h", [128, 2], F32)
        stat_m = sb("stat_m", [1, CH], F32)
        stat_a = sb("stat_a", [1, CH], F32)
        stat_t = sb("stat_t", [1, 8], F32)

        PS = [stk.enter_context(nc.psum_tensor(f"ps{i}", [128, 512], F32))
              for i in range(8)]
        pr = [(None, 0)] * 8

        with ExitStack() as pre1:
            def p1(name, shape, dt):
                return pre1.enter_context(nc.sbuf_tensor(name, shape, dt))

            i_i32 = p1("i_i32", [128, 1], I32)
            j_i32 = p1("j_i32", [128, 128], I32)
            h_i32 = p1("h_i32", [128, 128], I32)
            s_i32 = p1("s_i32", [128, 128], I32)
            fgrid = p1("fgrid", [128, 4 * CH], F32)
            trow = p1("trow", [1, S], F32)
            ang_t = p1("ang_t", [64, S], F32)
            red_t = p1("red_t", [64, S], F32)
            kf_t = p1("kf_t", [64, S], F32)
            ki_t = p1("ki_t", [64, S], I32)
            ang = ang_t[:]
            red = red_t[:]
            kf = kf_t[:]
            ki = ki_t[:]

            R("gpsimd", lambda e: e.iota(i_i32[:], [[0, 1]], base=0,
                                         channel_multiplier=1))
            R("gpsimd", lambda e: e.iota(j_i32[:], [[1, 128]], base=0,
                                         channel_multiplier=0))
            R("gpsimd", lambda e: e.iota(fgrid[:], [[1, 4 * CH]], base=0,
                                         channel_multiplier=-1,
                                         allow_small_or_imprecise_dtypes=True))
            R("gpsimd", lambda e: e.iota(trow[:], [[1, S]], base=0,
                                         channel_multiplier=0,
                                         allow_small_or_imprecise_dtypes=True))
            R("gpsimd", lambda e: e.iota(irow[:], [[1, 64]], base=0,
                                         channel_multiplier=0,
                                         allow_small_or_imprecise_dtypes=True))
            R("gpsimd", lambda e: e.memset(ones_bf[:], 1.0))
            R("gpsimd", lambda e: e.memset(epsr[:], EPS_RMS))
            R("gpsimd", lambda e: e.memset(epsl[:], EPS_LN))
            R("gpsimd", lambda e: e.memset(kmax2[:], 0.0))
            t_iota = R("gpsimd", lambda e: e.memset(ones_f[:], 1.0), inc=sq)

            for jr in range(4):
                R("vector", lambda e, jr=jr: e.tensor_scalar(
                    out=masks[:, jr * CH:(jr + 1) * CH],
                    in0=fgrid[:, jr * CH:(jr + 1) * CH],
                    scalar1=float(jr * (CH + 128)), scalar2=None,
                    op0=ALU.is_ge), waits=[(sq, t_iota)])

            t0 = R("vector", lambda e: e.tensor_scalar(
                out=h_i32[:], in0=j_i32[:], scalar1=i_i32[:], scalar2=None,
                op0=ALU.bitwise_and), waits=[(sq, t_iota)], inc=sv)
            for sh in (4, 2, 1):
                t0 = R("vector", lambda e, sh=sh: e.tensor_scalar(
                    out=s_i32[:], in0=h_i32[:], scalar1=sh, scalar2=None,
                    op0=ALU.logical_shift_right), waits=[(sv, t0)], inc=sv)
                t0 = R("vector", lambda e: e.tensor_tensor(
                    out=h_i32[:], in0=h_i32[:], in1=s_i32[:],
                    op=ALU.bitwise_xor), waits=[(sv, t0)], inc=sv)
            t0 = R("vector", lambda e: e.tensor_scalar(
                out=h_i32[:], in0=h_i32[:], scalar1=1, scalar2=None,
                op0=ALU.bitwise_and), waits=[(sv, t0)], inc=sv)
            t0 = R("vector", lambda e: e.tensor_copy(
                out=fgrid[:, 0:128], in_=h_i32[:]), waits=[(sv, t0)], inc=sv)
            t_h128 = R("vector", lambda e: e.tensor_scalar(
                out=h128[:], in0=fgrid[:, 0:128], scalar1=-2.0, scalar2=1.0,
                op0=ALU.mult, op1=ALU.add), waits=[(sv, t0)], inc=sv)
            t_h128n = R("vector", lambda e: e.tensor_scalar(
                out=h128n[:], in0=fgrid[:, 0:128], scalar1=2.0, scalar2=-1.0,
                op0=ALU.mult, op1=ALU.add), waits=[(sv, t0)], inc=sv)

            t1 = R("scalar", lambda e: e.activation(
                out=irow[:], in_=irow[:], func=AF.Exp,
                scale=-math.log(10000.0) / 64.0), waits=[(sq, t_iota)], inc=sa)
            ang_toks = []
            for sl in range(S // 512):
                b = sl % 2
                tmm = R("tensor", lambda e, sl=sl, b=b: e.matmul(
                    PS[b][0:64, :], irow[:, :],
                    trow[:, sl * 512:(sl + 1) * 512], start=True, stop=True),
                    waits=[(sa, t1), (sq, t_iota), pr[b]], inc=sp)
                tev = R("vector", lambda e, sl=sl, b=b: e.tensor_copy(
                    out=ang[:, sl * 512:(sl + 1) * 512], in_=PS[b][0:64, :]),
                    waits=[(sp, tmm)], inc=sv)
                pr[b] = (sv, tev)
            t_ang = sv.n
            C1 = float(np.float32(6.28125))
            C2 = float(np.float32(0.0019353069))
            C3 = float(2 * PI - np.float64(np.float32(6.28125))
                       - np.float64(np.float32(0.0019353069)))

            def vchain(fn, t0):
                return R("vector", fn, waits=[(sv, t0)], inc=sv)

            t0 = vchain(lambda e: e.tensor_scalar(
                out=kf, in0=ang, scalar1=float(1.0 / (2 * PI)),
                scalar2=None, op0=ALU.mult), t_ang)
            t0 = vchain(lambda e: e.tensor_copy(out=ki, in_=kf), t0)
            t0 = vchain(lambda e: e.tensor_copy(out=kf, in_=ki), t0)
            t0 = vchain(lambda e: e.scalar_tensor_tensor(
                out=red, in0=kf, scalar=-C1, in1=ang,
                op0=ALU.mult, op1=ALU.add), t0)
            t0 = vchain(lambda e: e.scalar_tensor_tensor(
                out=red, in0=kf, scalar=-C2, in1=red,
                op0=ALU.mult, op1=ALU.add), t0)
            t0 = vchain(lambda e: e.scalar_tensor_tensor(
                out=red, in0=kf, scalar=-C3, in1=red,
                op0=ALU.mult, op1=ALU.add), t0)
            t0 = vchain(lambda e: e.tensor_scalar(
                out=kf, in0=red, scalar1=PI, scalar2=None,
                op0=ALU.is_gt), t0)
            t0 = vchain(lambda e: e.scalar_tensor_tensor(
                out=red, in0=kf, scalar=-2 * PI, in1=red,
                op0=ALU.mult, op1=ALU.add), t0)
            t0 = vchain(lambda e: e.tensor_scalar(
                out=kf, in0=red, scalar1=-PI, scalar2=None,
                op0=ALU.is_lt), t0)
            t0 = vchain(lambda e: e.scalar_tensor_tensor(
                out=red, in0=kf, scalar=2 * PI, in1=red,
                op0=ALU.mult, op1=ALU.add), t0)
            t_sin = R("scalar", lambda e: e.activation(
                out=sin_t[0:64, :], in_=red, func=AF.Sin),
                waits=[(sv, t0), (sa, t1)], inc=sa)
            t_sin2 = R("vector", lambda e: e.tensor_copy(
                out=sin_t[64:128, :], in_=sin_t[0:64, :]),
                waits=[(sa, t_sin)], inc=sv)
            t0 = vchain(lambda e: e.tensor_scalar(
                out=ang, in0=red, scalar1=PI / 2, scalar2=None,
                op0=ALU.add), t0)
            t0 = vchain(lambda e: e.tensor_scalar(
                out=kf, in0=ang, scalar1=PI, scalar2=None,
                op0=ALU.is_gt), t0)
            t0 = vchain(lambda e: e.scalar_tensor_tensor(
                out=ang, in0=kf, scalar=-2 * PI, in1=ang,
                op0=ALU.mult, op1=ALU.add), t0)
            t_cos = R("scalar", lambda e: e.activation(
                out=cos_t[0:64, :], in_=ang, func=AF.Sin),
                waits=[(sv, t0), (sa, t_sin)], inc=sa)
            t_cos2 = R("vector", lambda e: e.tensor_copy(
                out=cos_t[64:128, :], in_=cos_t[0:64, :]),
                waits=[(sa, t_cos)], inc=sv)
            t_rope_tab = max(t_sin2, t_cos2)

        t_p1 = [(sv, sv.n), (sa, sa.n), (sp, sp.n), (sq, sq.n)]

        with ExitStack() as pre1:
            def p1(name, shape, dt):
                return pre1.enter_context(nc.sbuf_tensor(name, shape, dt))

            for (dst, src) in [(bq_sb, bqh), (bk_sb, bkh), (lnw_sb, lnwh)]:
                R("gpsimd", lambda e, dst=dst, src=src: e.dma_start(
                    out=dst[:], in_=src[:, :]), inc=sg_res, incn=16)
            R("gpsimd", lambda e: e.dma_start(
                out=stat_m[:, 0:KV], in_=bvr[:, :]), inc=sg_res, incn=16)
            R("gpsimd", lambda e: e.dma_start(
                out=wk_sb[:].rearrange("p (g d) -> p g d", d=DM),
                in_=wkh.rearrange("g p d -> p g d")),
              inc=sg_res, incn=16)
            t_wkv = R("gpsimd", lambda e: e.dma_start(
                out=wv_sb[:], in_=wvh.rearrange("p i e -> p (i e)")),
                inc=sg_res, incn=16)
            tb = R("tensor", lambda e: e.matmul(
                PS[2][:, :], ones_f[:, :], stat_m[:, 0:KV],
                start=True, stop=True),
                waits=[(sg_res, t_wkv), (sq, t_iota), pr[2]], inc=sp)
            tb = R("vector", lambda e: e.tensor_copy(
                out=bv_bc[:], in_=PS[2][:, :]), waits=[(sp, tb)], inc=sv)
            pr[2] = (sv, tb)

            castbuf = [p1(f"castb{i}", [128, DM], BF16) for i in range(2)]
            cast_free = [list(t_p1), list(t_p1)]
            nblk = 0
            for (srcw, dstw, n) in [(wq, wqs, NQH), (wo, wos, NDM),
                                    (w1, w1s, NH), (w3, w3s, NH)]:
                for blk in range(n):
                    bsel = nblk % 2
                    tl = R("gpsimd", lambda e, srcw=srcw, blk=blk, bsel=bsel:
                           e.dma_start(out=castbuf[bsel][:],
                                       in_=srcw[blk, :, :]),
                           waits=(cast_free[bsel]
                                  if isinstance(cast_free[bsel], list)
                                  else [cast_free[bsel]]),
                           inc=sg_pre[bsel], incn=16)
                    ts_ = R("sync", lambda e, dstw=dstw, blk=blk, bsel=bsel:
                            e.dma_start(out=dstw[blk, :, :],
                                        in_=castbuf[bsel][:]),
                            waits=[(sg_pre[bsel], tl)], inc=s_cst[bsel],
                            incn=16)
                    cast_free[bsel] = [(s_cst[bsel], ts_)]
                    nblk += 1

            w2ff = p1("w2ff", [128, NH * 64], F32)
            w2bf = p1("w2bf", [128, NH * 128], BF16)
            w2out = p1("w2out", [128, NH * 128], BF16)
            Av = p1("Av", [128, NH * 128], F32)
            Bv = p1("Bv", [128, NH * 128], F32)
            w2ff_free = list(t_p1)
            t_st_prev = (None, 0)
            t_mm_last = 0
            tq_prev = t_p1[0][1]

            def bfly(eng, semx, src_, dst, bit, t0_tok):
                n_inner = bit * 128
                toks = []
                for (sign, outoff) in ((ALU.add, 0), (ALU.subtract, n_inner)):
                    toks.append(R(eng,
                                  lambda e, src_=src_, dst=dst,
                                  n_inner=n_inner, sign=sign,
                                  outoff=outoff: e.tensor_tensor(
                                      out=dst[:].rearrange(
                                          "p (q i) -> p q i",
                                          i=2 * n_inner)[
                                          :, :, outoff:outoff + n_inner],
                                      in0=src_[:].rearrange(
                                          "p (q i) -> p q i",
                                          i=2 * n_inner)[:, :, 0:n_inner],
                                      in1=src_[:].rearrange(
                                          "p (q i) -> p q i",
                                          i=2 * n_inner)[
                                          :, :, n_inner:2 * n_inner],
                                      op=sign),
                                  waits=[t0_tok], inc=semx))
                return toks[-1]

            for s_ in range(ND):
                tcasts = []
                for hfs in range(2):
                    tl = R("scalar", lambda e, s_=s_, hfs=hfs: e.dma_start(
                        out=w2ff[:],
                        in_=w2sl[s_, :, hfs * (NH * 64):(hfs + 1) * (NH * 64)]),
                        waits=(w2ff_free if isinstance(w2ff_free, list)
                               else [w2ff_free]), inc=s_wff, incn=16)
                    for pc in range(2):
                        tc = R("scalar", lambda e, pc=pc, hfs=hfs: e.copy(
                            out=w2bf[:, (hfs * 2 + pc) * 2048:
                                     (hfs * 2 + pc + 1) * 2048],
                            in_=w2ff[:, pc * 2048:(pc + 1) * 2048]),
                            waits=[(s_wff, tl), (sp, t_mm_last)], inc=sa)
                        tcasts.append(tc)
                    w2ff_free = [(sa, tcasts[-1])]
                t_ev = 0
                for og in range(16):
                    b = og % 2
                    hi = og >> 1
                    t_mm = 0
                    for m in range(8):
                        hi2 = hi ^ m
                        ig = (hi2 << 1) | (og & 1)
                        sign = bin(hi & hi2).count("1") & 1
                        lhs = h128n if sign else h128
                        w_mm = [(sa, tcasts[3]), (sv, t_h128n)]
                        if m == 0:
                            w_mm.append(pr[b])
                        t_mm = R("tensor",
                                 lambda e, ig=ig, lhs=lhs, b=b, m=m:
                                 e.matmul(PS[b][:, :], lhs[:, :],
                                          w2bf[:, ig * 512:(ig + 1) * 512],
                                          start=(m == 0), stop=(m == 7)),
                                 waits=w_mm, inc=sp)
                    t_ev = R("scalar", lambda e, og=og, b=b: e.copy(
                        out=Av[:, og * 512:(og + 1) * 512],
                        in_=PS[b][:, :]),
                        waits=[(sp, t_mm)] +
                        ([(sv, tq_prev)] if og < 2 else []),
                        inc=sa)
                    pr[b] = (sa, t_ev)
                t_mm_last = sp.n
                t_b1 = bfly("vector", sv, Av, Bv, 1, (sa, t_ev))
                t_b2 = bfly("vector", sv, Bv, Av, 2, (sv, t_b1))
                t_b4 = bfly("vector", sv, Av, Bv, 4, (sv, t_b2))
                tq_prev = t_b4
                tsc = R("vector", lambda e: e.tensor_scalar(
                    out=w2out[:, 0:NH * 64], in0=Bv[:, 0:NH * 64],
                    scalar1=FWHT_SCALE, scalar2=None, op0=ALU.mult),
                    waits=[(sv, t_b4), t_st_prev], inc=sv)
                tsc = R("vector", lambda e: e.tensor_scalar(
                    out=w2out[:, NH * 64:NH * 128],
                    in0=Bv[:, NH * 64:NH * 128],
                    scalar1=FWHT_SCALE, scalar2=None, op0=ALU.mult),
                    waits=[(sv, t_b4)], inc=sv)
                R("scalar", lambda e, s_=s_: e.dma_start(
                    out=w2p[s_, :, :], in_=w2out[:]),
                  waits=[(sv, tsc)], inc=sd_pre, incn=16)
                t_st_prev = (sd_pre, sd_pre.n)
        t_pre_sync = sd_pre.n
        t_cst0 = s_cst[0].n
        t_cst1 = s_cst[1].n
        t_pre_g0 = sg_pre[0].n
        t_pre_g1 = sg_pre[1].n
        tok_scratch_ready = [(sd_pre, t_pre_sync), (s_cst[0], t_cst0),
                             (s_cst[1], t_cst1)]

        tok_Elast = (None, 0)
        tok_Eapply = (None, 0)
        tok_outst = (None, 0)
        tok_outst2 = (None, 0)
        tok_prev_spill = (None, 0)
        tok_kvstore = (None, 0)
        tok_x1spill = [(None, 0)] * NCH
        tok_kbuf_free = [(None, 0), (None, 0)]
        tok_vbuf_free = [(None, 0), (None, 0)]
        tok_wqbuf_free = [(None, 0), (None, 0)]
        tok_w13_free = [(None, 0), (None, 0)]
        tok_w13h_free = [[(None, 0), (None, 0)], [(None, 0), (None, 0)]]
        tok_w2buf_free = [(None, 0), (None, 0)]
        tok_qT_lastread = (None, 0)
        tok_aout_lastread = (None, 0)
        tok_hT_lastread = (None, 0)


        with ExitStack() as ck:
            def cb(name, shape, dt):
                return ck.enter_context(nc.sbuf_tensor(name, shape, dt))
            xc = cb("xc", [128, NDM * CH], F32)
            x1bf = cb("x1bf", [128, NDM * CH], BF16)
            rcbuf = [cb(f"rc{i}", [128, 512], F32) for i in range(2)]
            with ExitStack() as g1:
                def g1b(name, shape, dt):
                    return g1.enter_context(nc.sbuf_tensor(name, shape, dt))
                qT = g1b("qT", [128, NQH * CH], BF16)
                aoutT = g1b("aoutT", [128, NQH * CH], BF16)
                with ExitStack() as gA:
                    def gAb(name, shape, dt):
                        return gA.enter_context(
                            nc.sbuf_tensor(name, shape, dt))
                    hT = gAb("hT", [128, NDM * CH], BF16)
                    sqb = [gAb(f"sq{i}", [128, CH], BF16) for i in range(2)]
                    ktmp = [gAb(f"ktmp{i}", [128, CH], F32) for i in range(2)]
                    rot_b = [gAb(f"rot{i}", [128, CH], F32) for i in range(2)]
                    tc_b = [gAb(f"tc{i}", [128, CH], F32) for i in range(2)]
                    wqbuf = [gAb(f"wqb{i}", [128, DM], BF16) for i in range(2)]
                    knew = gAb("knew", [128, NKVH * CH], BF16)
                    vnew = gAb("vnew", [128, NKVH * CH], BF16)
                    ksq = gAb("ksq", [128, CH], BF16)
                with ExitStack() as gB:
                    def gBb(name, shape, dt):
                        return gB.enter_context(
                            nc.sbuf_tensor(name, shape, dt))
                    kbuf = [gBb(f"kb{i}", [128, S], BF16) for i in range(2)]
                    vbuf = [gBb(f"vb{i}", [128, S], BF16) for i in range(2)]
                    PT = [gBb(f"pt{i}", [128, CH], BF16) for i in range(4)]
                    qsq = gBb("qsq", [128, CH], BF16)
                with ExitStack() as gC:
                    def gCb(name, shape, dt):
                        return gC.enter_context(
                            nc.sbuf_tensor(name, shape, dt))
                    wobuf = [gCb(f"wob{i}", [128, DM], BF16)
                             for i in range(2)]
                    ybf = [gCb(f"ybf{i}", [128, CH], BF16) for i in range(2)]
                    ysq = [gCb(f"ysq{i}", [128, CH], BF16) for i in range(2)]
            with ExitStack() as g2:
                def g2b(name, shape, dt):
                    return g2.enter_context(nc.sbuf_tensor(name, shape, dt))
                hidden = g2b("hidden", [128, NH * CH], BF16)
                with ExitStack() as gD:
                    def gDb(name, shape, dt):
                        return gD.enter_context(
                            nc.sbuf_tensor(name, shape, dt))
                    w1b = [gDb(f"w1b{i}", [128, DM // 2], BF16)
                           for i in range(2)]
                    w3b = [gDb(f"w3b{i}", [128, DM // 2], BF16)
                           for i in range(2)]
                    silub = [gDb(f"silu{i}", [128, CH], F32)
                             for i in range(2)]
                with ExitStack() as gE:
                    def gEb(name, shape, dt):
                        return gE.enter_context(
                            nc.sbuf_tensor(name, shape, dt))
                    w2b = [gEb(f"w2b{i}", [128, HD // 8], BF16)
                           for i in range(2)]
                    y2t = [gEb(f"y2t{i}", [128, CH], F32) for i in range(2)]
                    x1rd = [gEb(f"x1rd{i}", [128, CH], F32)
                            for i in range(2)]
                    y2bf = [gEb(f"y2bf{i}", [128, CH], BF16)
                            for i in range(2)]
        y2sq = y2bf
        outb = y2t

        for c in range(NCH):
            sp = sp_ring[c % 4]
            NKT = 4 * (c + 1)
            if True:
                if True:
                    if True:
                        xw = [tok_prev_spill,
                              (sd_pre, t_pre_sync), (sg_pre[0], t_pre_g0),
                              (sg_pre[1], t_pre_g1), (sg_res, sg_res.n),
                              (s_cst[0], t_cst0), (s_cst[1], t_cst1)]
                        for ii in range(4):
                            t_x = R("sync", lambda e, ii=ii, c=c: e.dma_start(
                                out=xc[:, ii * 4 * CH:(ii + 1) * 4 * CH]
                                .rearrange("p (i t) -> p i t", t=CH),
                                in_=xT[ii * 512:(ii + 1) * 512,
                                       c * CH:(c + 1) * CH]
                                .rearrange("(i p) t -> p i t", p=128)),
                                waits=xw, inc=sd_x, incn=16)
                        tok_xload = t_x

                        sq_free = [(None, 0), (None, 0)]
                        t_mm = 0
                        for i in range(NDM):
                            bsel = i % 2
                            w_sq = [(sd_x, tok_xload), sq_free[bsel]]
                            if i == 0:
                                w_sq += [tok_Elast]
                            t_sq = R("scalar", lambda e, i=i, bsel=bsel:
                                     e.activation(
                                         out=sqb[bsel][:],
                                         in_=xc[:, i * CH:(i + 1) * CH],
                                         func=AF.Square),
                                     waits=w_sq, inc=sa)
                            t_mm = R("tensor", lambda e, i=i, bsel=bsel:
                                     e.matmul(PS[2][0:1, :], ones_bf[:, :],
                                              sqb[bsel][:],
                                              start=(i == 0),
                                              stop=(i == NDM - 1)),
                                     waits=[(sa, t_sq)] +
                                     ([pr[2]] if i == 0 else []), inc=sp)
                            sq_free[bsel] = (sp, t_mm)
                        t_ln = R("scalar", lambda e: e.activation(
                            out=stat_m[:, :], in_=PS[2][0:1, :], func=AF.Ln,
                            bias=epsr[0:1, :], scale=1.0 / DM),
                            waits=[(sp, t_mm)], inc=sa)
                        t_ex = R("scalar", lambda e: e.activation(
                            out=stat_m[:, :], in_=stat_m[:, :], func=AF.Exp,
                            scale=-0.5), waits=[(sa, t_ln)], inc=sa)
                        pr[2] = (sa, t_ln)
                        t_bc = R("tensor", lambda e: e.matmul(
                            PS[3][:, :], ones_f[:, :], stat_m[0:1, :],
                            start=True, stop=True),
                            waits=[(sa, t_ex), pr[3]], inc=sp)
                        t_rms = R("scalar", lambda e: e.copy(
                            out=rcbuf[0][:], in_=PS[3][:, :]),
                            waits=[(sp, t_bc)], inc=sa)
                        pr[3] = (sa, t_rms)
                        t_hT = 0
                        for i in range(NDM):
                            t_hT = R("vector", lambda e, i=i:
                                     e.scalar_tensor_tensor(
                                         out=hT[:, i * CH:(i + 1) * CH],
                                         in0=xc[:, i * CH:(i + 1) * CH],
                                         scalar=lnw_sb[:, i:i + 1],
                                         in1=rcbuf[0][:],
                                         op0=ALU.mult, op1=ALU.mult),
                                     waits=[(sa, t_rms), tok_hT_lastread],
                                     inc=sv)
                        tok_hTready = t_hT

                        tok_knorm = []
                        for g in range(NKVH):
                            b = g % 2
                            t_g = 0
                            for i in range(NDM):
                                t_g = R("tensor", lambda e, g=g, i=i, b=b:
                                        e.matmul(
                                            PS[b][:, :],
                                            wk_sb[:, (g * NDM + i) * 128:
                                                  (g * NDM + i + 1) * 128],
                                            hT[:, i * CH:(i + 1) * CH],
                                            start=(i == 0),
                                            stop=(i == NDM - 1)),
                                        waits=[(sv, tok_hTready)] +
                                        ([pr[b]] if i == 0 else []), inc=sp)
                            w_kb = [(sp, t_g)]
                            if g < 2:
                                w_kb += [tok_outst, tok_outst2]
                            t_kb = R("vector", lambda e, g=g, b=b:
                                     e.tensor_scalar(
                                         out=ktmp[b][:], in0=PS[b][:, :],
                                         scalar1=bk_sb[:, g:g + 1],
                                         scalar2=None, op0=ALU.add),
                                     waits=w_kb, inc=sv)
                            pr[b] = (sv, t_kb)
                            cosc = cos_t[:, c * CH:(c + 1) * CH]
                            sinc = sin_t[:, c * CH:(c + 1) * CH]
                            R("vector", lambda e, b=b: e.tensor_scalar(
                                out=rot_b[b][0:64, :],
                                in0=ktmp[b][64:128, :], scalar1=-1.0,
                                scalar2=None, op0=ALU.mult),
                              waits=[(sv, t_kb), (sv, t_rope_tab)])
                            t_r2 = R("vector", lambda e, b=b: e.tensor_copy(
                                out=rot_b[b][64:128, :],
                                in_=ktmp[b][0:64, :]),
                                waits=[(sv, t_kb)], inc=sv)
                            t_tc = R("vector", lambda e, b=b, cosc=cosc:
                                     e.tensor_tensor(
                                         out=tc_b[b][:], in0=ktmp[b][:],
                                         in1=cosc, op=ALU.mult),
                                     waits=[(sv, t_kb)], inc=sv)
                            t_ts = R("vector", lambda e, b=b, sinc=sinc:
                                     e.tensor_tensor(
                                         out=rot_b[b][:], in0=rot_b[b][:],
                                         in1=sinc, op=ALU.mult),
                                     waits=[(sv, t_r2)], inc=sv)
                            t_hi = R("vector", lambda e, g=g, b=b:
                                     e.tensor_tensor(
                                         out=knew[:, g * CH:(g + 1) * CH],
                                         in0=tc_b[b][:], in1=rot_b[b][:],
                                         op=ALU.add),
                                     waits=[(sv, t_ts), (sv, t_tc)], inc=sv)
                            t_lo = t_hi
                            t_ks = R("vector", lambda e, g=g: e.tensor_tensor(
                                out=ksq[:],
                                in0=knew[:, g * CH:(g + 1) * CH],
                                in1=knew[:, g * CH:(g + 1) * CH],
                                op=ALU.mult), waits=[(sv, t_hi)], inc=sv)
                            t_kn = R("tensor", lambda e: e.matmul(
                                PS[4][0:1, :], ones_bf[:, :], ksq[:],
                                start=True, stop=True),
                                waits=[(sv, t_ks), pr[4]], inc=sp)
                            t_km = R("vector", lambda e: e.reduce_max(
                                out=stat_t[:, 6:7], in_=PS[4][0:1, :],
                                axis=AXX), waits=[(sp, t_kn)], inc=sv)
                            pr[4] = (sv, t_km)
                            t_ku = R("vector", lambda e, g=g: e.tensor_tensor(
                                out=kmax2[:, g:g + 1], in0=kmax2[:, g:g + 1],
                                in1=stat_t[:, 6:7], op=ALU.max),
                                waits=[(sv, t_km)], inc=sv)
                            tok_knorm.append(t_ku)
                            t_kst = R("scalar", lambda e, g=g, c=c:
                                      e.dma_start(
                                          out=kscr[g, :,
                                                   c * CH:(c + 1) * CH],
                                          in_=knew[:, g * CH:(g + 1) * CH]),
                                      waits=[(sv, t_hi)],
                                      inc=s_kvst, incn=16)

                        for sub in range(4):
                            b = sub % 2
                            t_g = 0
                            for i in range(NDM):
                                t_g = R("tensor", lambda e, sub=sub, i=i, b=b:
                                        e.matmul(
                                            PS[b][:, :],
                                            hT[:, i * CH + sub * 128:
                                               i * CH + (sub + 1) * 128],
                                            wv_sb[:, i * KV:(i + 1) * KV],
                                            start=(i == 0),
                                            stop=(i == NDM - 1)),
                                        waits=[(sv, tok_hTready)] +
                                        ([pr[b]] if i == 0 else []), inc=sp)
                            w_vb = [(sp, t_g)]
                            if sub < 2:
                                w_vb += [tok_outst, tok_outst2]
                            t_vb = R("vector", lambda e, sub=sub, b=b:
                                     e.tensor_tensor(
                                         out=vnew[:, sub * KV:(sub + 1) * KV],
                                         in0=PS[b][:, :], in1=bv_bc[:],
                                         op=ALU.add),
                                     waits=w_vb, inc=sv)
                            pr[b] = (sv, t_vb)
                            for g in range(NKVH):
                                R("scalar", lambda e, sub=sub, g=g, c=c:
                                  e.dma_start(
                                      out=vscr[g, c * 4 + sub, :, :],
                                      in_=vnew[:, sub * KV + g * 128:
                                               sub * KV + (g + 1) * 128]),
                                  waits=[(sv, t_vb)], inc=s_kvst, incn=16)

                        tok_ropeq = [0] * NQH
                        for h in range(NQH):
                            b = h % 2
                            w_wq = [tok_wqbuf_free[b],
                                    (sd_pre, t_pre_sync),
                                    (s_cst[0], t_cst0), (s_cst[1], t_cst1)]
                            if h < 2:
                                w_wq += [tok_Elast, tok_outst, tok_outst2]
                            t_wl = R("sync", lambda e, h=h, b=b: e.dma_start(
                                out=wqbuf[b][:], in_=wqs[h, :, :]),
                                waits=w_wq, inc=s_qo[b], incn=16)
                            t_g = 0
                            for i in range(NDM):
                                t_g = R("tensor", lambda e, h=h, i=i, b=b:
                                        e.matmul(
                                            PS[b][:, :],
                                            wqbuf[b][:, i * 128:(i + 1) * 128],
                                            hT[:, i * CH:(i + 1) * CH],
                                            start=(i == 0),
                                            stop=(i == NDM - 1)),
                                        waits=[(sv, tok_hTready),
                                               (s_qo[b], t_wl)] +
                                        ([pr[b]] if i == 0 else []), inc=sp)
                            tok_wqbuf_free[b] = (sp, t_g)
                            t_qb = R("vector", lambda e, h=h, b=b:
                                     e.tensor_scalar(
                                         out=ktmp[b][:], in0=PS[b][:, :],
                                         scalar1=bq_sb[:, h:h + 1],
                                         scalar2=None, op0=ALU.add),
                                     waits=[(sp, t_g)], inc=sv)
                            pr[b] = (sv, t_qb)
                            cosc = cos_t[:, c * CH:(c + 1) * CH]
                            sinc = sin_t[:, c * CH:(c + 1) * CH]
                            R("vector", lambda e, b=b: e.tensor_scalar(
                                out=rot_b[b][0:64, :],
                                in0=ktmp[b][64:128, :], scalar1=-1.0,
                                scalar2=None, op0=ALU.mult),
                              waits=[(sv, t_qb), (sv, t_rope_tab)])
                            t_r2 = R("vector", lambda e, b=b: e.tensor_copy(
                                out=rot_b[b][64:128, :],
                                in_=ktmp[b][0:64, :]),
                                waits=[(sv, t_qb)], inc=sv)
                            t_tc = R("vector", lambda e, b=b, cosc=cosc:
                                     e.tensor_tensor(
                                         out=tc_b[b][:], in0=ktmp[b][:],
                                         in1=cosc, op=ALU.mult),
                                     waits=[(sv, t_qb)], inc=sv)
                            t_ts = R("vector", lambda e, b=b, sinc=sinc:
                                     e.tensor_tensor(
                                         out=rot_b[b][:], in0=rot_b[b][:],
                                         in1=sinc, op=ALU.mult),
                                     waits=[(sv, t_r2)], inc=sv)
                            t_hi = R("vector", lambda e, h=h, b=b:
                                     e.tensor_tensor(
                                         out=qT[:, h * CH:(h + 1) * CH],
                                         in0=tc_b[b][:], in1=rot_b[b][:],
                                         op=ALU.add),
                                     waits=[(sv, t_ts), (sv, t_tc)], inc=sv)
                            tok_ropeq[h] = t_hi
                        tok_qkv_done = sp.n
                        tok_Av = sv.n
                        tok_kvstore = (s_kvst, s_kvst.n)

                    if True:
                        tok_kload = [0, 0]
                        tok_vload = [0, 0]

                        def issue_kv(g):
                            b = g % 2
                            tok_kload[b] = R(
                                "scalar", lambda e, g=g, b=b, NKT=NKT:
                                e.dma_start(
                                    out=kbuf[b][:, 0:NKT * 128],
                                    in_=kscr[g, :, 0:NKT * 128]),
                                waits=[tok_kvstore, tok_kbuf_free[b],
                                       (sp, tok_qkv_done), (sv, tok_Av),
                                       tok_outst, tok_outst2],
                                inc=s_kv[b], incn=16)
                            tv = 0
                            nj = NKT
                            j0 = 0
                            while j0 < nj:
                                jn = min(8, nj - j0)
                                tv = R("scalar",
                                       lambda e, g=g, b=b, j0=j0, jn=jn:
                                       e.dma_start(
                                           out=vbuf[b][:, j0 * 128:
                                                       (j0 + jn) * 128]
                                           .rearrange("t (j d) -> t j d",
                                                      d=128),
                                           in_=vscr[g, j0:j0 + jn, :, :]
                                           .rearrange("j t d -> t j d")),
                                       waits=[tok_kvstore, tok_vbuf_free[b],
                                              (sp, tok_qkv_done),
                                              (sv, tok_Av)],
                                       inc=s_kv[b], incn=16)
                                j0 += jn
                            tok_vload[b] = tv

                        issue_kv(0)
                        if NKVH > 1:
                            issue_kv(1)

                        tok_exp = {}
                        tok_pv = {}
                        tok_norm = [(None, 0), (None, 0)]
                        tok_bias_done = [(None, 0), (None, 0)]
                        tok_qn_read = [(None, 0), (None, 0)]
                        ptfree = [(None, 0)] * 4

                        tok_bias_arr = {}
                        tok_qm_arr = {}

                        def emit_bias(h):
                            p_ = h % 2
                            g = h // 4
                            w_qs = [(sv, tok_ropeq[h]), tok_qn_read[p_]]
                            if h < 2:
                                w_qs += [tok_outst, tok_outst2]
                            t_qs = R("vector", lambda e, h=h:
                                     e.tensor_tensor(
                                         out=qsq[:],
                                         in0=qT[:, h * CH:(h + 1) * CH],
                                         in1=qT[:, h * CH:(h + 1) * CH],
                                         op=ALU.mult),
                                     waits=w_qs, inc=sv)
                            t_qn = R("tensor", lambda e, p_=p_: e.matmul(
                                PS[6 + p_][0:1, :], ones_bf[:, :],
                                qsq[:], start=True, stop=True),
                                waits=[(sv, t_qs), pr[6 + p_]], inc=sp)
                            t_qm = R("vector", lambda e, p_=p_:
                                     e.reduce_max(
                                         out=stat_t[:, p_:p_ + 1],
                                         in_=PS[6 + p_][0:1, :],
                                         axis=AXX),
                                     waits=[(sp, t_qn)], inc=sv)
                            tok_qn_read[p_] = (sv, t_qm)
                            t_pr = R("vector", lambda e, g=g, p_=p_:
                                     e.tensor_tensor(
                                         out=stat_t[:, 2 + p_:3 + p_],
                                         in0=stat_t[:, p_:p_ + 1],
                                         in1=kmax2[:, g:g + 1],
                                         op=ALU.mult),
                                     waits=[(sv, t_qm),
                                            (sv, tok_knorm[g])], inc=sv)
                            t_l1 = R("scalar", lambda e, p_=p_:
                                     e.activation(
                                         out=stat_t[:, 2 + p_:3 + p_],
                                         in_=stat_t[:, 2 + p_:3 + p_],
                                         func=AF.Ln),
                                     waits=[(sv, t_pr)], inc=sa)
                            t_l2 = R("scalar", lambda e, p_=p_:
                                     e.activation(
                                         out=stat_t[:, 2 + p_:3 + p_],
                                         in_=stat_t[:, 2 + p_:3 + p_],
                                         func=AF.Exp, scale=0.5),
                                     waits=[(sa, t_l1)], inc=sa)
                            t_bb = R("vector", lambda e, p_=p_:
                                     e.tensor_scalar(
                                         out=stat_t[:, 2 + p_:3 + p_],
                                         in0=stat_t[:, 2 + p_:3 + p_],
                                         scalar1=-INVSQ, scalar2=-2.0,
                                         op0=ALU.mult, op1=ALU.add),
                                     waits=[(sa, t_l2)], inc=sv)
                            t_bm = R("tensor", lambda e, p_=p_: e.matmul(
                                PS[6 + p_][:, 0:1], ones_f[:, :],
                                stat_t[0:1, 2 + p_:3 + p_],
                                start=True, stop=True),
                                waits=[(sv, t_bb), (sv, t_qm),
                                       pr[6 + p_]], inc=sp)
                            t_bv = R("vector", lambda e, p_=p_:
                                     e.tensor_copy(
                                         out=bias_h[:, p_:p_ + 1],
                                         in_=PS[6 + p_][:, 0:1]),
                                     waits=[(sp, t_bm),
                                            tok_bias_done[p_]], inc=sv)
                            pr[6 + p_] = (sv, t_bv)
                            tok_bias_arr[h] = t_bv
                            tok_qm_arr[h] = t_qm

                        emit_bias(0)
                        for g in range(NKVH):
                            gb = g % 2
                            for hh in range(4):
                                h = 4 * g + hh
                                p_ = h % 2
                                if h + 1 < NQH:
                                    emit_bias(h + 1)
                                tok_bias = tok_bias_arr[h]
                                t_qm = tok_qm_arr[h]
                                SBK = (0, 1, 4, 5)
                                t_l_end = 0
                                t_s = {}
                                t_ready = {}
                                ngrp = (NKT + 3) // 4
                                for grp in range(ngrp + 1):
                                    if grp >= 1:
                                        for j in range(4 * (grp - 1),
                                                       min(4 * grp, NKT)):
                                            w_pv = [t_ready[j],
                                                    (s_kv[gb],
                                                     tok_vload[gb])]
                                            if j == 0:
                                                w_pv.append(pr[2 + p_])
                                            t_pv = R(
                                                "tensor",
                                                lambda e, j=j, gb=gb, p_=p_,
                                                NKT=NKT: e.matmul(
                                                    PS[2 + p_][:, :],
                                                    vbuf[gb][:, j * 128:
                                                             (j + 1) * 128],
                                                    PT[j % 4][:],
                                                    start=(j == 0),
                                                    stop=(j == NKT - 1)),
                                                waits=w_pv, inc=sp)
                                            tok_pv[(h, j)] = t_pv
                                            w_l = [t_ready[j]]
                                            if j == 0:
                                                w_l.append(pr[6 + p_])
                                                w_l.append((sv, t_qm))
                                            t_l_end = R(
                                                "tensor",
                                                lambda e, j=j, p_=p_,
                                                NKT=NKT: e.matmul(
                                                    PS[6 + p_][0:1, :],
                                                    ones_bf[:, :],
                                                    PT[j % 4][:],
                                                    start=(j == 0),
                                                    stop=(j == NKT - 1)),
                                                waits=w_l, inc=sp)
                                            ptfree[j % 4] = (sp, t_l_end)
                                    if grp < ngrp:
                                        for j in range(4 * grp,
                                                       min(4 * grp + 4, NKT)):
                                            sb_ = SBK[j % 4]
                                            waits_s = [
                                                (s_kv[gb], tok_vload[gb]),
                                                (sv, tok_ropeq[h]), pr[sb_]]
                                            t_s[j] = R(
                                                "tensor",
                                                lambda e, h=h, j=j, sb_=sb_,
                                                gb=gb: e.matmul(
                                                    PS[sb_][:, :],
                                                    kbuf[gb][:, j * 128:
                                                             (j + 1) * 128],
                                                    qT[:, h * CH:
                                                       (h + 1) * CH],
                                                    start=True, stop=True),
                                                waits=waits_s, inc=sp)
                                        for j in range(4 * grp,
                                                       min(4 * grp + 4, NKT)):
                                            sb_ = SBK[j % 4]
                                            w_exp = [(sp, t_s[j]),
                                                     (sv, tok_bias),
                                                     ptfree[j % 4]]
                                            if h < 2 and j < 4:
                                                w_exp.append((sv, tok_Av))
                                            t_e = R("scalar",
                                                    lambda e, sb_=sb_, j=j,
                                                    p_=p_: e.activation(
                                                        out=PT[j % 4][:],
                                                        in_=PS[sb_][:, :],
                                                        func=AF.Exp,
                                                        bias=bias_h[:,
                                                                    p_:p_ + 1],
                                                        scale=INVSQ),
                                                    waits=w_exp, inc=sa)
                                            pr[sb_] = (sa, t_e)
                                            tok_exp[(h, j)] = t_e
                                            t_ready[j] = (sa, t_e)
                                            if j >= 4 * c:
                                                jr = j - 4 * c
                                                t_mk = R(
                                                    "vector",
                                                    lambda e, j=j, jr=jr:
                                                    e.tensor_tensor(
                                                        out=PT[j % 4][:],
                                                        in0=PT[j % 4][:],
                                                        in1=masks[:,
                                                                  jr * CH:
                                                                  (jr + 1) *
                                                                  CH],
                                                        op=ALU.mult),
                                                    waits=[(sa, t_e)],
                                                    inc=sv)
                                                t_ready[j] = (sv, t_mk)
                                t_rc = R("vector", lambda e, p_=p_:
                                         e.reciprocal(
                                             out=stat_a[:, 0:CH],
                                             in_=PS[6 + p_][0:1, :]),
                                         waits=[(sp, t_l_end)], inc=sv)
                                pr[6 + p_] = (sv, t_rc)
                                t_rb = R("tensor", lambda e, p_=p_: e.matmul(
                                    PS[6 + p_][:, :], ones_f[:, :],
                                    stat_a[0:1, 0:CH], start=True, stop=True),
                                    waits=[(sv, t_rc), pr[6 + p_]], inc=sp)
                                t_re = R("vector", lambda e, p_=p_:
                                         e.tensor_copy(out=rcbuf[p_][:],
                                                       in_=PS[6 + p_][:, :]),
                                         waits=[(sp, t_rb)], inc=sv)
                                pr[6 + p_] = (sv, t_re)
                                t_nm = R("vector", lambda e, h=h, p_=p_:
                                         e.tensor_tensor(
                                             out=aoutT[:, h * CH:
                                                       (h + 1) * CH],
                                             in0=PS[2 + p_][:, :],
                                             in1=rcbuf[p_][:], op=ALU.mult),
                                         waits=[(sv, t_re),
                                                (sp, tok_pv[(h, NKT - 1)])],
                                         inc=sv)
                                pr[2 + p_] = (sv, t_nm)
                                tok_norm[p_] = (sv, t_nm)
                                tok_bias_done[p_] = (sa, tok_exp[(h, NKT - 1)])
                            tok_kbuf_free[gb] = (sp, sp.n)
                            tok_vbuf_free[gb] = (sp, sp.n)
                            if g + 2 < NKVH:
                                issue_kv(g + 2)
                        tok_qT_lastread = (sp, sp.n)
                        tok_Bact = (sa, sa.n)
                        tok_attn_done = [tok_norm[0], tok_norm[1]]

                    if True:
                        ybf_free = [(None, 0), (None, 0)]
                        ysq_free = [(None, 0), (None, 0)]
                        t_sum = t_ssq = 0
                        t_ytt = {}
                        for et in range(NDM):
                            b = et % 2
                            w_wo = [tok_wqbuf_free[b],
                                    (sd_pre, t_pre_sync),
                                    (s_cst[0], t_cst0), (s_cst[1], t_cst1)]
                            if et < 2:
                                w_wo += [tok_qT_lastread, tok_attn_done[0],
                                         tok_attn_done[1], tok_bias_done[0],
                                         tok_bias_done[1]]
                            t_wl = R("sync", lambda e, et=et, b=b:
                                     e.dma_start(out=wobuf[b][:],
                                                 in_=wos[et, :, :]),
                                     waits=w_wo, inc=s_qo[b], incn=16)
                            t_g = 0
                            for j in range(NDM):
                                w_mm = [(s_qo[b], t_wl)]
                                if j == 0:
                                    w_mm += [pr[b], tok_attn_done[0],
                                             tok_attn_done[1]]
                                t_g = R("tensor", lambda e, et=et, j=j, b=b:
                                        e.matmul(
                                            PS[b][:, :],
                                            wobuf[b][:, j * 128:(j + 1) * 128],
                                            aoutT[:, j * CH:(j + 1) * CH],
                                            start=(j == 0),
                                            stop=(j == NDM - 1)),
                                        waits=w_mm, inc=sp)
                            tok_wqbuf_free[b] = (sp, t_g)
                            t_y = R("vector", lambda e, et=et, b=b:
                                    e.tensor_tensor(
                                        out=xc[:, et * CH:(et + 1) * CH],
                                        in0=PS[b][:, :],
                                        in1=xc[:, et * CH:(et + 1) * CH],
                                        op=ALU.add),
                                    waits=[(sp, t_g)], inc=sv)
                            pr[b] = (sv, t_y)
                            t_ytt[et] = t_y
                            t_cb = R("vector", lambda e, et=et, b=b:
                                     e.tensor_copy(
                                         out=ybf[b][:],
                                         in_=xc[:, et * CH:(et + 1) * CH]),
                                     waits=[(sv, t_y), ybf_free[b]], inc=sv)
                            t_sum = R("tensor", lambda e, et=et, b=b:
                                      e.matmul(PS[2][0:1, :], ones_bf[:, :],
                                               ybf[b][:], start=(et == 0),
                                               stop=(et == NDM - 1)),
                                      waits=[(sv, t_cb)] +
                                      ([pr[2]] if et == 0 else []), inc=sp)
                            ybf_free[b] = (sp, t_sum)
                            t_sq2 = R("vector", lambda e, et=et, b=b:
                                      e.tensor_tensor(
                                          out=ysq[b][:], in0=ybf[b][:],
                                          in1=ybf[b][:], op=ALU.mult),
                                      waits=[(sv, t_cb), ysq_free[b]],
                                      inc=sv)
                            t_ssq = R("tensor", lambda e, et=et, b=b:
                                      e.matmul(PS[3][0:1, :], ones_bf[:, :],
                                               ysq[b][:], start=(et == 0),
                                               stop=(et == NDM - 1)),
                                      waits=[(sv, t_sq2)] +
                                      ([pr[3]] if et == 0 else []), inc=sp)
                            ysq_free[b] = (sp, t_ssq)

                        t_m = R("vector", lambda e: e.tensor_scalar(
                            out=stat_m[:, :], in0=PS[2][0:1, :],
                            scalar1=1.0 / DM, scalar2=None, op0=ALU.mult),
                            waits=[(sp, t_sum)], inc=sv)
                        pr[2] = (sv, t_m)
                        t_v = R("vector", lambda e: e.tensor_scalar(
                            out=stat_a[:, :], in0=PS[3][0:1, :],
                            scalar1=1.0 / DM, scalar2=None, op0=ALU.mult),
                            waits=[(sp, t_ssq)], inc=sv)
                        pr[3] = (sv, t_v)
                        t_t1 = R("vector", lambda e: e.tensor_tensor(
                            out=rcbuf[0][0:1, :], in0=stat_m[:, :],
                            in1=stat_m[:, :], op=ALU.mult),
                            waits=[(sv, t_v), (sv, t_m)], inc=sv)
                        t_t2 = R("vector", lambda e: e.tensor_tensor(
                            out=stat_a[:, :], in0=stat_a[:, :],
                            in1=rcbuf[0][0:1, :], op=ALU.subtract),
                            waits=[(sv, t_t1)], inc=sv)
                        t_t3 = R("scalar", lambda e: e.activation(
                            out=stat_a[:, :], in_=stat_a[:, :], func=AF.Ln,
                            bias=epsl[0:1, :]), waits=[(sv, t_t2)], inc=sa)
                        t_t4 = R("scalar", lambda e: e.activation(
                            out=stat_a[:, :], in_=stat_a[:, :], func=AF.Exp,
                            scale=-0.5), waits=[(sa, t_t3)], inc=sa)
                        t_b1 = R("tensor", lambda e: e.matmul(
                            PS[4][:, :], ones_f[:, :], stat_m[0:1, :],
                            start=True, stop=True),
                            waits=[(sv, t_m), pr[4]], inc=sp)
                        t_b2 = R("tensor", lambda e: e.matmul(
                            PS[5][:, :], ones_f[:, :], stat_a[0:1, :],
                            start=True, stop=True),
                            waits=[(sa, t_t4), pr[5]], inc=sp)
                        t_ap = 0
                        tmp_free = [(None, 0), (None, 0)]
                        for et in range(NDM):
                            b = et % 2
                            t_a1 = R("vector", lambda e, et=et, b=b:
                                     e.tensor_tensor(
                                         out=rcbuf[b][:],
                                         in0=xc[:, et * CH:(et + 1) * CH],
                                         in1=PS[4][:, :], op=ALU.subtract),
                                     waits=[(sp, t_b1), (sv, t_ytt[et]),
                                            (sv, t_t2), tmp_free[b]], inc=sv)
                            t_a2 = R("vector", lambda e, et=et, b=b:
                                     e.tensor_tensor(
                                         out=xc[:, et * CH:(et + 1) * CH],
                                         in0=rcbuf[b][:], in1=PS[5][:, :],
                                         op=ALU.mult),
                                     waits=[(sv, t_a1), (sp, t_b2)], inc=sv)
                            tmp_free[b] = (sv, t_a2)
                            t_ap = R("vector", lambda e, et=et:
                                     e.tensor_copy(
                                         out=x1bf[:, et * CH:(et + 1) * CH],
                                         in_=xc[:, et * CH:(et + 1) * CH]),
                                     waits=[(sv, t_a2)], inc=sv)
                        pr[4] = (sv, t_ap)
                        pr[5] = (sv, t_ap)
                        t_sp = 0
                        for ii in range(4):
                            t_sp = R("scalar", lambda e, ii=ii, c=c:
                                     e.dma_start(
                                         out=x1scr[ii * 512:(ii + 1) * 512,
                                                   c * CH:(c + 1) * CH]
                                         .rearrange("(i p) t -> p i t",
                                                    p=128),
                                         in_=xc[:, ii * 4 * CH:
                                                (ii + 1) * 4 * CH]
                                         .rearrange("p (i t) -> p i t",
                                                    t=CH)),
                                     waits=[(sv, t_ap)], inc=s_spl, incn=16)
                        tok_x1spill[c] = (s_spl, t_sp)
                        tok_prev_spill = (s_spl, t_sp)
                        tok_Cdone_pe = (sp, sp.n)

                if True:
                    if True:
                        silu_free = [(None, 0), (None, 0)]
                        t_hid = 0
                        for r in range(NH):
                            b = r % 2
                            w_l = [tok_w13_free[b], (sd_pre, t_pre_sync),
                                   (s_cst[0], t_cst0), (s_cst[1], t_cst1)]
                            if r < 2:
                                w_l += [tok_qT_lastread, tok_Bact, tok_Elast,
                                        tok_attn_done[0], tok_attn_done[1]]
                            t_wh = [[0, 0], [0, 0]]
                            for (wi, wsrc_, wdst) in ((0, w1s, w1b),
                                                      (1, w3s, w3b)):
                                for hf in range(2):
                                    t_wh[wi][hf] = R(
                                        "sync",
                                        lambda e, r=r, hf=hf, wsrc_=wsrc_,
                                        wdst=wdst: e.dma_start(
                                            out=wdst[hf][:],
                                            in_=wsrc_[r, :,
                                                      hf * (DM // 2):
                                                      (hf + 1) * (DM // 2)]),
                                        waits=w_l + [tok_w13h_free[wi][hf]],
                                        inc=s_13h[wi][hf], incn=16)
                            t_gg = 0
                            t_ll = 0
                            for i in range(NDM):
                                hf = 0 if i < 8 else 1
                                w_mm = [(s_13h[0][hf], t_wh[0][hf])]
                                if i == 0:
                                    w_mm += [pr[0 + b], (sv, t_ap)]
                                t_gg = R("tensor",
                                         lambda e, r=r, i=i, b=b, hf=hf:
                                         e.matmul(
                                             PS[0 + b][:, :],
                                             w1b[hf][:, (i % 8) * 128:
                                                     (i % 8 + 1) * 128],
                                             x1bf[:, i * CH:(i + 1) * CH],
                                             start=(i == 0),
                                             stop=(i == NDM - 1)),
                                         waits=w_mm, inc=sp)
                                if i == 7 or i == NDM - 1:
                                    tok_w13h_free[0][hf] = (sp, t_gg)
                            for i in range(NDM):
                                hf = 0 if i < 8 else 1
                                w_mm = [(s_13h[1][hf], t_wh[1][hf])]
                                if i == 0:
                                    w_mm += [pr[2 + b], (sv, t_ap)]
                                t_ll = R("tensor",
                                         lambda e, r=r, i=i, b=b, hf=hf:
                                         e.matmul(
                                             PS[2 + b][:, :],
                                             w3b[hf][:, (i % 8) * 128:
                                                     (i % 8 + 1) * 128],
                                             x1bf[:, i * CH:(i + 1) * CH],
                                             start=(i == 0),
                                             stop=(i == NDM - 1)),
                                         waits=w_mm, inc=sp)
                                if i == 7 or i == NDM - 1:
                                    tok_w13h_free[1][hf] = (sp, t_ll)
                            tok_w13_free[b] = (sp, t_ll)
                            w_si = [(sp, t_gg), silu_free[b]]
                            if r < 2:
                                w_si += [tok_outst, tok_outst2, tok_Eapply,
                                         tok_attn_done[0], tok_attn_done[1]]
                            t_si = R("scalar", lambda e, b=b: e.activation(
                                out=silub[b][:], in_=PS[0 + b][:, :],
                                func=AF.Silu),
                                waits=w_si, inc=sa)
                            pr[0 + b] = (sa, t_si)
                            t_hid = R("vector", lambda e, r=r, b=b:
                                      e.tensor_tensor(
                                          out=hidden[:, r * CH:(r + 1) * CH],
                                          in0=silub[b][:],
                                          in1=PS[2 + b][:, :], op=ALU.mult),
                                      waits=[(sa, t_si), (sp, t_ll)], inc=sv)
                            pr[2 + b] = (sv, t_hid)
                            silu_free[b] = (sv, t_hid)
                        tok_hidden_all = t_hid

                    if True:
                        x1rd_free = [[tok_outst, tok_outst2],
                                     [tok_outst, tok_outst2]]
                        y2t_free = [(None, 0), (None, 0)]
                        y2bf_free = [(None, 0), (None, 0)]
                        y2sq_free = [(None, 0), (None, 0)]
                        t_sum = t_ssq = 0
                        t_y2 = {}
                        for jd in range(ND):
                            b = jd % 2
                            t_wq4 = [0] * 8

                            def issue_w2(q2, jd=jd):
                                bq2 = q2 % 2
                                w_lq = [tok_w2buf_free[bq2],
                                        (sd_pre, t_pre_sync)]
                                if jd < 2 and q2 < 2:
                                    w_lq += [(sv, tok_hidden_all)]
                                t_wq4[q2] = R(
                                    "sync", lambda e, jd=jd, q2=q2, bq2=bq2:
                                    e.dma_start(
                                        out=w2b[bq2][:],
                                        in_=w2p[jd, :, q2 * (HD // 8):
                                                (q2 + 1) * (HD // 8)]),
                                    waits=w_lq, inc=s_w2[bq2], incn=16)

                            issue_w2(0)
                            issue_w2(1)
                            w_xr = [tok_x1spill[c]] + \
                                (x1rd_free[b]
                                 if isinstance(x1rd_free[b], list)
                                 else [x1rd_free[b]])
                            if jd < 2:
                                w_xr += [(sv, tok_hidden_all)]
                            t_xr = R("sync", lambda e, jd=jd, c=c, b=b:
                                     e.dma_start(
                                         out=x1rd[b][:],
                                         in_=x1scr[jd * 128:(jd + 1) * 128,
                                                   c * CH:(c + 1) * CH]),
                                     waits=w_xr, inc=s_x1[b], incn=16)
                            t_g = 0
                            for r in range(NH):
                                q2 = r // 8
                                bq2 = q2 % 2
                                w_mm = [(s_w2[bq2], t_wq4[q2]),
                                        (sv, tok_hidden_all)]
                                if r == 0:
                                    w_mm += [pr[0 + b]]
                                rr = r % 8
                                t_g = R("tensor",
                                        lambda e, jd=jd, r=r, rr=rr, b=b,
                                        bq2=bq2: e.matmul(
                                            PS[0 + b][:, :],
                                            w2b[bq2][:, rr * 128:
                                                     (rr + 1) * 128],
                                            hidden[:, r * CH:(r + 1) * CH],
                                            start=(r == 0),
                                            stop=(r == NH - 1)),
                                        waits=w_mm, inc=sp)
                                if rr == 7:
                                    tok_w2buf_free[bq2] = (sp, t_g)
                                    if q2 + 2 < 8:
                                        issue_w2(q2 + 2)
                            w_yt = [(sp, t_g), (s_x1[b], t_xr),
                                    y2t_free[b]]
                            t_yt = R("vector", lambda e, jd=jd, b=b:
                                     e.tensor_tensor(
                                         out=y2t[b][:],
                                         in0=PS[0 + b][:, :], in1=x1rd[b][:],
                                         op=ALU.add),
                                     waits=w_yt, inc=sv)
                            pr[0 + b] = (sv, t_yt)
                            x1rd_free[b] = [(sv, t_yt)]
                            t_y2[jd] = t_yt
                            t_y2w = R("scalar", lambda e, jd=jd, c=c, b=b:
                                      e.dma_start(
                                          out=x1scr[jd * 128:(jd + 1) * 128,
                                                    c * CH:(c + 1) * CH],
                                          in_=y2t[b][:]),
                                      waits=[(sv, t_yt)], inc=s_y2w[b],
                                      incn=16)
                            y2t_free[b] = (s_y2w[b], t_y2w)
                            t_cb = R("vector", lambda e, jd=jd, b=b:
                                     e.tensor_copy(
                                         out=y2bf[b][:],
                                         in_=y2t[b][:]),
                                     waits=[(sv, t_yt), y2bf_free[b],
                                            y2sq_free[b]],
                                     inc=sv)
                            t_sum = R("tensor", lambda e, jd=jd, b=b:
                                      e.matmul(PS[2][0:1, :], ones_bf[:, :],
                                               y2bf[b][:], start=(jd == 0),
                                               stop=(jd == ND - 1)),
                                      waits=[(sv, t_cb)] +
                                      ([pr[2]] if jd == 0 else []), inc=sp)

                            t_sq2 = R("vector", lambda e, jd=jd, b=b:
                                      e.tensor_tensor(
                                          out=y2sq[b][:], in0=y2bf[b][:],
                                          in1=y2bf[b][:], op=ALU.mult),
                                      waits=[(sv, t_cb), y2sq_free[b],
                                             (sp, t_sum)],
                                      inc=sv)
                            t_ssq = R("tensor", lambda e, jd=jd, b=b:
                                      e.matmul(PS[3][0:1, :], ones_bf[:, :],
                                               y2sq[b][:], start=(jd == 0),
                                               stop=(jd == ND - 1)),
                                      waits=[(sv, t_sq2)] +
                                      ([pr[3]] if jd == 0 else []), inc=sp)
                            y2sq_free[b] = (sp, t_ssq)
                        tok_y2w_all = (s_y2w[0].n, s_y2w[1].n)

                        t_m = R("vector", lambda e: e.tensor_scalar(
                            out=stat_m[:, :], in0=PS[2][0:1, :],
                            scalar1=1.0 / DM, scalar2=None, op0=ALU.mult),
                            waits=[(sp, t_sum)], inc=sv)
                        pr[2] = (sv, t_m)
                        t_v = R("vector", lambda e: e.tensor_scalar(
                            out=stat_a[:, :], in0=PS[3][0:1, :],
                            scalar1=1.0 / DM, scalar2=None, op0=ALU.mult),
                            waits=[(sp, t_ssq)], inc=sv)
                        pr[3] = (sv, t_v)
                        t_t1 = R("vector", lambda e: e.tensor_tensor(
                            out=rcbuf[0][0:1, :], in0=stat_m[:, :],
                            in1=stat_m[:, :], op=ALU.mult),
                            waits=[(sv, t_m), tmp_free[0]], inc=sv)
                        t_t2 = R("vector", lambda e: e.tensor_tensor(
                            out=stat_a[:, :], in0=stat_a[:, :],
                            in1=rcbuf[0][0:1, :], op=ALU.subtract),
                            waits=[(sv, t_t1)], inc=sv)
                        t_t3 = R("scalar", lambda e: e.activation(
                            out=stat_a[:, :], in_=stat_a[:, :], func=AF.Ln,
                            bias=epsl[0:1, :]), waits=[(sv, t_t2)], inc=sa)
                        t_t4 = R("scalar", lambda e: e.activation(
                            out=stat_a[:, :], in_=stat_a[:, :], func=AF.Exp,
                            scale=-0.5), waits=[(sa, t_t3)], inc=sa)
                        t_b1 = R("tensor", lambda e: e.matmul(
                            PS[4][:, :], ones_f[:, :], stat_m[0:1, :],
                            start=True, stop=True),
                            waits=[(sv, t_m), pr[4]], inc=sp)
                        t_b2 = R("tensor", lambda e: e.matmul(
                            PS[5][:, :], ones_f[:, :], stat_a[0:1, :],
                            start=True, stop=True),
                            waits=[(sa, t_t4), pr[5]], inc=sp)
                        t_ap2 = 0
                        tmp2_free = [(None, 0), (None, 0)]
                        outb_free = [(None, 0), (None, 0)]
                        y2rd_free = [[(sv, t_y2[ND - 2])],
                                     [(sv, t_y2[ND - 1])]]
                        t_st = 0
                        for jd in range(ND):
                            b = jd % 2
                            t_rr = R("sync", lambda e, jd=jd, c=c, b=b:
                                     e.dma_start(
                                         out=x1rd[b][:],
                                         in_=x1scr[jd * 128:(jd + 1) * 128,
                                                   c * CH:(c + 1) * CH]),
                                     waits=[(s_y2w[0], tok_y2w_all[0]),
                                            (s_y2w[1], tok_y2w_all[1])] +
                                     y2rd_free[b], inc=s_y2r[b], incn=16)
                            t_a1 = R("vector", lambda e, jd=jd, b=b:
                                     e.tensor_tensor(
                                         out=rcbuf[b][:],
                                         in0=x1rd[b][:],
                                         in1=PS[4][:, :], op=ALU.subtract),
                                     waits=[(sp, t_b1), (s_y2r[b], t_rr),
                                            (sv, t_t2), tmp2_free[b]],
                                     inc=sv)
                            t_a2 = R("vector", lambda e, jd=jd, b=b:
                                     e.tensor_tensor(
                                         out=outb[b][:], in0=rcbuf[b][:],
                                         in1=PS[5][:, :], op=ALU.mult),
                                     waits=[(sv, t_a1), (sp, t_b2),
                                            outb_free[b], y2t_free[b]],
                                     inc=sv)
                            tmp2_free[b] = (sv, t_a2)
                            y2rd_free[b] = [(sv, t_a1)]
                            t_ap2 = t_a2
                            t_st = R("scalar", lambda e, jd=jd, c=c, b=b:
                                     e.dma_start(
                                         out=outT[jd * 128:(jd + 1) * 128,
                                                  c * CH:(c + 1) * CH],
                                         in_=outb[b][:]),
                                     waits=[(sv, t_a2)], inc=s_out[b],
                                     incn=16)
                            outb_free[b] = (s_out[b], t_st)
                            y2t_free[b] = (s_out[b], t_st)
                        pr[4] = (sv, t_ap2)
                        pr[5] = (sv, t_ap2)
                        tok_Elast = (sp, sp.n)
                        tok_Eapply = (sv, sv.n)
                        tok_outst = (s_out[0], s_out[0].n)
                        tok_outst2 = (s_out[1], s_out[1].n)

        def final_waits(e):
            for s in ([s_kvst, s_spl, sd_x, sd_pre, sg_res, sa, sv, sq,
                       s_wff] + s_y2w + sp_ring + s_y2r
                      + s_out + sg_pre + s_qo + s_13h[0] + s_13h[1] + s_w2
                      + s_kv + s_x1 + s_cst):
                if s.n > 0:
                    e.wait_ge(s.h, s.n)
            return e.nop()
        R("sync", final_waits)

        block = stk.enter_context(nc.Block())
        rec.emit(block)

    return nc


_NC_CACHE = {}


def _get_program():
    if "nc" not in _NC_CACHE:
        _NC_CACHE["nc"] = build_program()
    return _NC_CACHE["nc"]


def _prep_shared(inputs):
    f32 = np.float32
    qp = np.asarray(inputs["L0_attn_q_project"], f32)
    kp = np.asarray(inputs["L0_attn_k_project"], f32)
    vp = np.asarray(inputs["L0_attn_v_project"], f32)
    op = np.asarray(inputs["L0_attn_o_project"], f32)
    W1 = np.asarray(inputs["L0_ffn_W1"], f32)
    W2 = np.asarray(inputs["L0_ffn_W2"], f32)
    W3 = np.asarray(inputs["L0_ffn_W3"], f32)

    def lhsT_layout(w, nt):
        o, i = w.shape
        return np.ascontiguousarray(
            w.reshape(nt, 128, i // 128, 128).transpose(0, 3, 2, 1)
            .reshape(nt, 128, i))

    d = {
        "wq": lhsT_layout(qp, NQH),
        "wkh": lhsT_layout(kp, NKVH),
        "wvh": np.ascontiguousarray(
            vp.reshape(KV, NDM, 128).transpose(2, 1, 0)),
        "wo": lhsT_layout(op, NDM),
        "w1": lhsT_layout(W1, NH),
        "w3": lhsT_layout(W3, NH),
        "w2sl": np.ascontiguousarray(
            W2.reshape(ND, 128, NH, 128).transpose(0, 3, 2, 1)
            .reshape(ND, 128, NH * 128)),
        "bqh": np.ascontiguousarray(
            np.asarray(inputs["L0_attn_q_bias"], f32).reshape(NQH, 128).T),
        "bkh": np.ascontiguousarray(
            np.asarray(inputs["L0_attn_k_bias"], f32).reshape(NKVH, 128).T),
        "bvr": np.ascontiguousarray(
            np.asarray(inputs["L0_attn_v_bias"], f32).reshape(1, KV)),
        "lnwh": np.ascontiguousarray(
            np.asarray(inputs["L0_attn_ln_weight"], f32).reshape(NDM, 128).T),
    }
    return d


def kernel(**inputs):
    x = np.asarray(inputs["L0_attn_x"], np.float32)
    nb = x.shape[0]
    shared = _prep_shared(inputs)
    nc = _get_program()
    in_maps = []
    for b in range(nb):
        m = dict(shared)
        m["xT"] = np.ascontiguousarray(x[b].T)
        in_maps.append(m)
    res = run_bass_kernel_spmd(nc, in_maps, core_ids=list(range(nb)))
    out = np.stack([np.asarray(r["outT"]).T for r in res.results], axis=0)
    return out


# version 13
# speedup vs baseline: 1.0289x; 1.0289x over previous
import math
from contextlib import ExitStack

import numpy as np
import ml_dtypes
import concourse.bass as bass
import concourse.mybir as mybir
from concourse.bass_utils import run_bass_kernel_spmd
from concourse.alu_op_type import AluOpType as ALU

F32 = mybir.dt.float32
BF16 = mybir.dt.bfloat16
I32 = mybir.dt.int32
AF = mybir.ActivationFunctionType
AXX = mybir.AxisListType.X

S = 4096
DM = 2048
HD = 8192
CH = 512
NCH = S // CH
NDM = DM // 128
NH = HD // 128
ND = DM // 128
NQH = 16
NKVH = 4
DH = 128
KV = 512
PI = float(np.pi)
INVSQ = 1.0 / math.sqrt(DH)
EPS_RMS = 1e-6
EPS_LN = 1e-5
FWHT_SCALE = 1.0 / math.sqrt(HD)

ENGS = ["sync", "gpsimd", "vector", "scalar", "tensor"]


class Sem:
    def __init__(self, handle, name):
        self.h = handle
        self.name = name
        self.n = 0


class Rec:

    def __init__(self):
        self.ops = {e: [] for e in ENGS}

    def o(self, eng, fn, waits=(), inc=None, incn=1):
        w = tuple((s, v) for (s, v) in waits if s is not None and v > 0)
        self.ops[eng].append((w, fn, inc, incn))
        if inc is not None:
            inc.n += incn
            return inc.n
        return None

    def emit(self, block):
        decs = {"sync": block.sync, "gpsimd": block.gpsimd,
                "vector": block.vector, "scalar": block.scalar,
                "tensor": block.tensor}
        for eng in ENGS:
            oplist = self.ops[eng]
            if not oplist:
                continue

            @decs[eng]
            def _(e, oplist=oplist):
                seen = {}
                for (waits, fn, inc, incn) in oplist:
                    for (s, v) in waits:
                        if seen.get(s.name, 0) < v:
                            e.wait_ge(s.h, v)
                            seen[s.name] = v
                    inst = fn(e)
                    if inc is not None:
                        inst.then_inc(inc.h, incn)


def build_program():
    nc = bass.Bass("TRN2", target_bir_lowering=False)

    xT = nc.dram_tensor("xT", [DM, S], F32, kind="ExternalInput").ap()
    wq = nc.dram_tensor("wq", [NQH, 128, DM], F32, kind="ExternalInput").ap()
    wkh = nc.dram_tensor("wkh", [NKVH, 128, DM], F32, kind="ExternalInput").ap()
    wvh = nc.dram_tensor("wvh", [128, NDM, KV], F32, kind="ExternalInput").ap()
    wo = nc.dram_tensor("wo", [NDM, 128, DM], F32, kind="ExternalInput").ap()
    w1 = nc.dram_tensor("w1", [NH, 128, DM], F32, kind="ExternalInput").ap()
    w3 = nc.dram_tensor("w3", [NH, 128, DM], F32, kind="ExternalInput").ap()
    w2sl = nc.dram_tensor("w2sl", [ND, 128, NH * 128], F32, kind="ExternalInput").ap()
    bqh = nc.dram_tensor("bqh", [128, NQH], F32, kind="ExternalInput").ap()
    bkh = nc.dram_tensor("bkh", [128, NKVH], F32, kind="ExternalInput").ap()
    bvr = nc.dram_tensor("bvr", [1, KV], F32, kind="ExternalInput").ap()
    lnwh = nc.dram_tensor("lnwh", [128, NDM], F32, kind="ExternalInput").ap()

    wqs = nc.dram_tensor("wqs", [NQH, 128, DM], BF16, kind="Internal").ap()
    wos = nc.dram_tensor("wos", [NDM, 128, DM], BF16, kind="Internal").ap()
    w1s = nc.dram_tensor("w1s", [NH, 128, DM], BF16, kind="Internal").ap()
    w3s = nc.dram_tensor("w3s", [NH, 128, DM], BF16, kind="Internal").ap()
    w2p = nc.dram_tensor("w2p", [ND, 128, HD], BF16, kind="Internal").ap()
    kscr = nc.dram_tensor("kscr", [NKVH, 128, S], BF16, kind="Internal").ap()
    vscr = nc.dram_tensor("vscr", [NKVH, S // 128, 128, 128], BF16,
                          kind="Internal").ap()
    x1scr = nc.dram_tensor("x1scr", [DM, S], F32, kind="Internal").ap()
    outT = nc.dram_tensor("outT", [DM, S], F32, kind="ExternalOutput").ap()

    rec = Rec()
    R = rec.o
    stk = ExitStack()

    with stk:
        sems = {}

        def sem(name):
            s = Sem(stk.enter_context(nc.semaphore(name)), name)
            sems[name] = s
            return s

        sp_ring = [sem(f"sp{i}") for i in range(4)]
        sp = sp_ring[0]
        sa = sem("sa")
        sv = sem("sv")
        sq = sem("sq")
        sd_x = sem("sd_x")
        s_kvst = sem("s_kvst")
        s_spl = sem("s_spl")
        s_out = [sem("s_out0"), sem("s_out1")]
        sd_pre = sem("sd_pre")
        sg_res = sem("sg_res")
        sg_pre = [sem("sg_pre0"), sem("sg_pre1")]
        s_wff = sem("s_wff")
        s_w2st = [sem("s_w2st0"), sem("s_w2st1")]
        s_qo = [sem("s_qo0"), sem("s_qo1")]
        s_13h = [[sem("s_13a0"), sem("s_13a1")],
                 [sem("s_13b0"), sem("s_13b1")]]
        s_w2 = [sem("s_w2a"), sem("s_w2b")]
        s_kv = [sem("s_kv0"), sem("s_kv1")]
        s_x1 = [sem("s_x1a"), sem("s_x1b")]
        s_y2w = [sem("s_y2w0"), sem("s_y2w1")]
        s_y2r = [sem("s_y2r0"), sem("s_y2r1")]
        s_cst = [sem("s_cst0"), sem("s_cst1")]

        def sb(name, shape, dt):
            return stk.enter_context(nc.sbuf_tensor(name, shape, dt))

        cos_t = sb("cos_t", [128, S], F32)
        sin_t = sb("sin_t", [128, S], F32)
        masks = sb("masks", [128, 4 * CH], BF16)
        wk_sb = sb("wk_sb", [128, NKVH * DM], BF16)
        wv_sb = sb("wv_sb", [128, NDM * KV], BF16)
        bq_sb = sb("bq_sb", [128, NQH], F32)
        bk_sb = sb("bk_sb", [128, NKVH], F32)
        bv_bc = sb("bv_bc", [128, KV], BF16)
        lnw_sb = sb("lnw_sb", [128, NDM], F32)
        ones_bf = sb("ones_bf", [128, 1], BF16)
        epsr = sb("epsr", [128, 1], F32)
        epsl = sb("epsl", [128, 1], F32)
        ones_f = sb("ones_f", [1, 128], F32)
        h128 = sb("h128", [128, 128], BF16)
        h128n = sb("h128n", [128, 128], BF16)
        kmax2 = sb("kmax2", [1, NKVH], F32)
        irow = sb("irow", [1, 64], F32)
        bias_h = sb("bias_h", [128, 2], F32)
        stat_m = sb("stat_m", [1, CH], F32)
        stat_a = sb("stat_a", [1, CH], F32)
        stat_t = sb("stat_t", [1, 8], F32)

        PS = [stk.enter_context(nc.psum_tensor(f"ps{i}", [128, 512], F32))
              for i in range(8)]
        pr = [(None, 0)] * 8

        with ExitStack() as pre1:
            def p1(name, shape, dt):
                return pre1.enter_context(nc.sbuf_tensor(name, shape, dt))

            i_i32 = p1("i_i32", [128, 1], I32)
            j_i32 = p1("j_i32", [128, 128], I32)
            h_i32 = p1("h_i32", [128, 128], I32)
            s_i32 = p1("s_i32", [128, 128], I32)
            fgrid = p1("fgrid", [128, 4 * CH], F32)
            trow = p1("trow", [1, S], F32)
            ang_t = p1("ang_t", [64, S], F32)
            red_t = p1("red_t", [64, S], F32)
            kf_t = p1("kf_t", [64, S], F32)
            ki_t = p1("ki_t", [64, S], I32)
            ang = ang_t[:]
            red = red_t[:]
            kf = kf_t[:]
            ki = ki_t[:]

            R("gpsimd", lambda e: e.iota(i_i32[:], [[0, 1]], base=0,
                                         channel_multiplier=1))
            R("gpsimd", lambda e: e.iota(j_i32[:], [[1, 128]], base=0,
                                         channel_multiplier=0))
            R("gpsimd", lambda e: e.iota(fgrid[:], [[1, 4 * CH]], base=0,
                                         channel_multiplier=-1,
                                         allow_small_or_imprecise_dtypes=True))
            R("gpsimd", lambda e: e.iota(trow[:], [[1, S]], base=0,
                                         channel_multiplier=0,
                                         allow_small_or_imprecise_dtypes=True))
            R("gpsimd", lambda e: e.iota(irow[:], [[1, 64]], base=0,
                                         channel_multiplier=0,
                                         allow_small_or_imprecise_dtypes=True))
            R("gpsimd", lambda e: e.memset(ones_bf[:], 1.0))
            R("gpsimd", lambda e: e.memset(epsr[:], EPS_RMS))
            R("gpsimd", lambda e: e.memset(epsl[:], EPS_LN))
            R("gpsimd", lambda e: e.memset(kmax2[:], 0.0))
            t_iota = R("gpsimd", lambda e: e.memset(ones_f[:], 1.0), inc=sq)

            for jr in range(4):
                R("vector", lambda e, jr=jr: e.tensor_scalar(
                    out=masks[:, jr * CH:(jr + 1) * CH],
                    in0=fgrid[:, jr * CH:(jr + 1) * CH],
                    scalar1=float(jr * (CH + 128)), scalar2=None,
                    op0=ALU.is_ge), waits=[(sq, t_iota)])

            t0 = R("vector", lambda e: e.tensor_scalar(
                out=h_i32[:], in0=j_i32[:], scalar1=i_i32[:], scalar2=None,
                op0=ALU.bitwise_and), waits=[(sq, t_iota)], inc=sv)
            for sh in (4, 2, 1):
                t0 = R("vector", lambda e, sh=sh: e.tensor_scalar(
                    out=s_i32[:], in0=h_i32[:], scalar1=sh, scalar2=None,
                    op0=ALU.logical_shift_right), waits=[(sv, t0)], inc=sv)
                t0 = R("vector", lambda e: e.tensor_tensor(
                    out=h_i32[:], in0=h_i32[:], in1=s_i32[:],
                    op=ALU.bitwise_xor), waits=[(sv, t0)], inc=sv)
            t0 = R("vector", lambda e: e.tensor_scalar(
                out=h_i32[:], in0=h_i32[:], scalar1=1, scalar2=None,
                op0=ALU.bitwise_and), waits=[(sv, t0)], inc=sv)
            t0 = R("vector", lambda e: e.tensor_copy(
                out=fgrid[:, 0:128], in_=h_i32[:]), waits=[(sv, t0)], inc=sv)
            t_h128 = R("vector", lambda e: e.tensor_scalar(
                out=h128[:], in0=fgrid[:, 0:128], scalar1=-2.0, scalar2=1.0,
                op0=ALU.mult, op1=ALU.add), waits=[(sv, t0)], inc=sv)
            t_h128n = R("vector", lambda e: e.tensor_scalar(
                out=h128n[:], in0=fgrid[:, 0:128], scalar1=2.0, scalar2=-1.0,
                op0=ALU.mult, op1=ALU.add), waits=[(sv, t0)], inc=sv)

            t1 = R("scalar", lambda e: e.activation(
                out=irow[:], in_=irow[:], func=AF.Exp,
                scale=-math.log(10000.0) / 64.0), waits=[(sq, t_iota)], inc=sa)
            ang_toks = []
            for sl in range(S // 512):
                b = sl % 2
                tmm = R("tensor", lambda e, sl=sl, b=b: e.matmul(
                    PS[b][0:64, :], irow[:, :],
                    trow[:, sl * 512:(sl + 1) * 512], start=True, stop=True),
                    waits=[(sa, t1), (sq, t_iota), pr[b]], inc=sp)
                tev = R("vector", lambda e, sl=sl, b=b: e.tensor_copy(
                    out=ang[:, sl * 512:(sl + 1) * 512], in_=PS[b][0:64, :]),
                    waits=[(sp, tmm)], inc=sv)
                pr[b] = (sv, tev)
            t_ang = sv.n
            C1 = float(np.float32(6.28125))
            C2 = float(np.float32(0.0019353069))
            C3 = float(2 * PI - np.float64(np.float32(6.28125))
                       - np.float64(np.float32(0.0019353069)))

            def vchain(fn, t0):
                return R("vector", fn, waits=[(sv, t0)], inc=sv)

            t0 = vchain(lambda e: e.tensor_scalar(
                out=kf, in0=ang, scalar1=float(1.0 / (2 * PI)),
                scalar2=None, op0=ALU.mult), t_ang)
            t0 = vchain(lambda e: e.tensor_copy(out=ki, in_=kf), t0)
            t0 = vchain(lambda e: e.tensor_copy(out=kf, in_=ki), t0)
            t0 = vchain(lambda e: e.scalar_tensor_tensor(
                out=red, in0=kf, scalar=-C1, in1=ang,
                op0=ALU.mult, op1=ALU.add), t0)
            t0 = vchain(lambda e: e.scalar_tensor_tensor(
                out=red, in0=kf, scalar=-C2, in1=red,
                op0=ALU.mult, op1=ALU.add), t0)
            t0 = vchain(lambda e: e.scalar_tensor_tensor(
                out=red, in0=kf, scalar=-C3, in1=red,
                op0=ALU.mult, op1=ALU.add), t0)
            t0 = vchain(lambda e: e.tensor_scalar(
                out=kf, in0=red, scalar1=PI, scalar2=None,
                op0=ALU.is_gt), t0)
            t0 = vchain(lambda e: e.scalar_tensor_tensor(
                out=red, in0=kf, scalar=-2 * PI, in1=red,
                op0=ALU.mult, op1=ALU.add), t0)
            t0 = vchain(lambda e: e.tensor_scalar(
                out=kf, in0=red, scalar1=-PI, scalar2=None,
                op0=ALU.is_lt), t0)
            t0 = vchain(lambda e: e.scalar_tensor_tensor(
                out=red, in0=kf, scalar=2 * PI, in1=red,
                op0=ALU.mult, op1=ALU.add), t0)
            t_sin = R("scalar", lambda e: e.activation(
                out=sin_t[0:64, :], in_=red, func=AF.Sin),
                waits=[(sv, t0), (sa, t1)], inc=sa)
            t_sin2 = R("vector", lambda e: e.tensor_copy(
                out=sin_t[64:128, :], in_=sin_t[0:64, :]),
                waits=[(sa, t_sin)], inc=sv)
            t0 = vchain(lambda e: e.tensor_scalar(
                out=ang, in0=red, scalar1=PI / 2, scalar2=None,
                op0=ALU.add), t0)
            t0 = vchain(lambda e: e.tensor_scalar(
                out=kf, in0=ang, scalar1=PI, scalar2=None,
                op0=ALU.is_gt), t0)
            t0 = vchain(lambda e: e.scalar_tensor_tensor(
                out=ang, in0=kf, scalar=-2 * PI, in1=ang,
                op0=ALU.mult, op1=ALU.add), t0)
            t_cos = R("scalar", lambda e: e.activation(
                out=cos_t[0:64, :], in_=ang, func=AF.Sin),
                waits=[(sv, t0), (sa, t_sin)], inc=sa)
            t_cos2 = R("vector", lambda e: e.tensor_copy(
                out=cos_t[64:128, :], in_=cos_t[0:64, :]),
                waits=[(sa, t_cos)], inc=sv)
            t_rope_tab = max(t_sin2, t_cos2)

        t_p1 = [(sv, sv.n), (sa, sa.n), (sp, sp.n), (sq, sq.n)]

        with ExitStack() as pre1:
            def p1(name, shape, dt):
                return pre1.enter_context(nc.sbuf_tensor(name, shape, dt))

            for (dst, src) in [(bq_sb, bqh), (bk_sb, bkh), (lnw_sb, lnwh)]:
                R("gpsimd", lambda e, dst=dst, src=src: e.dma_start(
                    out=dst[:], in_=src[:, :]), inc=sg_res, incn=16)
            R("gpsimd", lambda e: e.dma_start(
                out=stat_m[:, 0:KV], in_=bvr[:, :]), inc=sg_res, incn=16)
            R("gpsimd", lambda e: e.dma_start(
                out=wk_sb[:].rearrange("p (g d) -> p g d", d=DM),
                in_=wkh.rearrange("g p d -> p g d")),
              inc=sg_res, incn=16)
            t_wkv = R("gpsimd", lambda e: e.dma_start(
                out=wv_sb[:], in_=wvh.rearrange("p i e -> p (i e)")),
                inc=sg_res, incn=16)
            tb = R("tensor", lambda e: e.matmul(
                PS[2][:, :], ones_f[:, :], stat_m[:, 0:KV],
                start=True, stop=True),
                waits=[(sg_res, t_wkv), (sq, t_iota), pr[2]], inc=sp)
            tb = R("vector", lambda e: e.tensor_copy(
                out=bv_bc[:], in_=PS[2][:, :]), waits=[(sp, tb)], inc=sv)
            pr[2] = (sv, tb)

            castbuf = [p1(f"castb{i}", [128, DM], BF16) for i in range(2)]
            cast_free = [list(t_p1), list(t_p1)]
            nblk = 0
            for (srcw, dstw, n) in [(wq, wqs, NQH), (wo, wos, NDM),
                                    (w1, w1s, NH), (w3, w3s, NH)]:
                for blk in range(n):
                    bsel = nblk % 2
                    tl = R("gpsimd", lambda e, srcw=srcw, blk=blk, bsel=bsel:
                           e.dma_start(out=castbuf[bsel][:],
                                       in_=srcw[blk, :, :]),
                           waits=(cast_free[bsel]
                                  if isinstance(cast_free[bsel], list)
                                  else [cast_free[bsel]]),
                           inc=sg_pre[bsel], incn=16)
                    ts_ = R("sync", lambda e, dstw=dstw, blk=blk, bsel=bsel:
                            e.dma_start(out=dstw[blk, :, :],
                                        in_=castbuf[bsel][:]),
                            waits=[(sg_pre[bsel], tl)], inc=s_cst[bsel],
                            incn=16)
                    cast_free[bsel] = [(s_cst[bsel], ts_)]
                    nblk += 1

            w2ff = p1("w2ff", [128, NH * 64], F32)
            w2bf = p1("w2bf", [128, NH * 128], BF16)
            w2out = p1("w2out", [128, NH * 128], BF16)
            Av = p1("Av", [128, NH * 128], F32)
            Bv = p1("Bv", [128, NH * 128], F32)
            w2ff_free = list(t_p1)
            t_st_prev = (None, 0)
            t_mm_last = 0
            tq_prev = t_p1[0][1]

            def bfly(eng, semx, src_, dst, bit, t0_tok):
                n_inner = bit * 128
                toks = []
                for (sign, outoff) in ((ALU.add, 0), (ALU.subtract, n_inner)):
                    toks.append(R(eng,
                                  lambda e, src_=src_, dst=dst,
                                  n_inner=n_inner, sign=sign,
                                  outoff=outoff: e.tensor_tensor(
                                      out=dst[:].rearrange(
                                          "p (q i) -> p q i",
                                          i=2 * n_inner)[
                                          :, :, outoff:outoff + n_inner],
                                      in0=src_[:].rearrange(
                                          "p (q i) -> p q i",
                                          i=2 * n_inner)[:, :, 0:n_inner],
                                      in1=src_[:].rearrange(
                                          "p (q i) -> p q i",
                                          i=2 * n_inner)[
                                          :, :, n_inner:2 * n_inner],
                                      op=sign),
                                  waits=[t0_tok], inc=semx))
                return toks[-1]

            bf_free = [list(t_p1), list(t_p1)]
            out_free = [(None, 0), (None, 0)]
            for s_ in range(ND):
                tcasts = []
                for hfs in range(2):
                    tl = R("scalar", lambda e, s_=s_, hfs=hfs: e.dma_start(
                        out=w2ff[:],
                        in_=w2sl[s_, :, hfs * (NH * 64):(hfs + 1) * (NH * 64)]),
                        waits=(w2ff_free if isinstance(w2ff_free, list)
                               else [w2ff_free]), inc=s_wff, incn=16)
                    for pc in range(2):
                        tc = R("scalar", lambda e, pc=pc, hfs=hfs: e.copy(
                            out=w2bf[:, (hfs * 2 + pc) * 2048:
                                     (hfs * 2 + pc + 1) * 2048],
                            in_=w2ff[:, pc * 2048:(pc + 1) * 2048]),
                            waits=[(s_wff, tl), (sp, t_mm_last)], inc=sa)
                        tcasts.append(tc)
                    w2ff_free = [(sa, tcasts[-1])]
                t_ev_h = [0, 0]
                for og in range(16):
                    half = og // 8
                    b_ = og % 2
                    hi = og >> 1
                    t_mm = 0
                    for m in range(8):
                        hi2 = hi ^ m
                        ig = (hi2 << 1) | (og & 1)
                        sign = bin(hi & hi2).count("1") & 1
                        lhs = h128n if sign else h128
                        w_mm = [(sa, tcasts[3]), (sv, t_h128n)]
                        if m == 0:
                            w_mm.append(pr[b_])
                        t_mm = R("tensor",
                                 lambda e, ig=ig, lhs=lhs, b_=b_, m=m:
                                 e.matmul(PS[b_][:, :], lhs[:, :],
                                          w2bf[:, ig * 512:(ig + 1) * 512],
                                          start=(m == 0), stop=(m == 7)),
                                 waits=w_mm, inc=sp)
                    t_ev = R("scalar", lambda e, og=og, b_=b_: e.copy(
                        out=Av[:, og * 512:(og + 1) * 512],
                        in_=PS[b_][:, :]),
                        waits=[(sp, t_mm)] +
                        (bf_free[half] if og % 8 == 0 else []),
                        inc=sa)
                    pr[b_] = (sa, t_ev)
                    t_ev_h[half] = t_ev
                t_mm_last = sp.n
                for half in range(2):
                    hof = half * (NH * 64)
                    hl = NH * 64
                    t_bb_ = (sa, t_ev_h[half])
                    srcb, dstb = Av, Bv
                    for bit in (1, 2, 4):
                        n_inner = bit * 128
                        for (sign, outoff) in ((ALU.add, 0),
                                               (ALU.subtract, n_inner)):
                            t_bb_ = (sv, R("vector",
                                     lambda e, srcb=srcb, dstb=dstb,
                                     n_inner=n_inner, sign=sign,
                                     outoff=outoff, hof=hof, hl=hl:
                                     e.tensor_tensor(
                                         out=dstb[:, hof:hof + hl].rearrange(
                                             "p (q i) -> p q i",
                                             i=2 * n_inner)[
                                             :, :, outoff:outoff + n_inner],
                                         in0=srcb[:, hof:hof + hl].rearrange(
                                             "p (q i) -> p q i",
                                             i=2 * n_inner)[:, :, 0:n_inner],
                                         in1=srcb[:, hof:hof + hl].rearrange(
                                             "p (q i) -> p q i",
                                             i=2 * n_inner)[
                                             :, :, n_inner:2 * n_inner],
                                         op=sign),
                                     waits=[t_bb_], inc=sv))
                        srcb, dstb = dstb, srcb
                    bf_free[half] = [t_bb_]
                    tsc = R("vector", lambda e, hof=hof, hl=hl:
                            e.tensor_scalar(
                                out=w2out[:, hof:hof + hl],
                                in0=Bv[:, hof:hof + hl],
                                scalar1=FWHT_SCALE, scalar2=None,
                                op0=ALU.mult),
                            waits=[t_bb_, out_free[half]], inc=sv)
                    tst = R("scalar", lambda e, s_=s_, hof=hof, hl=hl:
                            e.dma_start(
                                out=w2p[s_, :, hof:hof + hl],
                                in_=w2out[:, hof:hof + hl]),
                            waits=[(sv, tsc)], inc=s_w2st[half], incn=16)
                    out_free[half] = (s_w2st[half], s_w2st[half].n)
        t_pre_all = [(sd_pre, sd_pre.n), (s_wff, s_wff.n),
                     (s_w2st[0], s_w2st[0].n), (s_w2st[1], s_w2st[1].n),
                     (sa, sa.n), (sv, sv.n), (sp, sp.n), (sq, sq.n)]
        t_pre_sync = sd_pre.n
        t_cst0 = s_cst[0].n
        t_cst1 = s_cst[1].n
        t_pre_g0 = sg_pre[0].n
        t_pre_g1 = sg_pre[1].n
        tok_scratch_ready = [(sd_pre, t_pre_sync), (s_cst[0], t_cst0),
                             (s_cst[1], t_cst1)]

        tok_Elast = (None, 0)
        tok_Eapply = (None, 0)
        tok_outst = (None, 0)
        tok_outst2 = (None, 0)
        tok_prev_spill = (None, 0)
        tok_kvstore = (None, 0)
        tok_x1spill = [(None, 0)] * NCH
        tok_kbuf_free = [(None, 0), (None, 0)]
        tok_vbuf_free = [(None, 0), (None, 0)]
        tok_wqbuf_free = [(None, 0), (None, 0)]
        tok_w13_free = [(None, 0), (None, 0)]
        tok_w13h_free = [[(None, 0), (None, 0)], [(None, 0), (None, 0)]]
        tok_w2buf_free = [(None, 0), (None, 0)]
        tok_qT_lastread = (None, 0)
        tok_aout_lastread = (None, 0)
        tok_hT_lastread = (None, 0)


        with ExitStack() as ck:
            def cb(name, shape, dt):
                return ck.enter_context(nc.sbuf_tensor(name, shape, dt))
            xc = cb("xc", [128, NDM * CH], F32)
            x1bf = cb("x1bf", [128, NDM * CH], BF16)
            rcbuf = [cb(f"rc{i}", [128, 512], F32) for i in range(2)]
            with ExitStack() as g1:
                def g1b(name, shape, dt):
                    return g1.enter_context(nc.sbuf_tensor(name, shape, dt))
                qT = g1b("qT", [128, NQH * CH], BF16)
                aoutT = g1b("aoutT", [128, NQH * CH], BF16)
                with ExitStack() as gA:
                    def gAb(name, shape, dt):
                        return gA.enter_context(
                            nc.sbuf_tensor(name, shape, dt))
                    hT = gAb("hT", [128, NDM * CH], BF16)
                    sqb = [gAb(f"sq{i}", [128, CH], BF16) for i in range(2)]
                    ktmp = [gAb(f"ktmp{i}", [128, CH], F32) for i in range(2)]
                    rot_b = [gAb(f"rot{i}", [128, CH], F32) for i in range(2)]
                    tc_b = [gAb(f"tc{i}", [128, CH], F32) for i in range(2)]
                    wqbuf = [gAb(f"wqb{i}", [128, DM], BF16) for i in range(2)]
                    knew = gAb("knew", [128, NKVH * CH], BF16)
                    vnew = gAb("vnew", [128, NKVH * CH], BF16)
                    ksq = gAb("ksq", [128, CH], BF16)
                with ExitStack() as gB:
                    def gBb(name, shape, dt):
                        return gB.enter_context(
                            nc.sbuf_tensor(name, shape, dt))
                    kbuf = [gBb(f"kb{i}", [128, S], BF16) for i in range(2)]
                    vbuf = [gBb(f"vb{i}", [128, S], BF16) for i in range(2)]
                    PT = [gBb(f"pt{i}", [128, CH], BF16) for i in range(4)]
                    qsq = gBb("qsq", [128, CH], BF16)
                with ExitStack() as gC:
                    def gCb(name, shape, dt):
                        return gC.enter_context(
                            nc.sbuf_tensor(name, shape, dt))
                    wobuf = [gCb(f"wob{i}", [128, DM], BF16)
                             for i in range(2)]
                    ybf = [gCb(f"ybf{i}", [128, CH], BF16) for i in range(2)]
                    ysq = [gCb(f"ysq{i}", [128, CH], BF16) for i in range(2)]
            with ExitStack() as g2:
                def g2b(name, shape, dt):
                    return g2.enter_context(nc.sbuf_tensor(name, shape, dt))
                hidden = g2b("hidden", [128, NH * CH], BF16)
                with ExitStack() as gD:
                    def gDb(name, shape, dt):
                        return gD.enter_context(
                            nc.sbuf_tensor(name, shape, dt))
                    w1b = [gDb(f"w1b{i}", [128, DM // 2], BF16)
                           for i in range(2)]
                    w3b = [gDb(f"w3b{i}", [128, DM // 2], BF16)
                           for i in range(2)]
                    silub = [gDb(f"silu{i}", [128, CH], F32)
                             for i in range(2)]
                with ExitStack() as gE:
                    def gEb(name, shape, dt):
                        return gE.enter_context(
                            nc.sbuf_tensor(name, shape, dt))
                    w2b = [gEb(f"w2b{i}", [128, HD // 8], BF16)
                           for i in range(2)]
                    y2t = [gEb(f"y2t{i}", [128, CH], F32) for i in range(2)]
                    x1rd = [gEb(f"x1rd{i}", [128, CH], F32)
                            for i in range(2)]
                    y2bf = [gEb(f"y2bf{i}", [128, CH], BF16)
                            for i in range(2)]
        y2sq = y2bf
        outb = y2t

        for c in range(NCH):
            sp = sp_ring[c % 4]
            NKT = 4 * (c + 1)
            if True:
                if True:
                    if True:
                        xw = [tok_prev_spill,
                              (sg_pre[0], t_pre_g0),
                              (sg_pre[1], t_pre_g1), (sg_res, sg_res.n),
                              (s_cst[0], t_cst0), (s_cst[1], t_cst1)] + \
                            t_pre_all
                        for ii in range(4):
                            t_x = R("sync", lambda e, ii=ii, c=c: e.dma_start(
                                out=xc[:, ii * 4 * CH:(ii + 1) * 4 * CH]
                                .rearrange("p (i t) -> p i t", t=CH),
                                in_=xT[ii * 512:(ii + 1) * 512,
                                       c * CH:(c + 1) * CH]
                                .rearrange("(i p) t -> p i t", p=128)),
                                waits=xw, inc=sd_x, incn=16)
                        tok_xload = t_x

                        sq_free = [(None, 0), (None, 0)]
                        t_mm = 0
                        for i in range(NDM):
                            bsel = i % 2
                            w_sq = [(sd_x, tok_xload), sq_free[bsel]]
                            if i == 0:
                                w_sq += [tok_Elast]
                            t_sq = R("scalar", lambda e, i=i, bsel=bsel:
                                     e.activation(
                                         out=sqb[bsel][:],
                                         in_=xc[:, i * CH:(i + 1) * CH],
                                         func=AF.Square),
                                     waits=w_sq, inc=sa)
                            t_mm = R("tensor", lambda e, i=i, bsel=bsel:
                                     e.matmul(PS[2][0:1, :], ones_bf[:, :],
                                              sqb[bsel][:],
                                              start=(i == 0),
                                              stop=(i == NDM - 1)),
                                     waits=[(sa, t_sq)] +
                                     ([pr[2]] if i == 0 else []), inc=sp)
                            sq_free[bsel] = (sp, t_mm)
                        t_ln = R("scalar", lambda e: e.activation(
                            out=stat_m[:, :], in_=PS[2][0:1, :], func=AF.Ln,
                            bias=epsr[0:1, :], scale=1.0 / DM),
                            waits=[(sp, t_mm)], inc=sa)
                        t_ex = R("scalar", lambda e: e.activation(
                            out=stat_m[:, :], in_=stat_m[:, :], func=AF.Exp,
                            scale=-0.5), waits=[(sa, t_ln)], inc=sa)
                        pr[2] = (sa, t_ln)
                        t_bc = R("tensor", lambda e: e.matmul(
                            PS[3][:, :], ones_f[:, :], stat_m[0:1, :],
                            start=True, stop=True),
                            waits=[(sa, t_ex), pr[3]], inc=sp)
                        t_rms = R("scalar", lambda e: e.copy(
                            out=rcbuf[0][:], in_=PS[3][:, :]),
                            waits=[(sp, t_bc)], inc=sa)
                        pr[3] = (sa, t_rms)
                        t_hT = 0
                        hT_toks = [0] * NDM
                        for i in range(NDM):
                            t_hT = R("vector", lambda e, i=i:
                                     e.scalar_tensor_tensor(
                                         out=hT[:, i * CH:(i + 1) * CH],
                                         in0=xc[:, i * CH:(i + 1) * CH],
                                         scalar=lnw_sb[:, i:i + 1],
                                         in1=rcbuf[0][:],
                                         op0=ALU.mult, op1=ALU.mult),
                                     waits=[(sa, t_rms), tok_hT_lastread],
                                     inc=sv)
                            hT_toks[i] = t_hT
                        tok_hTready = t_hT

                        tok_knorm = []
                        for g in range(NKVH):
                            b = g % 2
                            t_g = 0
                            for i in range(NDM):
                                t_g = R("tensor", lambda e, g=g, i=i, b=b:
                                        e.matmul(
                                            PS[b][:, :],
                                            wk_sb[:, (g * NDM + i) * 128:
                                                  (g * NDM + i + 1) * 128],
                                            hT[:, i * CH:(i + 1) * CH],
                                            start=(i == 0),
                                            stop=(i == NDM - 1)),
                                        waits=[(sv, hT_toks[i])] +
                                        ([pr[b]] if i == 0 else []), inc=sp)
                            w_kb = [(sp, t_g)]
                            if g < 2:
                                w_kb += [tok_outst, tok_outst2]
                            t_kb = R("vector", lambda e, g=g, b=b:
                                     e.tensor_scalar(
                                         out=ktmp[b][:], in0=PS[b][:, :],
                                         scalar1=bk_sb[:, g:g + 1],
                                         scalar2=None, op0=ALU.add),
                                     waits=w_kb, inc=sv)
                            pr[b] = (sv, t_kb)
                            cosc = cos_t[:, c * CH:(c + 1) * CH]
                            sinc = sin_t[:, c * CH:(c + 1) * CH]
                            R("vector", lambda e, b=b: e.tensor_scalar(
                                out=rot_b[b][0:64, :],
                                in0=ktmp[b][64:128, :], scalar1=-1.0,
                                scalar2=None, op0=ALU.mult),
                              waits=[(sv, t_kb), (sv, t_rope_tab)])
                            t_r2 = R("vector", lambda e, b=b: e.tensor_copy(
                                out=rot_b[b][64:128, :],
                                in_=ktmp[b][0:64, :]),
                                waits=[(sv, t_kb)], inc=sv)
                            t_tc = R("vector", lambda e, b=b, cosc=cosc:
                                     e.tensor_tensor(
                                         out=tc_b[b][:], in0=ktmp[b][:],
                                         in1=cosc, op=ALU.mult),
                                     waits=[(sv, t_kb)], inc=sv)
                            t_ts = R("vector", lambda e, b=b, sinc=sinc:
                                     e.tensor_tensor(
                                         out=rot_b[b][:], in0=rot_b[b][:],
                                         in1=sinc, op=ALU.mult),
                                     waits=[(sv, t_r2)], inc=sv)
                            t_hi = R("vector", lambda e, g=g, b=b:
                                     e.tensor_tensor(
                                         out=knew[:, g * CH:(g + 1) * CH],
                                         in0=tc_b[b][:], in1=rot_b[b][:],
                                         op=ALU.add),
                                     waits=[(sv, t_ts), (sv, t_tc)], inc=sv)
                            t_lo = t_hi
                            t_ks = R("vector", lambda e, g=g: e.tensor_tensor(
                                out=ksq[:],
                                in0=knew[:, g * CH:(g + 1) * CH],
                                in1=knew[:, g * CH:(g + 1) * CH],
                                op=ALU.mult), waits=[(sv, t_hi)], inc=sv)
                            t_kn = R("tensor", lambda e: e.matmul(
                                PS[4][0:1, :], ones_bf[:, :], ksq[:],
                                start=True, stop=True),
                                waits=[(sv, t_ks), pr[4]], inc=sp)
                            t_km = R("vector", lambda e: e.reduce_max(
                                out=stat_t[:, 6:7], in_=PS[4][0:1, :],
                                axis=AXX), waits=[(sp, t_kn)], inc=sv)
                            pr[4] = (sv, t_km)
                            t_ku = R("vector", lambda e, g=g: e.tensor_tensor(
                                out=kmax2[:, g:g + 1], in0=kmax2[:, g:g + 1],
                                in1=stat_t[:, 6:7], op=ALU.max),
                                waits=[(sv, t_km)], inc=sv)
                            tok_knorm.append(t_ku)
                            t_kst = R("scalar", lambda e, g=g, c=c:
                                      e.dma_start(
                                          out=kscr[g, :,
                                                   c * CH:(c + 1) * CH],
                                          in_=knew[:, g * CH:(g + 1) * CH]),
                                      waits=[(sv, t_hi)],
                                      inc=s_kvst, incn=16)

                        for sub in range(4):
                            b = sub % 2
                            t_g = 0
                            for i in range(NDM):
                                t_g = R("tensor", lambda e, sub=sub, i=i, b=b:
                                        e.matmul(
                                            PS[b][:, :],
                                            hT[:, i * CH + sub * 128:
                                               i * CH + (sub + 1) * 128],
                                            wv_sb[:, i * KV:(i + 1) * KV],
                                            start=(i == 0),
                                            stop=(i == NDM - 1)),
                                        waits=[(sv, hT_toks[i])] +
                                        ([pr[b]] if i == 0 else []), inc=sp)
                            w_vb = [(sp, t_g)]
                            if sub < 2:
                                w_vb += [tok_outst, tok_outst2]
                            t_vb = R("vector", lambda e, sub=sub, b=b:
                                     e.tensor_tensor(
                                         out=vnew[:, sub * KV:(sub + 1) * KV],
                                         in0=PS[b][:, :], in1=bv_bc[:],
                                         op=ALU.add),
                                     waits=w_vb, inc=sv)
                            pr[b] = (sv, t_vb)
                            for g in range(NKVH):
                                R("scalar", lambda e, sub=sub, g=g, c=c:
                                  e.dma_start(
                                      out=vscr[g, c * 4 + sub, :, :],
                                      in_=vnew[:, sub * KV + g * 128:
                                               sub * KV + (g + 1) * 128]),
                                  waits=[(sv, t_vb)], inc=s_kvst, incn=16)

                        tok_ropeq = [0] * NQH
                        for h in range(NQH):
                            b = h % 2
                            w_wq = [tok_wqbuf_free[b],
                                    (sd_pre, t_pre_sync),
                                    (s_cst[0], t_cst0), (s_cst[1], t_cst1)]
                            if h < 2:
                                w_wq += [tok_Elast, tok_outst, tok_outst2]
                            t_wl = R("sync", lambda e, h=h, b=b: e.dma_start(
                                out=wqbuf[b][:], in_=wqs[h, :, :]),
                                waits=w_wq, inc=s_qo[b], incn=16)
                            t_g = 0
                            for i in range(NDM):
                                t_g = R("tensor", lambda e, h=h, i=i, b=b:
                                        e.matmul(
                                            PS[b][:, :],
                                            wqbuf[b][:, i * 128:(i + 1) * 128],
                                            hT[:, i * CH:(i + 1) * CH],
                                            start=(i == 0),
                                            stop=(i == NDM - 1)),
                                        waits=[(sv, hT_toks[i]),
                                               (s_qo[b], t_wl)] +
                                        ([pr[b]] if i == 0 else []), inc=sp)
                            tok_wqbuf_free[b] = (sp, t_g)
                            t_qb = R("vector", lambda e, h=h, b=b:
                                     e.tensor_scalar(
                                         out=ktmp[b][:], in0=PS[b][:, :],
                                         scalar1=bq_sb[:, h:h + 1],
                                         scalar2=None, op0=ALU.add),
                                     waits=[(sp, t_g)], inc=sv)
                            pr[b] = (sv, t_qb)
                            cosc = cos_t[:, c * CH:(c + 1) * CH]
                            sinc = sin_t[:, c * CH:(c + 1) * CH]
                            R("vector", lambda e, b=b: e.tensor_scalar(
                                out=rot_b[b][0:64, :],
                                in0=ktmp[b][64:128, :], scalar1=-1.0,
                                scalar2=None, op0=ALU.mult),
                              waits=[(sv, t_qb), (sv, t_rope_tab)])
                            t_r2 = R("vector", lambda e, b=b: e.tensor_copy(
                                out=rot_b[b][64:128, :],
                                in_=ktmp[b][0:64, :]),
                                waits=[(sv, t_qb)], inc=sv)
                            t_tc = R("vector", lambda e, b=b, cosc=cosc:
                                     e.tensor_tensor(
                                         out=tc_b[b][:], in0=ktmp[b][:],
                                         in1=cosc, op=ALU.mult),
                                     waits=[(sv, t_qb)], inc=sv)
                            t_ts = R("vector", lambda e, b=b, sinc=sinc:
                                     e.tensor_tensor(
                                         out=rot_b[b][:], in0=rot_b[b][:],
                                         in1=sinc, op=ALU.mult),
                                     waits=[(sv, t_r2)], inc=sv)
                            t_hi = R("vector", lambda e, h=h, b=b:
                                     e.tensor_tensor(
                                         out=qT[:, h * CH:(h + 1) * CH],
                                         in0=tc_b[b][:], in1=rot_b[b][:],
                                         op=ALU.add),
                                     waits=[(sv, t_ts), (sv, t_tc)], inc=sv)
                            tok_ropeq[h] = t_hi
                        tok_qkv_done = sp.n
                        tok_Av = sv.n
                        tok_kvstore = (s_kvst, s_kvst.n)

                    if True:
                        tok_kload = [0, 0]
                        tok_vload = [0, 0]

                        def issue_kv(g):
                            b = g % 2
                            tok_kload[b] = R(
                                "scalar", lambda e, g=g, b=b, NKT=NKT:
                                e.dma_start(
                                    out=kbuf[b][:, 0:NKT * 128],
                                    in_=kscr[g, :, 0:NKT * 128]),
                                waits=[tok_kvstore, tok_kbuf_free[b],
                                       (sp, tok_qkv_done), (sv, tok_Av),
                                       tok_outst, tok_outst2],
                                inc=s_kv[b], incn=16)
                            tv = 0
                            nj = NKT
                            j0 = 0
                            while j0 < nj:
                                jn = min(8, nj - j0)
                                tv = R("scalar",
                                       lambda e, g=g, b=b, j0=j0, jn=jn:
                                       e.dma_start(
                                           out=vbuf[b][:, j0 * 128:
                                                       (j0 + jn) * 128]
                                           .rearrange("t (j d) -> t j d",
                                                      d=128),
                                           in_=vscr[g, j0:j0 + jn, :, :]
                                           .rearrange("j t d -> t j d")),
                                       waits=[tok_kvstore, tok_vbuf_free[b],
                                              (sp, tok_qkv_done),
                                              (sv, tok_Av)],
                                       inc=s_kv[b], incn=16)
                                j0 += jn
                            tok_vload[b] = tv

                        issue_kv(0)
                        if NKVH > 1:
                            issue_kv(1)

                        tok_exp = {}
                        tok_pv = {}
                        tok_norm = [(None, 0), (None, 0)]
                        tok_bias_done = [(None, 0), (None, 0)]
                        tok_qn_read = [(None, 0), (None, 0)]
                        ptfree = [(None, 0)] * 4

                        tok_bias_arr = {}
                        tok_qm_arr = {}

                        def emit_bias(h):
                            p_ = h % 2
                            g = h // 4
                            w_qs = [(sv, tok_ropeq[h]), tok_qn_read[p_]]
                            if h < 2:
                                w_qs += [tok_outst, tok_outst2]
                            t_qs = R("vector", lambda e, h=h:
                                     e.tensor_tensor(
                                         out=qsq[:],
                                         in0=qT[:, h * CH:(h + 1) * CH],
                                         in1=qT[:, h * CH:(h + 1) * CH],
                                         op=ALU.mult),
                                     waits=w_qs, inc=sv)
                            t_qn = R("tensor", lambda e, p_=p_: e.matmul(
                                PS[6 + p_][0:1, :], ones_bf[:, :],
                                qsq[:], start=True, stop=True),
                                waits=[(sv, t_qs), pr[6 + p_]], inc=sp)
                            t_qm = R("vector", lambda e, p_=p_:
                                     e.reduce_max(
                                         out=stat_t[:, p_:p_ + 1],
                                         in_=PS[6 + p_][0:1, :],
                                         axis=AXX),
                                     waits=[(sp, t_qn)], inc=sv)
                            tok_qn_read[p_] = (sv, t_qm)
                            t_pr = R("vector", lambda e, g=g, p_=p_:
                                     e.tensor_tensor(
                                         out=stat_t[:, 2 + p_:3 + p_],
                                         in0=stat_t[:, p_:p_ + 1],
                                         in1=kmax2[:, g:g + 1],
                                         op=ALU.mult),
                                     waits=[(sv, t_qm),
                                            (sv, tok_knorm[g])], inc=sv)
                            t_l1 = R("scalar", lambda e, p_=p_:
                                     e.activation(
                                         out=stat_t[:, 2 + p_:3 + p_],
                                         in_=stat_t[:, 2 + p_:3 + p_],
                                         func=AF.Ln),
                                     waits=[(sv, t_pr)], inc=sa)
                            t_l2 = R("scalar", lambda e, p_=p_:
                                     e.activation(
                                         out=stat_t[:, 2 + p_:3 + p_],
                                         in_=stat_t[:, 2 + p_:3 + p_],
                                         func=AF.Exp, scale=0.5),
                                     waits=[(sa, t_l1)], inc=sa)
                            t_bb = R("vector", lambda e, p_=p_:
                                     e.tensor_scalar(
                                         out=stat_t[:, 2 + p_:3 + p_],
                                         in0=stat_t[:, 2 + p_:3 + p_],
                                         scalar1=-INVSQ, scalar2=-2.0,
                                         op0=ALU.mult, op1=ALU.add),
                                     waits=[(sa, t_l2)], inc=sv)
                            t_bm = R("tensor", lambda e, p_=p_: e.matmul(
                                PS[6 + p_][:, 0:1], ones_f[:, :],
                                stat_t[0:1, 2 + p_:3 + p_],
                                start=True, stop=True),
                                waits=[(sv, t_bb), (sv, t_qm),
                                       pr[6 + p_]], inc=sp)
                            t_bv = R("vector", lambda e, p_=p_:
                                     e.tensor_copy(
                                         out=bias_h[:, p_:p_ + 1],
                                         in_=PS[6 + p_][:, 0:1]),
                                     waits=[(sp, t_bm),
                                            tok_bias_done[p_]], inc=sv)
                            pr[6 + p_] = (sv, t_bv)
                            tok_bias_arr[h] = t_bv
                            tok_qm_arr[h] = t_qm

                        emit_bias(0)
                        for g in range(NKVH):
                            gb = g % 2
                            for hh in range(4):
                                h = 4 * g + hh
                                p_ = h % 2
                                if h + 1 < NQH:
                                    emit_bias(h + 1)
                                tok_bias = tok_bias_arr[h]
                                t_qm = tok_qm_arr[h]
                                SBK = (0, 1, 4, 5)
                                t_l_end = 0
                                t_s = {}
                                t_ready = {}
                                ngrp = (NKT + 3) // 4
                                for grp in range(ngrp + 1):
                                    if grp >= 1:
                                        for j in range(4 * (grp - 1),
                                                       min(4 * grp, NKT)):
                                            w_pv = [t_ready[j],
                                                    (s_kv[gb],
                                                     tok_vload[gb])]
                                            if j == 0:
                                                w_pv.append(pr[2 + p_])
                                            t_pv = R(
                                                "tensor",
                                                lambda e, j=j, gb=gb, p_=p_,
                                                NKT=NKT: e.matmul(
                                                    PS[2 + p_][:, :],
                                                    vbuf[gb][:, j * 128:
                                                             (j + 1) * 128],
                                                    PT[j % 4][:],
                                                    start=(j == 0),
                                                    stop=(j == NKT - 1)),
                                                waits=w_pv, inc=sp)
                                            tok_pv[(h, j)] = t_pv
                                            w_l = [t_ready[j]]
                                            if j == 0:
                                                w_l.append(pr[6 + p_])
                                                w_l.append((sv, t_qm))
                                            t_l_end = R(
                                                "tensor",
                                                lambda e, j=j, p_=p_,
                                                NKT=NKT: e.matmul(
                                                    PS[6 + p_][0:1, :],
                                                    ones_bf[:, :],
                                                    PT[j % 4][:],
                                                    start=(j == 0),
                                                    stop=(j == NKT - 1)),
                                                waits=w_l, inc=sp)
                                            ptfree[j % 4] = (sp, t_l_end)
                                    if grp < ngrp:
                                        for j in range(4 * grp,
                                                       min(4 * grp + 4, NKT)):
                                            sb_ = SBK[j % 4]
                                            waits_s = [
                                                (s_kv[gb], tok_vload[gb]),
                                                (sv, tok_ropeq[h]), pr[sb_]]
                                            t_s[j] = R(
                                                "tensor",
                                                lambda e, h=h, j=j, sb_=sb_,
                                                gb=gb: e.matmul(
                                                    PS[sb_][:, :],
                                                    kbuf[gb][:, j * 128:
                                                             (j + 1) * 128],
                                                    qT[:, h * CH:
                                                       (h + 1) * CH],
                                                    start=True, stop=True),
                                                waits=waits_s, inc=sp)
                                        for j in range(4 * grp,
                                                       min(4 * grp + 4, NKT)):
                                            sb_ = SBK[j % 4]
                                            w_exp = [(sp, t_s[j]),
                                                     (sv, tok_bias),
                                                     ptfree[j % 4]]
                                            if h < 2 and j < 4:
                                                w_exp.append((sv, tok_Av))
                                            t_e = R("scalar",
                                                    lambda e, sb_=sb_, j=j,
                                                    p_=p_: e.activation(
                                                        out=PT[j % 4][:],
                                                        in_=PS[sb_][:, :],
                                                        func=AF.Exp,
                                                        bias=bias_h[:,
                                                                    p_:p_ + 1],
                                                        scale=INVSQ),
                                                    waits=w_exp, inc=sa)
                                            pr[sb_] = (sa, t_e)
                                            tok_exp[(h, j)] = t_e
                                            t_ready[j] = (sa, t_e)
                                            if j >= 4 * c:
                                                jr = j - 4 * c
                                                t_mk = R(
                                                    "vector",
                                                    lambda e, j=j, jr=jr:
                                                    e.tensor_tensor(
                                                        out=PT[j % 4][:],
                                                        in0=PT[j % 4][:],
                                                        in1=masks[:,
                                                                  jr * CH:
                                                                  (jr + 1) *
                                                                  CH],
                                                        op=ALU.mult),
                                                    waits=[(sa, t_e)],
                                                    inc=sv)
                                                t_ready[j] = (sv, t_mk)
                                t_rc = R("vector", lambda e, p_=p_:
                                         e.reciprocal(
                                             out=stat_a[:, 0:CH],
                                             in_=PS[6 + p_][0:1, :]),
                                         waits=[(sp, t_l_end)], inc=sv)
                                pr[6 + p_] = (sv, t_rc)
                                t_rb = R("tensor", lambda e, p_=p_: e.matmul(
                                    PS[6 + p_][:, :], ones_f[:, :],
                                    stat_a[0:1, 0:CH], start=True, stop=True),
                                    waits=[(sv, t_rc), pr[6 + p_]], inc=sp)
                                t_re = R("vector", lambda e, p_=p_:
                                         e.tensor_copy(out=rcbuf[p_][:],
                                                       in_=PS[6 + p_][:, :]),
                                         waits=[(sp, t_rb)], inc=sv)
                                pr[6 + p_] = (sv, t_re)
                                t_nm = R("vector", lambda e, h=h, p_=p_:
                                         e.tensor_tensor(
                                             out=aoutT[:, h * CH:
                                                       (h + 1) * CH],
                                             in0=PS[2 + p_][:, :],
                                             in1=rcbuf[p_][:], op=ALU.mult),
                                         waits=[(sv, t_re),
                                                (sp, tok_pv[(h, NKT - 1)])],
                                         inc=sv)
                                pr[2 + p_] = (sv, t_nm)
                                tok_norm[p_] = (sv, t_nm)
                                tok_bias_done[p_] = (sa, tok_exp[(h, NKT - 1)])
                            tok_kbuf_free[gb] = (sp, sp.n)
                            tok_vbuf_free[gb] = (sp, sp.n)
                            if g + 2 < NKVH:
                                issue_kv(g + 2)
                        tok_qT_lastread = (sp, sp.n)
                        tok_Bact = (sa, sa.n)
                        tok_attn_done = [tok_norm[0], tok_norm[1]]

                    if True:
                        ybf_free = [(None, 0), (None, 0)]
                        ysq_free = [(None, 0), (None, 0)]
                        t_sum = t_ssq = 0
                        t_ytt = {}
                        for et in range(NDM):
                            b = et % 2
                            w_wo = [tok_wqbuf_free[b],
                                    (sd_pre, t_pre_sync),
                                    (s_cst[0], t_cst0), (s_cst[1], t_cst1)]
                            if et < 2:
                                w_wo += [tok_qT_lastread, tok_attn_done[0],
                                         tok_attn_done[1], tok_bias_done[0],
                                         tok_bias_done[1]]
                            t_wl = R("sync", lambda e, et=et, b=b:
                                     e.dma_start(out=wobuf[b][:],
                                                 in_=wos[et, :, :]),
                                     waits=w_wo, inc=s_qo[b], incn=16)
                            t_g = 0
                            for j in range(NDM):
                                w_mm = [(s_qo[b], t_wl)]
                                if j == 0:
                                    w_mm += [pr[b], tok_attn_done[0],
                                             tok_attn_done[1]]
                                t_g = R("tensor", lambda e, et=et, j=j, b=b:
                                        e.matmul(
                                            PS[b][:, :],
                                            wobuf[b][:, j * 128:(j + 1) * 128],
                                            aoutT[:, j * CH:(j + 1) * CH],
                                            start=(j == 0),
                                            stop=(j == NDM - 1)),
                                        waits=w_mm, inc=sp)
                            tok_wqbuf_free[b] = (sp, t_g)
                            t_y = R("vector", lambda e, et=et, b=b:
                                    e.tensor_tensor(
                                        out=xc[:, et * CH:(et + 1) * CH],
                                        in0=PS[b][:, :],
                                        in1=xc[:, et * CH:(et + 1) * CH],
                                        op=ALU.add),
                                    waits=[(sp, t_g)], inc=sv)
                            pr[b] = (sv, t_y)
                            t_ytt[et] = t_y
                            t_cb = R("vector", lambda e, et=et, b=b:
                                     e.tensor_copy(
                                         out=ybf[b][:],
                                         in_=xc[:, et * CH:(et + 1) * CH]),
                                     waits=[(sv, t_y), ybf_free[b]], inc=sv)
                            t_sum = R("tensor", lambda e, et=et, b=b:
                                      e.matmul(PS[2][0:1, :], ones_bf[:, :],
                                               ybf[b][:], start=(et == 0),
                                               stop=(et == NDM - 1)),
                                      waits=[(sv, t_cb)] +
                                      ([pr[2]] if et == 0 else []), inc=sp)
                            ybf_free[b] = (sp, t_sum)
                            t_sq2 = R("vector", lambda e, et=et, b=b:
                                      e.tensor_tensor(
                                          out=ysq[b][:], in0=ybf[b][:],
                                          in1=ybf[b][:], op=ALU.mult),
                                      waits=[(sv, t_cb), ysq_free[b]],
                                      inc=sv)
                            t_ssq = R("tensor", lambda e, et=et, b=b:
                                      e.matmul(PS[3][0:1, :], ones_bf[:, :],
                                               ysq[b][:], start=(et == 0),
                                               stop=(et == NDM - 1)),
                                      waits=[(sv, t_sq2)] +
                                      ([pr[3]] if et == 0 else []), inc=sp)
                            ysq_free[b] = (sp, t_ssq)

                        t_m = R("vector", lambda e: e.tensor_scalar(
                            out=stat_m[:, :], in0=PS[2][0:1, :],
                            scalar1=1.0 / DM, scalar2=None, op0=ALU.mult),
                            waits=[(sp, t_sum)], inc=sv)
                        pr[2] = (sv, t_m)
                        t_v = R("vector", lambda e: e.tensor_scalar(
                            out=stat_a[:, :], in0=PS[3][0:1, :],
                            scalar1=1.0 / DM, scalar2=None, op0=ALU.mult),
                            waits=[(sp, t_ssq)], inc=sv)
                        pr[3] = (sv, t_v)
                        t_t1 = R("vector", lambda e: e.tensor_tensor(
                            out=rcbuf[0][0:1, :], in0=stat_m[:, :],
                            in1=stat_m[:, :], op=ALU.mult),
                            waits=[(sv, t_v), (sv, t_m)], inc=sv)
                        t_t2 = R("vector", lambda e: e.tensor_tensor(
                            out=stat_a[:, :], in0=stat_a[:, :],
                            in1=rcbuf[0][0:1, :], op=ALU.subtract),
                            waits=[(sv, t_t1)], inc=sv)
                        t_t3 = R("scalar", lambda e: e.activation(
                            out=stat_a[:, :], in_=stat_a[:, :], func=AF.Ln,
                            bias=epsl[0:1, :]), waits=[(sv, t_t2)], inc=sa)
                        t_t4 = R("scalar", lambda e: e.activation(
                            out=stat_a[:, :], in_=stat_a[:, :], func=AF.Exp,
                            scale=-0.5), waits=[(sa, t_t3)], inc=sa)
                        t_b1 = R("tensor", lambda e: e.matmul(
                            PS[4][:, :], ones_f[:, :], stat_m[0:1, :],
                            start=True, stop=True),
                            waits=[(sv, t_m), pr[4]], inc=sp)
                        t_b2 = R("tensor", lambda e: e.matmul(
                            PS[5][:, :], ones_f[:, :], stat_a[0:1, :],
                            start=True, stop=True),
                            waits=[(sa, t_t4), pr[5]], inc=sp)
                        t_ap = 0
                        tmp_free = [(None, 0), (None, 0)]
                        for et in range(NDM):
                            b = et % 2
                            t_a1 = R("vector", lambda e, et=et, b=b:
                                     e.tensor_tensor(
                                         out=rcbuf[b][:],
                                         in0=xc[:, et * CH:(et + 1) * CH],
                                         in1=PS[4][:, :], op=ALU.subtract),
                                     waits=[(sp, t_b1), (sv, t_ytt[et]),
                                            (sv, t_t2), tmp_free[b]], inc=sv)
                            t_a2 = R("vector", lambda e, et=et, b=b:
                                     e.tensor_tensor(
                                         out=xc[:, et * CH:(et + 1) * CH],
                                         in0=rcbuf[b][:], in1=PS[5][:, :],
                                         op=ALU.mult),
                                     waits=[(sv, t_a1), (sp, t_b2)], inc=sv)
                            tmp_free[b] = (sv, t_a2)
                            t_ap = R("vector", lambda e, et=et:
                                     e.tensor_copy(
                                         out=x1bf[:, et * CH:(et + 1) * CH],
                                         in_=xc[:, et * CH:(et + 1) * CH]),
                                     waits=[(sv, t_a2)], inc=sv)
                        pr[4] = (sv, t_ap)
                        pr[5] = (sv, t_ap)
                        t_sp = 0
                        for ii in range(4):
                            t_sp = R("scalar", lambda e, ii=ii, c=c:
                                     e.dma_start(
                                         out=x1scr[ii * 512:(ii + 1) * 512,
                                                   c * CH:(c + 1) * CH]
                                         .rearrange("(i p) t -> p i t",
                                                    p=128),
                                         in_=xc[:, ii * 4 * CH:
                                                (ii + 1) * 4 * CH]
                                         .rearrange("p (i t) -> p i t",
                                                    t=CH)),
                                     waits=[(sv, t_ap)], inc=s_spl, incn=16)
                        tok_x1spill[c] = (s_spl, t_sp)
                        tok_prev_spill = (s_spl, t_sp)
                        tok_Cdone_pe = (sp, sp.n)

                if True:
                    if True:
                        silu_free = [(None, 0), (None, 0)]
                        t_hid = 0
                        for r in range(NH):
                            b = r % 2
                            w_l = [tok_w13_free[b], (sd_pre, t_pre_sync),
                                   (s_cst[0], t_cst0), (s_cst[1], t_cst1)]
                            if r < 2:
                                w_l += [tok_qT_lastread, tok_Bact, tok_Elast,
                                        tok_attn_done[0], tok_attn_done[1]]
                            t_wh = [[0, 0], [0, 0]]
                            for (wi, wsrc_, wdst) in ((0, w1s, w1b),
                                                      (1, w3s, w3b)):
                                for hf in range(2):
                                    t_wh[wi][hf] = R(
                                        "sync",
                                        lambda e, r=r, hf=hf, wsrc_=wsrc_,
                                        wdst=wdst: e.dma_start(
                                            out=wdst[hf][:],
                                            in_=wsrc_[r, :,
                                                      hf * (DM // 2):
                                                      (hf + 1) * (DM // 2)]),
                                        waits=w_l + [tok_w13h_free[wi][hf]],
                                        inc=s_13h[wi][hf], incn=16)
                            t_gg = 0
                            t_ll = 0
                            for i in range(NDM):
                                hf = 0 if i < 8 else 1
                                w_mm = [(s_13h[0][hf], t_wh[0][hf])]
                                if i == 0:
                                    w_mm += [pr[0 + b], (sv, t_ap)]
                                t_gg = R("tensor",
                                         lambda e, r=r, i=i, b=b, hf=hf:
                                         e.matmul(
                                             PS[0 + b][:, :],
                                             w1b[hf][:, (i % 8) * 128:
                                                     (i % 8 + 1) * 128],
                                             x1bf[:, i * CH:(i + 1) * CH],
                                             start=(i == 0),
                                             stop=(i == NDM - 1)),
                                         waits=w_mm, inc=sp)
                                if i == 7 or i == NDM - 1:
                                    tok_w13h_free[0][hf] = (sp, t_gg)
                            for i in range(NDM):
                                hf = 0 if i < 8 else 1
                                w_mm = [(s_13h[1][hf], t_wh[1][hf])]
                                if i == 0:
                                    w_mm += [pr[2 + b], (sv, t_ap)]
                                t_ll = R("tensor",
                                         lambda e, r=r, i=i, b=b, hf=hf:
                                         e.matmul(
                                             PS[2 + b][:, :],
                                             w3b[hf][:, (i % 8) * 128:
                                                     (i % 8 + 1) * 128],
                                             x1bf[:, i * CH:(i + 1) * CH],
                                             start=(i == 0),
                                             stop=(i == NDM - 1)),
                                         waits=w_mm, inc=sp)
                                if i == 7 or i == NDM - 1:
                                    tok_w13h_free[1][hf] = (sp, t_ll)
                            tok_w13_free[b] = (sp, t_ll)
                            w_si = [(sp, t_gg), silu_free[b]]
                            if r < 2:
                                w_si += [tok_outst, tok_outst2, tok_Eapply,
                                         tok_attn_done[0], tok_attn_done[1]]
                            t_si = R("scalar", lambda e, b=b: e.activation(
                                out=silub[b][:], in_=PS[0 + b][:, :],
                                func=AF.Silu),
                                waits=w_si, inc=sa)
                            pr[0 + b] = (sa, t_si)
                            t_hid = R("vector", lambda e, r=r, b=b:
                                      e.tensor_tensor(
                                          out=hidden[:, r * CH:(r + 1) * CH],
                                          in0=silub[b][:],
                                          in1=PS[2 + b][:, :], op=ALU.mult),
                                      waits=[(sa, t_si), (sp, t_ll)], inc=sv)
                            pr[2 + b] = (sv, t_hid)
                            silu_free[b] = (sv, t_hid)
                        tok_hidden_all = t_hid

                    if True:
                        x1rd_free = [[tok_outst, tok_outst2],
                                     [tok_outst, tok_outst2]]
                        y2t_free = [(None, 0), (None, 0)]
                        y2bf_free = [(None, 0), (None, 0)]
                        y2sq_free = [(None, 0), (None, 0)]
                        t_sum = t_ssq = 0
                        t_y2 = {}
                        for jd in range(ND):
                            b = jd % 2
                            t_wq4 = [0] * 8

                            def issue_w2(q2, jd=jd):
                                bq2 = q2 % 2
                                w_lq = [tok_w2buf_free[bq2],
                                        (sd_pre, t_pre_sync)]
                                if jd < 2 and q2 < 2:
                                    w_lq += [(sv, tok_hidden_all)]
                                t_wq4[q2] = R(
                                    "sync", lambda e, jd=jd, q2=q2, bq2=bq2:
                                    e.dma_start(
                                        out=w2b[bq2][:],
                                        in_=w2p[jd, :, q2 * (HD // 8):
                                                (q2 + 1) * (HD // 8)]),
                                    waits=w_lq, inc=s_w2[bq2], incn=16)

                            issue_w2(0)
                            issue_w2(1)
                            w_xr = [tok_x1spill[c]] + \
                                (x1rd_free[b]
                                 if isinstance(x1rd_free[b], list)
                                 else [x1rd_free[b]])
                            if jd < 2:
                                w_xr += [(sv, tok_hidden_all)]
                            t_xr = R("sync", lambda e, jd=jd, c=c, b=b:
                                     e.dma_start(
                                         out=x1rd[b][:],
                                         in_=x1scr[jd * 128:(jd + 1) * 128,
                                                   c * CH:(c + 1) * CH]),
                                     waits=w_xr, inc=s_x1[b], incn=16)
                            t_g = 0
                            for r in range(NH):
                                q2 = r // 8
                                bq2 = q2 % 2
                                w_mm = [(s_w2[bq2], t_wq4[q2]),
                                        (sv, tok_hidden_all)]
                                if r == 0:
                                    w_mm += [pr[0 + b]]
                                rr = r % 8
                                t_g = R("tensor",
                                        lambda e, jd=jd, r=r, rr=rr, b=b,
                                        bq2=bq2: e.matmul(
                                            PS[0 + b][:, :],
                                            w2b[bq2][:, rr * 128:
                                                     (rr + 1) * 128],
                                            hidden[:, r * CH:(r + 1) * CH],
                                            start=(r == 0),
                                            stop=(r == NH - 1)),
                                        waits=w_mm, inc=sp)
                                if rr == 7:
                                    tok_w2buf_free[bq2] = (sp, t_g)
                                    if q2 + 2 < 8:
                                        issue_w2(q2 + 2)
                            w_yt = [(sp, t_g), (s_x1[b], t_xr),
                                    y2t_free[b]]
                            t_yt = R("vector", lambda e, jd=jd, b=b:
                                     e.tensor_tensor(
                                         out=y2t[b][:],
                                         in0=PS[0 + b][:, :], in1=x1rd[b][:],
                                         op=ALU.add),
                                     waits=w_yt, inc=sv)
                            pr[0 + b] = (sv, t_yt)
                            x1rd_free[b] = [(sv, t_yt)]
                            t_y2[jd] = t_yt
                            t_y2w = R("scalar", lambda e, jd=jd, c=c, b=b:
                                      e.dma_start(
                                          out=x1scr[jd * 128:(jd + 1) * 128,
                                                    c * CH:(c + 1) * CH],
                                          in_=y2t[b][:]),
                                      waits=[(sv, t_yt)], inc=s_y2w[b],
                                      incn=16)
                            y2t_free[b] = (s_y2w[b], t_y2w)
                            t_cb = R("vector", lambda e, jd=jd, b=b:
                                     e.tensor_copy(
                                         out=y2bf[b][:],
                                         in_=y2t[b][:]),
                                     waits=[(sv, t_yt), y2bf_free[b],
                                            y2sq_free[b]],
                                     inc=sv)
                            t_sum = R("tensor", lambda e, jd=jd, b=b:
                                      e.matmul(PS[2][0:1, :], ones_bf[:, :],
                                               y2bf[b][:], start=(jd == 0),
                                               stop=(jd == ND - 1)),
                                      waits=[(sv, t_cb)] +
                                      ([pr[2]] if jd == 0 else []), inc=sp)

                            t_sq2 = R("vector", lambda e, jd=jd, b=b:
                                      e.tensor_tensor(
                                          out=y2sq[b][:], in0=y2bf[b][:],
                                          in1=y2bf[b][:], op=ALU.mult),
                                      waits=[(sv, t_cb), y2sq_free[b],
                                             (sp, t_sum)],
                                      inc=sv)
                            t_ssq = R("tensor", lambda e, jd=jd, b=b:
                                      e.matmul(PS[3][0:1, :], ones_bf[:, :],
                                               y2sq[b][:], start=(jd == 0),
                                               stop=(jd == ND - 1)),
                                      waits=[(sv, t_sq2)] +
                                      ([pr[3]] if jd == 0 else []), inc=sp)
                            y2sq_free[b] = (sp, t_ssq)
                        tok_y2w_all = (s_y2w[0].n, s_y2w[1].n)

                        t_m = R("vector", lambda e: e.tensor_scalar(
                            out=stat_m[:, :], in0=PS[2][0:1, :],
                            scalar1=1.0 / DM, scalar2=None, op0=ALU.mult),
                            waits=[(sp, t_sum)], inc=sv)
                        pr[2] = (sv, t_m)
                        t_v = R("vector", lambda e: e.tensor_scalar(
                            out=stat_a[:, :], in0=PS[3][0:1, :],
                            scalar1=1.0 / DM, scalar2=None, op0=ALU.mult),
                            waits=[(sp, t_ssq)], inc=sv)
                        pr[3] = (sv, t_v)
                        t_t1 = R("vector", lambda e: e.tensor_tensor(
                            out=rcbuf[0][0:1, :], in0=stat_m[:, :],
                            in1=stat_m[:, :], op=ALU.mult),
                            waits=[(sv, t_m), tmp_free[0]], inc=sv)
                        t_t2 = R("vector", lambda e: e.tensor_tensor(
                            out=stat_a[:, :], in0=stat_a[:, :],
                            in1=rcbuf[0][0:1, :], op=ALU.subtract),
                            waits=[(sv, t_t1)], inc=sv)
                        t_t3 = R("scalar", lambda e: e.activation(
                            out=stat_a[:, :], in_=stat_a[:, :], func=AF.Ln,
                            bias=epsl[0:1, :]), waits=[(sv, t_t2)], inc=sa)
                        t_t4 = R("scalar", lambda e: e.activation(
                            out=stat_a[:, :], in_=stat_a[:, :], func=AF.Exp,
                            scale=-0.5), waits=[(sa, t_t3)], inc=sa)
                        t_b1 = R("tensor", lambda e: e.matmul(
                            PS[4][:, :], ones_f[:, :], stat_m[0:1, :],
                            start=True, stop=True),
                            waits=[(sv, t_m), pr[4]], inc=sp)
                        t_b2 = R("tensor", lambda e: e.matmul(
                            PS[5][:, :], ones_f[:, :], stat_a[0:1, :],
                            start=True, stop=True),
                            waits=[(sa, t_t4), pr[5]], inc=sp)
                        t_ap2 = 0
                        tmp2_free = [(None, 0), (None, 0)]
                        outb_free = [(None, 0), (None, 0)]
                        y2rd_free = [[(sv, t_y2[ND - 2])],
                                     [(sv, t_y2[ND - 1])]]
                        t_st = 0
                        for jd in range(ND):
                            b = jd % 2
                            t_rr = R("sync", lambda e, jd=jd, c=c, b=b:
                                     e.dma_start(
                                         out=x1rd[b][:],
                                         in_=x1scr[jd * 128:(jd + 1) * 128,
                                                   c * CH:(c + 1) * CH]),
                                     waits=[(s_y2w[0], tok_y2w_all[0]),
                                            (s_y2w[1], tok_y2w_all[1])] +
                                     y2rd_free[b], inc=s_y2r[b], incn=16)
                            t_a1 = R("vector", lambda e, jd=jd, b=b:
                                     e.tensor_tensor(
                                         out=rcbuf[b][:],
                                         in0=x1rd[b][:],
                                         in1=PS[4][:, :], op=ALU.subtract),
                                     waits=[(sp, t_b1), (s_y2r[b], t_rr),
                                            (sv, t_t2), tmp2_free[b]],
                                     inc=sv)
                            t_a2 = R("vector", lambda e, jd=jd, b=b:
                                     e.tensor_tensor(
                                         out=outb[b][:], in0=rcbuf[b][:],
                                         in1=PS[5][:, :], op=ALU.mult),
                                     waits=[(sv, t_a1), (sp, t_b2),
                                            outb_free[b], y2t_free[b]],
                                     inc=sv)
                            tmp2_free[b] = (sv, t_a2)
                            y2rd_free[b] = [(sv, t_a1)]
                            t_ap2 = t_a2
                            t_st = R("scalar", lambda e, jd=jd, c=c, b=b:
                                     e.dma_start(
                                         out=outT[jd * 128:(jd + 1) * 128,
                                                  c * CH:(c + 1) * CH],
                                         in_=outb[b][:]),
                                     waits=[(sv, t_a2)], inc=s_out[b],
                                     incn=16)
                            outb_free[b] = (s_out[b], t_st)
                            y2t_free[b] = (s_out[b], t_st)
                        pr[4] = (sv, t_ap2)
                        pr[5] = (sv, t_ap2)
                        tok_Elast = (sp, sp.n)
                        tok_Eapply = (sv, sv.n)
                        tok_outst = (s_out[0], s_out[0].n)
                        tok_outst2 = (s_out[1], s_out[1].n)

        def final_waits(e):
            for s in ([s_kvst, s_spl, sd_x, sd_pre, sg_res, sa, sv, sq,
                       s_wff] + s_w2st + s_y2w + sp_ring + s_y2r
                      + s_out + sg_pre + s_qo + s_13h[0] + s_13h[1] + s_w2
                      + s_kv + s_x1 + s_cst):
                if s.n > 0:
                    e.wait_ge(s.h, s.n)
            return e.nop()
        R("sync", final_waits)

        block = stk.enter_context(nc.Block())
        rec.emit(block)

    return nc


_NC_CACHE = {}


def _get_program():
    if "nc" not in _NC_CACHE:
        _NC_CACHE["nc"] = build_program()
    return _NC_CACHE["nc"]


def _prep_shared(inputs):
    f32 = np.float32
    qp = np.asarray(inputs["L0_attn_q_project"], f32)
    kp = np.asarray(inputs["L0_attn_k_project"], f32)
    vp = np.asarray(inputs["L0_attn_v_project"], f32)
    op = np.asarray(inputs["L0_attn_o_project"], f32)
    W1 = np.asarray(inputs["L0_ffn_W1"], f32)
    W2 = np.asarray(inputs["L0_ffn_W2"], f32)
    W3 = np.asarray(inputs["L0_ffn_W3"], f32)

    def lhsT_layout(w, nt):
        o, i = w.shape
        return np.ascontiguousarray(
            w.reshape(nt, 128, i // 128, 128).transpose(0, 3, 2, 1)
            .reshape(nt, 128, i))

    d = {
        "wq": lhsT_layout(qp, NQH),
        "wkh": lhsT_layout(kp, NKVH),
        "wvh": np.ascontiguousarray(
            vp.reshape(KV, NDM, 128).transpose(2, 1, 0)),
        "wo": lhsT_layout(op, NDM),
        "w1": lhsT_layout(W1, NH),
        "w3": lhsT_layout(W3, NH),
        "w2sl": np.ascontiguousarray(
            W2.reshape(ND, 128, NH, 128).transpose(0, 3, 2, 1)
            .reshape(ND, 128, NH * 128)),
        "bqh": np.ascontiguousarray(
            np.asarray(inputs["L0_attn_q_bias"], f32).reshape(NQH, 128).T),
        "bkh": np.ascontiguousarray(
            np.asarray(inputs["L0_attn_k_bias"], f32).reshape(NKVH, 128).T),
        "bvr": np.ascontiguousarray(
            np.asarray(inputs["L0_attn_v_bias"], f32).reshape(1, KV)),
        "lnwh": np.ascontiguousarray(
            np.asarray(inputs["L0_attn_ln_weight"], f32).reshape(NDM, 128).T),
    }
    return d


def kernel(**inputs):
    x = np.asarray(inputs["L0_attn_x"], np.float32)
    nb = x.shape[0]
    shared = _prep_shared(inputs)
    nc = _get_program()
    in_maps = []
    for b in range(nb):
        m = dict(shared)
        m["xT"] = np.ascontiguousarray(x[b].T)
        in_maps.append(m)
    res = run_bass_kernel_spmd(nc, in_maps, core_ids=list(range(nb)))
    out = np.stack([np.asarray(r["outT"]).T for r in res.results], axis=0)
    return out


# version 14
# speedup vs baseline: 1.0411x; 1.0118x over previous
import math
from contextlib import ExitStack

import numpy as np
import ml_dtypes
import concourse.bass as bass
import concourse.mybir as mybir
from concourse.bass_utils import run_bass_kernel_spmd
from concourse.alu_op_type import AluOpType as ALU

F32 = mybir.dt.float32
BF16 = mybir.dt.bfloat16
I32 = mybir.dt.int32
AF = mybir.ActivationFunctionType
AXX = mybir.AxisListType.X

S = 4096
DM = 2048
HD = 8192
CH = 512
NCH = S // CH
NDM = DM // 128
NH = HD // 128
ND = DM // 128
NQH = 16
NKVH = 4
DH = 128
KV = 512
PI = float(np.pi)
INVSQ = 1.0 / math.sqrt(DH)
EPS_RMS = 1e-6
EPS_LN = 1e-5
FWHT_SCALE = 1.0 / math.sqrt(HD)

ENGS = ["sync", "gpsimd", "vector", "scalar", "tensor"]


class Sem:
    def __init__(self, handle, name):
        self.h = handle
        self.name = name
        self.n = 0


class Rec:

    def __init__(self):
        self.ops = {e: [] for e in ENGS}

    def o(self, eng, fn, waits=(), inc=None, incn=1):
        w = tuple((s, v) for (s, v) in waits if s is not None and v > 0)
        self.ops[eng].append((w, fn, inc, incn))
        if inc is not None:
            inc.n += incn
            return inc.n
        return None

    def emit(self, block):
        decs = {"sync": block.sync, "gpsimd": block.gpsimd,
                "vector": block.vector, "scalar": block.scalar,
                "tensor": block.tensor}
        for eng in ENGS:
            oplist = self.ops[eng]
            if not oplist:
                continue

            @decs[eng]
            def _(e, oplist=oplist):
                seen = {}
                for (waits, fn, inc, incn) in oplist:
                    for (s, v) in waits:
                        if seen.get(s.name, 0) < v:
                            e.wait_ge(s.h, v)
                            seen[s.name] = v
                    inst = fn(e)
                    if inc is not None:
                        inst.then_inc(inc.h, incn)


def build_program():
    nc = bass.Bass("TRN2", target_bir_lowering=False)

    xT = nc.dram_tensor("xT", [DM, S], F32, kind="ExternalInput").ap()
    wq = nc.dram_tensor("wq", [NQH, 128, DM], F32, kind="ExternalInput").ap()
    wkh = nc.dram_tensor("wkh", [NKVH, 128, DM], F32, kind="ExternalInput").ap()
    wvh = nc.dram_tensor("wvh", [128, NDM, KV], F32, kind="ExternalInput").ap()
    wo = nc.dram_tensor("wo", [NDM, 128, DM], F32, kind="ExternalInput").ap()
    w1 = nc.dram_tensor("w1", [NH, 128, DM], F32, kind="ExternalInput").ap()
    w3 = nc.dram_tensor("w3", [NH, 128, DM], F32, kind="ExternalInput").ap()
    w2sl = nc.dram_tensor("w2sl", [ND, 128, NH * 128], F32, kind="ExternalInput").ap()
    bqh = nc.dram_tensor("bqh", [128, NQH], F32, kind="ExternalInput").ap()
    bkh = nc.dram_tensor("bkh", [128, NKVH], F32, kind="ExternalInput").ap()
    bvr = nc.dram_tensor("bvr", [1, KV], F32, kind="ExternalInput").ap()
    lnwh = nc.dram_tensor("lnwh", [128, NDM], F32, kind="ExternalInput").ap()

    wqs = nc.dram_tensor("wqs", [NQH, 128, DM], BF16, kind="Internal").ap()
    wos = nc.dram_tensor("wos", [NDM, 128, DM], BF16, kind="Internal").ap()
    w1s = nc.dram_tensor("w1s", [NH, 128, DM], BF16, kind="Internal").ap()
    w3s = nc.dram_tensor("w3s", [NH, 128, DM], BF16, kind="Internal").ap()
    w2p = nc.dram_tensor("w2p", [ND, 128, HD], BF16, kind="Internal").ap()
    kscr = nc.dram_tensor("kscr", [NKVH, 128, S], BF16, kind="Internal").ap()
    vscr = nc.dram_tensor("vscr", [NKVH, S // 128, 128, 128], BF16,
                          kind="Internal").ap()
    x1scr = nc.dram_tensor("x1scr", [DM, S], F32, kind="Internal").ap()
    outT = nc.dram_tensor("outT", [DM, S], F32, kind="ExternalOutput").ap()

    rec = Rec()
    R = rec.o
    stk = ExitStack()

    with stk:
        sems = {}

        def sem(name):
            s = Sem(stk.enter_context(nc.semaphore(name)), name)
            sems[name] = s
            return s

        sp_ring = [sem(f"sp{i}") for i in range(4)]
        sp = sp_ring[0]
        sa = sem("sa")
        sv = sem("sv")
        sq = sem("sq")
        sd_x = sem("sd_x")
        s_kvst = sem("s_kvst")
        s_spl = sem("s_spl")
        s_out = [sem("s_out0"), sem("s_out1")]
        sd_pre = sem("sd_pre")
        sg_res = sem("sg_res")
        sg_pre = [sem("sg_pre0"), sem("sg_pre1")]
        s_wff = sem("s_wff")
        s_w2st = [sem("s_w2st0"), sem("s_w2st1")]
        s_qo = [sem("s_qo0"), sem("s_qo1")]
        s_13h = [[sem("s_13a0"), sem("s_13a1")],
                 [sem("s_13b0"), sem("s_13b1")]]
        s_w2 = [sem("s_w2a"), sem("s_w2b")]
        s_kv = [sem("s_kv0"), sem("s_kv1")]
        s_x1 = [sem("s_x1a"), sem("s_x1b")]
        s_y2w = [sem("s_y2w0"), sem("s_y2w1")]
        s_y2r = [sem("s_y2r0"), sem("s_y2r1")]
        s_cst = [sem("s_cst0"), sem("s_cst1")]

        def sb(name, shape, dt):
            return stk.enter_context(nc.sbuf_tensor(name, shape, dt))

        cos_t = sb("cos_t", [128, S], F32)
        sin_t = sb("sin_t", [128, S], F32)
        masks = sb("masks", [128, 4 * CH], BF16)
        wk_sb = sb("wk_sb", [128, NKVH * DM], BF16)
        wv_sb = sb("wv_sb", [128, NDM * KV], BF16)
        bq_sb = sb("bq_sb", [128, NQH], F32)
        bk_sb = sb("bk_sb", [128, NKVH], F32)
        bv_bc = sb("bv_bc", [128, KV], BF16)
        lnw_sb = sb("lnw_sb", [128, NDM], F32)
        ones_bf = sb("ones_bf", [128, 1], BF16)
        epsr = sb("epsr", [128, 1], F32)
        epsl = sb("epsl", [128, 1], F32)
        ones_f = sb("ones_f", [1, 128], F32)
        h128 = sb("h128", [128, 128], BF16)
        h128n = sb("h128n", [128, 128], BF16)
        kmax2 = sb("kmax2", [1, NKVH], F32)
        irow = sb("irow", [1, 64], F32)
        bias_h = sb("bias_h", [128, 2], F32)
        stat_m = sb("stat_m", [1, CH], F32)
        stat_a = sb("stat_a", [1, CH], F32)
        stat_t = sb("stat_t", [1, 8], F32)

        PS = [stk.enter_context(nc.psum_tensor(f"ps{i}", [128, 512], F32))
              for i in range(8)]
        pr = [(None, 0)] * 8

        with ExitStack() as pre1:
            def p1(name, shape, dt):
                return pre1.enter_context(nc.sbuf_tensor(name, shape, dt))

            i_i32 = p1("i_i32", [128, 1], I32)
            j_i32 = p1("j_i32", [128, 128], I32)
            h_i32 = p1("h_i32", [128, 128], I32)
            s_i32 = p1("s_i32", [128, 128], I32)
            fgrid = p1("fgrid", [128, 4 * CH], F32)
            trow = p1("trow", [1, S], F32)
            ang_t = p1("ang_t", [64, S], F32)
            red_t = p1("red_t", [64, S], F32)
            kf_t = p1("kf_t", [64, S], F32)
            ki_t = p1("ki_t", [64, S], I32)
            ang = ang_t[:]
            red = red_t[:]
            kf = kf_t[:]
            ki = ki_t[:]

            R("gpsimd", lambda e: e.iota(i_i32[:], [[0, 1]], base=0,
                                         channel_multiplier=1))
            R("gpsimd", lambda e: e.iota(j_i32[:], [[1, 128]], base=0,
                                         channel_multiplier=0))
            R("gpsimd", lambda e: e.iota(fgrid[:], [[1, 4 * CH]], base=0,
                                         channel_multiplier=-1,
                                         allow_small_or_imprecise_dtypes=True))
            R("gpsimd", lambda e: e.iota(trow[:], [[1, S]], base=0,
                                         channel_multiplier=0,
                                         allow_small_or_imprecise_dtypes=True))
            R("gpsimd", lambda e: e.iota(irow[:], [[1, 64]], base=0,
                                         channel_multiplier=0,
                                         allow_small_or_imprecise_dtypes=True))
            R("gpsimd", lambda e: e.memset(ones_bf[:], 1.0))
            R("gpsimd", lambda e: e.memset(epsr[:], EPS_RMS))
            R("gpsimd", lambda e: e.memset(epsl[:], EPS_LN))
            R("gpsimd", lambda e: e.memset(kmax2[:], 0.0))
            t_iota = R("gpsimd", lambda e: e.memset(ones_f[:], 1.0), inc=sq)

            for jr in range(4):
                R("vector", lambda e, jr=jr: e.tensor_scalar(
                    out=masks[:, jr * CH:(jr + 1) * CH],
                    in0=fgrid[:, jr * CH:(jr + 1) * CH],
                    scalar1=float(jr * (CH + 128)), scalar2=None,
                    op0=ALU.is_ge), waits=[(sq, t_iota)])

            t0 = R("vector", lambda e: e.tensor_scalar(
                out=h_i32[:], in0=j_i32[:], scalar1=i_i32[:], scalar2=None,
                op0=ALU.bitwise_and), waits=[(sq, t_iota)], inc=sv)
            for sh in (4, 2, 1):
                t0 = R("vector", lambda e, sh=sh: e.tensor_scalar(
                    out=s_i32[:], in0=h_i32[:], scalar1=sh, scalar2=None,
                    op0=ALU.logical_shift_right), waits=[(sv, t0)], inc=sv)
                t0 = R("vector", lambda e: e.tensor_tensor(
                    out=h_i32[:], in0=h_i32[:], in1=s_i32[:],
                    op=ALU.bitwise_xor), waits=[(sv, t0)], inc=sv)
            t0 = R("vector", lambda e: e.tensor_scalar(
                out=h_i32[:], in0=h_i32[:], scalar1=1, scalar2=None,
                op0=ALU.bitwise_and), waits=[(sv, t0)], inc=sv)
            t0 = R("vector", lambda e: e.tensor_copy(
                out=fgrid[:, 0:128], in_=h_i32[:]), waits=[(sv, t0)], inc=sv)
            t_h128 = R("vector", lambda e: e.tensor_scalar(
                out=h128[:], in0=fgrid[:, 0:128], scalar1=-2.0, scalar2=1.0,
                op0=ALU.mult, op1=ALU.add), waits=[(sv, t0)], inc=sv)
            t_h128n = R("vector", lambda e: e.tensor_scalar(
                out=h128n[:], in0=fgrid[:, 0:128], scalar1=2.0, scalar2=-1.0,
                op0=ALU.mult, op1=ALU.add), waits=[(sv, t0)], inc=sv)

            t1 = R("scalar", lambda e: e.activation(
                out=irow[:], in_=irow[:], func=AF.Exp,
                scale=-math.log(10000.0) / 64.0), waits=[(sq, t_iota)], inc=sa)
            ang_toks = []
            for sl in range(S // 512):
                b = sl % 2
                tmm = R("tensor", lambda e, sl=sl, b=b: e.matmul(
                    PS[b][0:64, :], irow[:, :],
                    trow[:, sl * 512:(sl + 1) * 512], start=True, stop=True),
                    waits=[(sa, t1), (sq, t_iota), pr[b]], inc=sp)
                tev = R("vector", lambda e, sl=sl, b=b: e.tensor_copy(
                    out=ang[:, sl * 512:(sl + 1) * 512], in_=PS[b][0:64, :]),
                    waits=[(sp, tmm)], inc=sv)
                pr[b] = (sv, tev)
            t_ang = sv.n
            C1 = float(np.float32(6.28125))
            C2 = float(np.float32(0.0019353069))
            C3 = float(2 * PI - np.float64(np.float32(6.28125))
                       - np.float64(np.float32(0.0019353069)))

            def vchain(fn, t0):
                return R("vector", fn, waits=[(sv, t0)], inc=sv)

            t0 = vchain(lambda e: e.tensor_scalar(
                out=kf, in0=ang, scalar1=float(1.0 / (2 * PI)),
                scalar2=None, op0=ALU.mult), t_ang)
            t0 = vchain(lambda e: e.tensor_copy(out=ki, in_=kf), t0)
            t0 = vchain(lambda e: e.tensor_copy(out=kf, in_=ki), t0)
            t0 = vchain(lambda e: e.scalar_tensor_tensor(
                out=red, in0=kf, scalar=-C1, in1=ang,
                op0=ALU.mult, op1=ALU.add), t0)
            t0 = vchain(lambda e: e.scalar_tensor_tensor(
                out=red, in0=kf, scalar=-C2, in1=red,
                op0=ALU.mult, op1=ALU.add), t0)
            t0 = vchain(lambda e: e.scalar_tensor_tensor(
                out=red, in0=kf, scalar=-C3, in1=red,
                op0=ALU.mult, op1=ALU.add), t0)
            t0 = vchain(lambda e: e.tensor_scalar(
                out=kf, in0=red, scalar1=PI, scalar2=None,
                op0=ALU.is_gt), t0)
            t0 = vchain(lambda e: e.scalar_tensor_tensor(
                out=red, in0=kf, scalar=-2 * PI, in1=red,
                op0=ALU.mult, op1=ALU.add), t0)
            t0 = vchain(lambda e: e.tensor_scalar(
                out=kf, in0=red, scalar1=-PI, scalar2=None,
                op0=ALU.is_lt), t0)
            t0 = vchain(lambda e: e.scalar_tensor_tensor(
                out=red, in0=kf, scalar=2 * PI, in1=red,
                op0=ALU.mult, op1=ALU.add), t0)
            t_sin = R("scalar", lambda e: e.activation(
                out=sin_t[0:64, :], in_=red, func=AF.Sin),
                waits=[(sv, t0), (sa, t1)], inc=sa)
            t_sin2 = R("vector", lambda e: e.tensor_copy(
                out=sin_t[64:128, :], in_=sin_t[0:64, :]),
                waits=[(sa, t_sin)], inc=sv)
            t0 = vchain(lambda e: e.tensor_scalar(
                out=ang, in0=red, scalar1=PI / 2, scalar2=None,
                op0=ALU.add), t0)
            t0 = vchain(lambda e: e.tensor_scalar(
                out=kf, in0=ang, scalar1=PI, scalar2=None,
                op0=ALU.is_gt), t0)
            t0 = vchain(lambda e: e.scalar_tensor_tensor(
                out=ang, in0=kf, scalar=-2 * PI, in1=ang,
                op0=ALU.mult, op1=ALU.add), t0)
            t_cos = R("scalar", lambda e: e.activation(
                out=cos_t[0:64, :], in_=ang, func=AF.Sin),
                waits=[(sv, t0), (sa, t_sin)], inc=sa)
            t_cos2 = R("vector", lambda e: e.tensor_copy(
                out=cos_t[64:128, :], in_=cos_t[0:64, :]),
                waits=[(sa, t_cos)], inc=sv)
            t_rope_tab = max(t_sin2, t_cos2)

        t_p1 = [(sv, sv.n), (sa, sa.n), (sp, sp.n), (sq, sq.n)]

        with ExitStack() as pre1:
            def p1(name, shape, dt):
                return pre1.enter_context(nc.sbuf_tensor(name, shape, dt))

            for (dst, src) in [(bq_sb, bqh), (bk_sb, bkh), (lnw_sb, lnwh)]:
                R("gpsimd", lambda e, dst=dst, src=src: e.dma_start(
                    out=dst[:], in_=src[:, :]), inc=sg_res, incn=16)
            R("gpsimd", lambda e: e.dma_start(
                out=stat_m[:, 0:KV], in_=bvr[:, :]), inc=sg_res, incn=16)
            R("gpsimd", lambda e: e.dma_start(
                out=wk_sb[:].rearrange("p (g d) -> p g d", d=DM),
                in_=wkh.rearrange("g p d -> p g d")),
              inc=sg_res, incn=16)
            t_wkv = R("gpsimd", lambda e: e.dma_start(
                out=wv_sb[:], in_=wvh.rearrange("p i e -> p (i e)")),
                inc=sg_res, incn=16)
            tb = R("tensor", lambda e: e.matmul(
                PS[2][:, :], ones_f[:, :], stat_m[:, 0:KV],
                start=True, stop=True),
                waits=[(sg_res, t_wkv), (sq, t_iota), pr[2]], inc=sp)
            tb = R("vector", lambda e: e.tensor_copy(
                out=bv_bc[:], in_=PS[2][:, :]), waits=[(sp, tb)], inc=sv)
            pr[2] = (sv, tb)

            castbuf = [p1(f"castb{i}", [128, DM], BF16) for i in range(2)]
            cast_free = [list(t_p1), list(t_p1)]
            nblk = 0
            for (srcw, dstw, n) in [(wq, wqs, NQH), (wo, wos, NDM),
                                    (w1, w1s, NH), (w3, w3s, NH)]:
                for blk in range(n):
                    bsel = nblk % 2
                    tl = R("gpsimd", lambda e, srcw=srcw, blk=blk, bsel=bsel:
                           e.dma_start(out=castbuf[bsel][:],
                                       in_=srcw[blk, :, :]),
                           waits=(cast_free[bsel]
                                  if isinstance(cast_free[bsel], list)
                                  else [cast_free[bsel]]),
                           inc=sg_pre[bsel], incn=16)
                    ts_ = R("sync", lambda e, dstw=dstw, blk=blk, bsel=bsel:
                            e.dma_start(out=dstw[blk, :, :],
                                        in_=castbuf[bsel][:]),
                            waits=[(sg_pre[bsel], tl)], inc=s_cst[bsel],
                            incn=16)
                    cast_free[bsel] = [(s_cst[bsel], ts_)]
                    nblk += 1

            w2ff = p1("w2ff", [128, NH * 64], F32)
            w2bf = p1("w2bf", [128, NH * 128], BF16)
            w2out = p1("w2out", [128, NH * 128], BF16)
            Av = p1("Av", [128, NH * 128], F32)
            Bv = p1("Bv", [128, NH * 128], F32)
            w2ff_free = list(t_p1)
            t_st_prev = (None, 0)
            t_mm_last = 0
            tq_prev = t_p1[0][1]

            def bfly(eng, semx, src_, dst, bit, t0_tok):
                n_inner = bit * 128
                toks = []
                for (sign, outoff) in ((ALU.add, 0), (ALU.subtract, n_inner)):
                    toks.append(R(eng,
                                  lambda e, src_=src_, dst=dst,
                                  n_inner=n_inner, sign=sign,
                                  outoff=outoff: e.tensor_tensor(
                                      out=dst[:].rearrange(
                                          "p (q i) -> p q i",
                                          i=2 * n_inner)[
                                          :, :, outoff:outoff + n_inner],
                                      in0=src_[:].rearrange(
                                          "p (q i) -> p q i",
                                          i=2 * n_inner)[:, :, 0:n_inner],
                                      in1=src_[:].rearrange(
                                          "p (q i) -> p q i",
                                          i=2 * n_inner)[
                                          :, :, n_inner:2 * n_inner],
                                      op=sign),
                                  waits=[t0_tok], inc=semx))
                return toks[-1]

            bf_free = [list(t_p1), list(t_p1)]
            out_free = [(None, 0), (None, 0)]
            for s_ in range(ND):
                tcasts = []
                for hfs in range(2):
                    tl = R("scalar", lambda e, s_=s_, hfs=hfs: e.dma_start(
                        out=w2ff[:],
                        in_=w2sl[s_, :, hfs * (NH * 64):(hfs + 1) * (NH * 64)]),
                        waits=(w2ff_free if isinstance(w2ff_free, list)
                               else [w2ff_free]), inc=s_wff, incn=16)
                    for pc in range(2):
                        tc = R("scalar", lambda e, pc=pc, hfs=hfs: e.copy(
                            out=w2bf[:, (hfs * 2 + pc) * 2048:
                                     (hfs * 2 + pc + 1) * 2048],
                            in_=w2ff[:, pc * 2048:(pc + 1) * 2048]),
                            waits=[(s_wff, tl), (sp, t_mm_last)], inc=sa)
                        tcasts.append(tc)
                    w2ff_free = [(sa, tcasts[-1])]
                t_ev_h = [0, 0]
                for og in range(16):
                    half = og // 8
                    b_ = og % 2
                    hi = og >> 1
                    t_mm = 0
                    for m in range(8):
                        hi2 = hi ^ m
                        ig = (hi2 << 1) | (og & 1)
                        sign = bin(hi & hi2).count("1") & 1
                        lhs = h128n if sign else h128
                        w_mm = [(sa, tcasts[3]), (sv, t_h128n)]
                        if m == 0:
                            w_mm.append(pr[b_])
                        t_mm = R("tensor",
                                 lambda e, ig=ig, lhs=lhs, b_=b_, m=m:
                                 e.matmul(PS[b_][:, :], lhs[:, :],
                                          w2bf[:, ig * 512:(ig + 1) * 512],
                                          start=(m == 0), stop=(m == 7)),
                                 waits=w_mm, inc=sp)
                    t_ev = R("scalar", lambda e, og=og, b_=b_: e.copy(
                        out=Av[:, og * 512:(og + 1) * 512],
                        in_=PS[b_][:, :]),
                        waits=[(sp, t_mm)] +
                        (bf_free[half] if og % 8 == 0 else []),
                        inc=sa)
                    pr[b_] = (sa, t_ev)
                    t_ev_h[half] = t_ev
                t_mm_last = sp.n
                for half in range(2):
                    hof = half * (NH * 64)
                    hl = NH * 64
                    t_bb_ = (sa, t_ev_h[half])
                    srcb, dstb = Av, Bv
                    for bit in (1, 2, 4):
                        n_inner = bit * 128
                        for (sign, outoff) in ((ALU.add, 0),
                                               (ALU.subtract, n_inner)):
                            t_bb_ = (sv, R("vector",
                                     lambda e, srcb=srcb, dstb=dstb,
                                     n_inner=n_inner, sign=sign,
                                     outoff=outoff, hof=hof, hl=hl:
                                     e.tensor_tensor(
                                         out=dstb[:, hof:hof + hl].rearrange(
                                             "p (q i) -> p q i",
                                             i=2 * n_inner)[
                                             :, :, outoff:outoff + n_inner],
                                         in0=srcb[:, hof:hof + hl].rearrange(
                                             "p (q i) -> p q i",
                                             i=2 * n_inner)[:, :, 0:n_inner],
                                         in1=srcb[:, hof:hof + hl].rearrange(
                                             "p (q i) -> p q i",
                                             i=2 * n_inner)[
                                             :, :, n_inner:2 * n_inner],
                                         op=sign),
                                     waits=[t_bb_], inc=sv))
                        srcb, dstb = dstb, srcb
                    bf_free[half] = [t_bb_]
                    tsc = R("vector", lambda e, hof=hof, hl=hl:
                            e.tensor_scalar(
                                out=w2out[:, hof:hof + hl],
                                in0=Bv[:, hof:hof + hl],
                                scalar1=FWHT_SCALE, scalar2=None,
                                op0=ALU.mult),
                            waits=[t_bb_, out_free[half]], inc=sv)
                    tst = R("scalar", lambda e, s_=s_, hof=hof, hl=hl:
                            e.dma_start(
                                out=w2p[s_, :, hof:hof + hl],
                                in_=w2out[:, hof:hof + hl]),
                            waits=[(sv, tsc)], inc=s_w2st[half], incn=16)
                    out_free[half] = (s_w2st[half], s_w2st[half].n)
        t_pre_all = [(sd_pre, sd_pre.n), (s_wff, s_wff.n),
                     (s_w2st[0], s_w2st[0].n), (s_w2st[1], s_w2st[1].n),
                     (sa, sa.n), (sv, sv.n), (sp, sp.n), (sq, sq.n)]
        t_pre_sync = sd_pre.n
        t_cst0 = s_cst[0].n
        t_cst1 = s_cst[1].n
        t_pre_g0 = sg_pre[0].n
        t_pre_g1 = sg_pre[1].n
        tok_scratch_ready = [(sd_pre, t_pre_sync), (s_cst[0], t_cst0),
                             (s_cst[1], t_cst1)]

        tok_Elast = (None, 0)
        tok_Eapply = (None, 0)
        tok_outst = (None, 0)
        tok_outst2 = (None, 0)
        tok_prev_spill = (None, 0)
        tok_kvstore = (None, 0)
        tok_x1spill = [(None, 0)] * NCH
        tok_kbuf_free = [(None, 0), (None, 0)]
        tok_vbuf_free = [(None, 0), (None, 0)]
        tok_wqbuf_free = [(None, 0), (None, 0)]
        tok_w13_free = [(None, 0), (None, 0)]
        tok_w13h_free = [[(None, 0), (None, 0)], [(None, 0), (None, 0)]]
        tok_w2buf_free = [(None, 0), (None, 0)]
        tok_qT_lastread = (None, 0)
        tok_aout_lastread = (None, 0)
        tok_hT_lastread = (None, 0)


        with ExitStack() as ck:
            def cb(name, shape, dt):
                return ck.enter_context(nc.sbuf_tensor(name, shape, dt))
            xc = cb("xc", [128, NDM * CH], F32)
            x1bf = cb("x1bf", [128, NDM * CH], BF16)
            rcbuf = [cb(f"rc{i}", [128, 512], F32) for i in range(2)]
            sqb1 = cb("sqb1", [128, CH], BF16)
            rms_bc = cb("rms_bc", [128, 512], F32)
            with ExitStack() as g1:
                def g1b(name, shape, dt):
                    return g1.enter_context(nc.sbuf_tensor(name, shape, dt))
                qT = g1b("qT", [128, NQH * CH], BF16)
                aoutT = g1b("aoutT", [128, NQH * CH], BF16)
                with ExitStack() as gA:
                    def gAb(name, shape, dt):
                        return gA.enter_context(
                            nc.sbuf_tensor(name, shape, dt))
                    hT = gAb("hT", [128, NDM * CH], BF16)
                    sqb = [gAb(f"sq{i}", [128, CH], BF16) for i in range(2)]
                    ktmp = [gAb(f"ktmp{i}", [128, CH], F32) for i in range(2)]
                    rot_b = [gAb(f"rot{i}", [128, CH], F32) for i in range(2)]
                    tc_b = [gAb(f"tc{i}", [128, CH], F32) for i in range(2)]
                    wqbuf = [gAb(f"wqb{i}", [128, DM], BF16) for i in range(2)]
                    knew = gAb("knew", [128, NKVH * CH], BF16)
                    vnew = gAb("vnew", [128, NKVH * CH], BF16)
                with ExitStack() as gB:
                    def gBb(name, shape, dt):
                        return gB.enter_context(
                            nc.sbuf_tensor(name, shape, dt))
                    kbuf = [gBb(f"kb{i}", [128, S], BF16) for i in range(2)]
                    vbuf = [gBb(f"vb{i}", [128, S], BF16) for i in range(2)]
                    PT = [gBb(f"pt{i}", [128, CH], BF16) for i in range(4)]
                    qsq = gBb("qsq", [128, CH], BF16)
                with ExitStack() as gC:
                    def gCb(name, shape, dt):
                        return gC.enter_context(
                            nc.sbuf_tensor(name, shape, dt))
                    wobuf = [gCb(f"wob{i}", [128, DM], BF16)
                             for i in range(2)]
                    ybf = [gCb(f"ybf{i}", [128, CH], BF16) for i in range(2)]
                    ysq = [gCb(f"ysq{i}", [128, CH], BF16) for i in range(2)]
            with ExitStack() as g2:
                def g2b(name, shape, dt):
                    return g2.enter_context(nc.sbuf_tensor(name, shape, dt))
                hidden = g2b("hidden", [128, NH * CH], BF16)
                with ExitStack() as gD:
                    def gDb(name, shape, dt):
                        return gD.enter_context(
                            nc.sbuf_tensor(name, shape, dt))
                    w1b = [gDb(f"w1b{i}", [128, DM // 2], BF16)
                           for i in range(2)]
                    w3b = [gDb(f"w3b{i}", [128, DM // 2], BF16)
                           for i in range(2)]
                    silub = [gDb(f"silu{i}", [128, CH], F32)
                             for i in range(2)]
                with ExitStack() as gE:
                    def gEb(name, shape, dt):
                        return gE.enter_context(
                            nc.sbuf_tensor(name, shape, dt))
                    w2b = [gEb(f"w2b{i}", [128, HD // 8], BF16)
                           for i in range(2)]
                    y2t = [gEb(f"y2t{i}", [128, CH], F32) for i in range(2)]
                    x1rd = [gEb(f"x1rd{i}", [128, CH], F32)
                            for i in range(2)]
                    y2bf = [gEb(f"y2bf{i}", [128, CH], BF16)
                            for i in range(2)]
        y2sq = y2bf
        outb = y2t


        aprep = {}

        def emit_Aprep(c, first_waits):
            t_x = 0
            for ii in range(4):
                t_x = R("sync", lambda e, ii=ii, c=c: e.dma_start(
                    out=xc[:, ii * 4 * CH:(ii + 1) * 4 * CH]
                    .rearrange("p (i t) -> p i t", t=CH),
                    in_=xT[ii * 512:(ii + 1) * 512,
                           c * CH:(c + 1) * CH]
                    .rearrange("(i p) t -> p i t", p=128)),
                    waits=first_waits, inc=sd_x, incn=16)
            t_mm = 0
            t_sq = 0
            for i in range(NDM):
                w_sq = [(sd_x, t_x)]
                if i > 0:
                    w_sq.append((sp, t_mm))
                t_sq = R("scalar", lambda e, i=i: e.activation(
                    out=sqb1[:], in_=xc[:, i * CH:(i + 1) * CH],
                    func=AF.Square), waits=w_sq, inc=sa)
                t_mm = R("tensor", lambda e, i=i: e.matmul(
                    PS[6][0:1, :], ones_bf[:, :], sqb1[:],
                    start=(i == 0), stop=(i == NDM - 1)),
                    waits=[(sa, t_sq)] + ([pr[6]] if i == 0 else []),
                    inc=sp)
            t_ln = R("scalar", lambda e: e.activation(
                out=rms_bc[0:1, :], in_=PS[6][0:1, :], func=AF.Ln,
                bias=epsr[0:1, :], scale=1.0 / DM),
                waits=[(sp, t_mm)], inc=sa)
            pr[6] = (sa, t_ln)
            t_ex = R("scalar", lambda e: e.activation(
                out=rms_bc[0:1, :], in_=rms_bc[0:1, :], func=AF.Exp,
                scale=-0.5), waits=[(sa, t_ln)], inc=sa)
            t_bc = R("tensor", lambda e: e.matmul(
                PS[7][:, :], ones_f[:, :], rms_bc[0:1, :],
                start=True, stop=True),
                waits=[(sa, t_ex), pr[7]], inc=sp)
            t_rms = R("scalar", lambda e: e.copy(
                out=rms_bc[:], in_=PS[7][:, :]),
                waits=[(sp, t_bc)], inc=sa)
            pr[7] = (sa, t_rms)
            aprep[c] = {"rms": t_rms, "xload": t_x}

        emit_Aprep(0, [(sg_pre[0], t_pre_g0), (sg_pre[1], t_pre_g1),
                       (sg_res, sg_res.n), (s_cst[0], t_cst0),
                       (s_cst[1], t_cst1)] + t_pre_all)

        for c in range(NCH):
            sp = sp_ring[c % 4]
            NKT = 4 * (c + 1)
            if True:
                if True:
                    if True:
                        t_rms = aprep[c]["rms"]
                        tok_xload = aprep[c]["xload"]
                        t_hT = 0
                        hT_toks = [0] * NDM
                        for i in range(NDM):
                            t_hT = R("vector", lambda e, i=i:
                                     e.scalar_tensor_tensor(
                                         out=hT[:, i * CH:(i + 1) * CH],
                                         in0=xc[:, i * CH:(i + 1) * CH],
                                         scalar=lnw_sb[:, i:i + 1],
                                         in1=rms_bc[:],
                                         op0=ALU.mult, op1=ALU.mult),
                                     waits=[(sa, t_rms), tok_hT_lastread],
                                     inc=sv)
                            hT_toks[i] = t_hT
                        tok_hTready = t_hT

                        tok_knorm = []
                        for g in range(NKVH):
                            b = g % 2
                            t_g = 0
                            for i in range(NDM):
                                t_g = R("tensor", lambda e, g=g, i=i, b=b:
                                        e.matmul(
                                            PS[b][:, :],
                                            wk_sb[:, (g * NDM + i) * 128:
                                                  (g * NDM + i + 1) * 128],
                                            hT[:, i * CH:(i + 1) * CH],
                                            start=(i == 0),
                                            stop=(i == NDM - 1)),
                                        waits=[(sv, hT_toks[i])] +
                                        ([pr[b]] if i == 0 else []), inc=sp)
                            w_kb = [(sp, t_g)]
                            if g < 2:
                                w_kb += [tok_outst, tok_outst2]
                            t_kb = R("vector", lambda e, g=g, b=b:
                                     e.tensor_scalar(
                                         out=ktmp[b][:], in0=PS[b][:, :],
                                         scalar1=bk_sb[:, g:g + 1],
                                         scalar2=None, op0=ALU.add),
                                     waits=w_kb, inc=sv)
                            pr[b] = (sv, t_kb)
                            cosc = cos_t[:, c * CH:(c + 1) * CH]
                            sinc = sin_t[:, c * CH:(c + 1) * CH]
                            R("vector", lambda e, b=b: e.tensor_scalar(
                                out=rot_b[b][0:64, :],
                                in0=ktmp[b][64:128, :], scalar1=-1.0,
                                scalar2=None, op0=ALU.mult),
                              waits=[(sv, t_kb), (sv, t_rope_tab)])
                            t_r2 = R("vector", lambda e, b=b: e.tensor_copy(
                                out=rot_b[b][64:128, :],
                                in_=ktmp[b][0:64, :]),
                                waits=[(sv, t_kb)], inc=sv)
                            t_tc = R("vector", lambda e, b=b, cosc=cosc:
                                     e.tensor_tensor(
                                         out=tc_b[b][:], in0=ktmp[b][:],
                                         in1=cosc, op=ALU.mult),
                                     waits=[(sv, t_kb)], inc=sv)
                            t_ts = R("vector", lambda e, b=b, sinc=sinc:
                                     e.tensor_tensor(
                                         out=rot_b[b][:], in0=rot_b[b][:],
                                         in1=sinc, op=ALU.mult),
                                     waits=[(sv, t_r2)], inc=sv)
                            t_hi = R("vector", lambda e, g=g, b=b:
                                     e.tensor_tensor(
                                         out=knew[:, g * CH:(g + 1) * CH],
                                         in0=tc_b[b][:], in1=rot_b[b][:],
                                         op=ALU.add),
                                     waits=[(sv, t_ts), (sv, t_tc)], inc=sv)
                            t_lo = t_hi
                            t_ks = R("vector", lambda e, g=g: e.tensor_tensor(
                                out=sqb1[:],
                                in0=knew[:, g * CH:(g + 1) * CH],
                                in1=knew[:, g * CH:(g + 1) * CH],
                                op=ALU.mult), waits=[(sv, t_hi)], inc=sv)
                            t_kn = R("tensor", lambda e: e.matmul(
                                PS[4][0:1, :], ones_bf[:, :], sqb1[:],
                                start=True, stop=True),
                                waits=[(sv, t_ks), pr[4]], inc=sp)
                            t_km = R("vector", lambda e: e.reduce_max(
                                out=stat_t[:, 6:7], in_=PS[4][0:1, :],
                                axis=AXX), waits=[(sp, t_kn)], inc=sv)
                            pr[4] = (sv, t_km)
                            t_ku = R("vector", lambda e, g=g: e.tensor_tensor(
                                out=kmax2[:, g:g + 1], in0=kmax2[:, g:g + 1],
                                in1=stat_t[:, 6:7], op=ALU.max),
                                waits=[(sv, t_km)], inc=sv)
                            tok_knorm.append(t_ku)
                            t_kst = R("scalar", lambda e, g=g, c=c:
                                      e.dma_start(
                                          out=kscr[g, :,
                                                   c * CH:(c + 1) * CH],
                                          in_=knew[:, g * CH:(g + 1) * CH]),
                                      waits=[(sv, t_hi)],
                                      inc=s_kvst, incn=16)

                        for sub in range(4):
                            b = sub % 2
                            t_g = 0
                            for i in range(NDM):
                                t_g = R("tensor", lambda e, sub=sub, i=i, b=b:
                                        e.matmul(
                                            PS[b][:, :],
                                            hT[:, i * CH + sub * 128:
                                               i * CH + (sub + 1) * 128],
                                            wv_sb[:, i * KV:(i + 1) * KV],
                                            start=(i == 0),
                                            stop=(i == NDM - 1)),
                                        waits=[(sv, hT_toks[i])] +
                                        ([pr[b]] if i == 0 else []), inc=sp)
                            w_vb = [(sp, t_g)]
                            if sub < 2:
                                w_vb += [tok_outst, tok_outst2]
                            t_vb = R("vector", lambda e, sub=sub, b=b:
                                     e.tensor_tensor(
                                         out=vnew[:, sub * KV:(sub + 1) * KV],
                                         in0=PS[b][:, :], in1=bv_bc[:],
                                         op=ALU.add),
                                     waits=w_vb, inc=sv)
                            pr[b] = (sv, t_vb)
                            for g in range(NKVH):
                                R("scalar", lambda e, sub=sub, g=g, c=c:
                                  e.dma_start(
                                      out=vscr[g, c * 4 + sub, :, :],
                                      in_=vnew[:, sub * KV + g * 128:
                                               sub * KV + (g + 1) * 128]),
                                  waits=[(sv, t_vb)], inc=s_kvst, incn=16)

                        tok_ropeq = [0] * NQH
                        for h in range(NQH):
                            b = h % 2
                            w_wq = [tok_wqbuf_free[b],
                                    (sd_pre, t_pre_sync),
                                    (s_cst[0], t_cst0), (s_cst[1], t_cst1)]
                            if h < 2:
                                w_wq += [tok_Elast, tok_outst, tok_outst2]
                            t_wl = R("sync", lambda e, h=h, b=b: e.dma_start(
                                out=wqbuf[b][:], in_=wqs[h, :, :]),
                                waits=w_wq, inc=s_qo[b], incn=16)
                            t_g = 0
                            for i in range(NDM):
                                t_g = R("tensor", lambda e, h=h, i=i, b=b:
                                        e.matmul(
                                            PS[b][:, :],
                                            wqbuf[b][:, i * 128:(i + 1) * 128],
                                            hT[:, i * CH:(i + 1) * CH],
                                            start=(i == 0),
                                            stop=(i == NDM - 1)),
                                        waits=[(sv, hT_toks[i]),
                                               (s_qo[b], t_wl)] +
                                        ([pr[b]] if i == 0 else []), inc=sp)
                            tok_wqbuf_free[b] = (sp, t_g)
                            t_qb = R("vector", lambda e, h=h, b=b:
                                     e.tensor_scalar(
                                         out=ktmp[b][:], in0=PS[b][:, :],
                                         scalar1=bq_sb[:, h:h + 1],
                                         scalar2=None, op0=ALU.add),
                                     waits=[(sp, t_g)], inc=sv)
                            pr[b] = (sv, t_qb)
                            cosc = cos_t[:, c * CH:(c + 1) * CH]
                            sinc = sin_t[:, c * CH:(c + 1) * CH]
                            R("vector", lambda e, b=b: e.tensor_scalar(
                                out=rot_b[b][0:64, :],
                                in0=ktmp[b][64:128, :], scalar1=-1.0,
                                scalar2=None, op0=ALU.mult),
                              waits=[(sv, t_qb), (sv, t_rope_tab)])
                            t_r2 = R("vector", lambda e, b=b: e.tensor_copy(
                                out=rot_b[b][64:128, :],
                                in_=ktmp[b][0:64, :]),
                                waits=[(sv, t_qb)], inc=sv)
                            t_tc = R("vector", lambda e, b=b, cosc=cosc:
                                     e.tensor_tensor(
                                         out=tc_b[b][:], in0=ktmp[b][:],
                                         in1=cosc, op=ALU.mult),
                                     waits=[(sv, t_qb)], inc=sv)
                            t_ts = R("vector", lambda e, b=b, sinc=sinc:
                                     e.tensor_tensor(
                                         out=rot_b[b][:], in0=rot_b[b][:],
                                         in1=sinc, op=ALU.mult),
                                     waits=[(sv, t_r2)], inc=sv)
                            t_hi = R("vector", lambda e, h=h, b=b:
                                     e.tensor_tensor(
                                         out=qT[:, h * CH:(h + 1) * CH],
                                         in0=tc_b[b][:], in1=rot_b[b][:],
                                         op=ALU.add),
                                     waits=[(sv, t_ts), (sv, t_tc)], inc=sv)
                            tok_ropeq[h] = t_hi
                        tok_qkv_done = sp.n
                        tok_Av = sv.n
                        tok_kvstore = (s_kvst, s_kvst.n)

                    if True:
                        tok_kload = [0, 0]
                        tok_vload = [0, 0]

                        def issue_kv(g):
                            b = g % 2
                            tok_kload[b] = R(
                                "scalar", lambda e, g=g, b=b, NKT=NKT:
                                e.dma_start(
                                    out=kbuf[b][:, 0:NKT * 128],
                                    in_=kscr[g, :, 0:NKT * 128]),
                                waits=[tok_kvstore, tok_kbuf_free[b],
                                       (sp, tok_qkv_done), (sv, tok_Av),
                                       tok_outst, tok_outst2],
                                inc=s_kv[b], incn=16)
                            tv = 0
                            nj = NKT
                            j0 = 0
                            while j0 < nj:
                                jn = min(8, nj - j0)
                                tv = R("scalar",
                                       lambda e, g=g, b=b, j0=j0, jn=jn:
                                       e.dma_start(
                                           out=vbuf[b][:, j0 * 128:
                                                       (j0 + jn) * 128]
                                           .rearrange("t (j d) -> t j d",
                                                      d=128),
                                           in_=vscr[g, j0:j0 + jn, :, :]
                                           .rearrange("j t d -> t j d")),
                                       waits=[tok_kvstore, tok_vbuf_free[b],
                                              (sp, tok_qkv_done),
                                              (sv, tok_Av)],
                                       inc=s_kv[b], incn=16)
                                j0 += jn
                            tok_vload[b] = tv

                        issue_kv(0)
                        if NKVH > 1:
                            issue_kv(1)

                        tok_exp = {}
                        tok_pv = {}
                        tok_norm = [(None, 0), (None, 0)]
                        tok_bias_done = [(None, 0), (None, 0)]
                        tok_qn_read = [(None, 0), (None, 0)]
                        ptfree = [(None, 0)] * 4

                        tok_bias_arr = {}
                        tok_qm_arr = {}

                        def emit_bias(h):
                            p_ = h % 2
                            g = h // 4
                            w_qs = [(sv, tok_ropeq[h]), tok_qn_read[p_]]
                            if h < 2:
                                w_qs += [tok_outst, tok_outst2]
                            t_qs = R("vector", lambda e, h=h:
                                     e.tensor_tensor(
                                         out=qsq[:],
                                         in0=qT[:, h * CH:(h + 1) * CH],
                                         in1=qT[:, h * CH:(h + 1) * CH],
                                         op=ALU.mult),
                                     waits=w_qs, inc=sv)
                            t_qn = R("tensor", lambda e, p_=p_: e.matmul(
                                PS[6 + p_][0:1, :], ones_bf[:, :],
                                qsq[:], start=True, stop=True),
                                waits=[(sv, t_qs), pr[6 + p_]], inc=sp)
                            t_qm = R("vector", lambda e, p_=p_:
                                     e.reduce_max(
                                         out=stat_t[:, p_:p_ + 1],
                                         in_=PS[6 + p_][0:1, :],
                                         axis=AXX),
                                     waits=[(sp, t_qn)], inc=sv)
                            tok_qn_read[p_] = (sv, t_qm)
                            t_pr = R("vector", lambda e, g=g, p_=p_:
                                     e.tensor_tensor(
                                         out=stat_t[:, 2 + p_:3 + p_],
                                         in0=stat_t[:, p_:p_ + 1],
                                         in1=kmax2[:, g:g + 1],
                                         op=ALU.mult),
                                     waits=[(sv, t_qm),
                                            (sv, tok_knorm[g])], inc=sv)
                            t_l1 = R("scalar", lambda e, p_=p_:
                                     e.activation(
                                         out=stat_t[:, 2 + p_:3 + p_],
                                         in_=stat_t[:, 2 + p_:3 + p_],
                                         func=AF.Ln),
                                     waits=[(sv, t_pr)], inc=sa)
                            t_l2 = R("scalar", lambda e, p_=p_:
                                     e.activation(
                                         out=stat_t[:, 2 + p_:3 + p_],
                                         in_=stat_t[:, 2 + p_:3 + p_],
                                         func=AF.Exp, scale=0.5),
                                     waits=[(sa, t_l1)], inc=sa)
                            t_bb = R("vector", lambda e, p_=p_:
                                     e.tensor_scalar(
                                         out=stat_t[:, 2 + p_:3 + p_],
                                         in0=stat_t[:, 2 + p_:3 + p_],
                                         scalar1=-INVSQ, scalar2=-2.0,
                                         op0=ALU.mult, op1=ALU.add),
                                     waits=[(sa, t_l2)], inc=sv)
                            t_bm = R("tensor", lambda e, p_=p_: e.matmul(
                                PS[6 + p_][:, 0:1], ones_f[:, :],
                                stat_t[0:1, 2 + p_:3 + p_],
                                start=True, stop=True),
                                waits=[(sv, t_bb), (sv, t_qm),
                                       pr[6 + p_]], inc=sp)
                            t_bv = R("vector", lambda e, p_=p_:
                                     e.tensor_copy(
                                         out=bias_h[:, p_:p_ + 1],
                                         in_=PS[6 + p_][:, 0:1]),
                                     waits=[(sp, t_bm),
                                            tok_bias_done[p_]], inc=sv)
                            pr[6 + p_] = (sv, t_bv)
                            tok_bias_arr[h] = t_bv
                            tok_qm_arr[h] = t_qm

                        emit_bias(0)
                        for g in range(NKVH):
                            gb = g % 2
                            for hh in range(4):
                                h = 4 * g + hh
                                p_ = h % 2
                                if h + 1 < NQH:
                                    emit_bias(h + 1)
                                tok_bias = tok_bias_arr[h]
                                t_qm = tok_qm_arr[h]
                                SBK = (0, 1, 4, 5)
                                t_l_end = 0
                                t_s = {}
                                t_ready = {}
                                ngrp = (NKT + 3) // 4
                                for grp in range(ngrp + 1):
                                    if grp >= 1:
                                        for j in range(4 * (grp - 1),
                                                       min(4 * grp, NKT)):
                                            w_pv = [t_ready[j],
                                                    (s_kv[gb],
                                                     tok_vload[gb])]
                                            if j == 0:
                                                w_pv.append(pr[2 + p_])
                                            t_pv = R(
                                                "tensor",
                                                lambda e, j=j, gb=gb, p_=p_,
                                                NKT=NKT: e.matmul(
                                                    PS[2 + p_][:, :],
                                                    vbuf[gb][:, j * 128:
                                                             (j + 1) * 128],
                                                    PT[j % 4][:],
                                                    start=(j == 0),
                                                    stop=(j == NKT - 1)),
                                                waits=w_pv, inc=sp)
                                            tok_pv[(h, j)] = t_pv
                                            w_l = [t_ready[j]]
                                            if j == 0:
                                                w_l.append(pr[6 + p_])
                                                w_l.append((sv, t_qm))
                                            t_l_end = R(
                                                "tensor",
                                                lambda e, j=j, p_=p_,
                                                NKT=NKT: e.matmul(
                                                    PS[6 + p_][0:1, :],
                                                    ones_bf[:, :],
                                                    PT[j % 4][:],
                                                    start=(j == 0),
                                                    stop=(j == NKT - 1)),
                                                waits=w_l, inc=sp)
                                            ptfree[j % 4] = (sp, t_l_end)
                                    if grp < ngrp:
                                        for j in range(4 * grp,
                                                       min(4 * grp + 4, NKT)):
                                            sb_ = SBK[j % 4]
                                            waits_s = [
                                                (s_kv[gb], tok_vload[gb]),
                                                (sv, tok_ropeq[h]), pr[sb_]]
                                            t_s[j] = R(
                                                "tensor",
                                                lambda e, h=h, j=j, sb_=sb_,
                                                gb=gb: e.matmul(
                                                    PS[sb_][:, :],
                                                    kbuf[gb][:, j * 128:
                                                             (j + 1) * 128],
                                                    qT[:, h * CH:
                                                       (h + 1) * CH],
                                                    start=True, stop=True),
                                                waits=waits_s, inc=sp)
                                        for j in range(4 * grp,
                                                       min(4 * grp + 4, NKT)):
                                            sb_ = SBK[j % 4]
                                            w_exp = [(sp, t_s[j]),
                                                     (sv, tok_bias),
                                                     ptfree[j % 4]]
                                            if h < 2 and j < 4:
                                                w_exp.append((sv, tok_Av))
                                            t_e = R("scalar",
                                                    lambda e, sb_=sb_, j=j,
                                                    p_=p_: e.activation(
                                                        out=PT[j % 4][:],
                                                        in_=PS[sb_][:, :],
                                                        func=AF.Exp,
                                                        bias=bias_h[:,
                                                                    p_:p_ + 1],
                                                        scale=INVSQ),
                                                    waits=w_exp, inc=sa)
                                            pr[sb_] = (sa, t_e)
                                            tok_exp[(h, j)] = t_e
                                            t_ready[j] = (sa, t_e)
                                            if j >= 4 * c:
                                                jr = j - 4 * c
                                                t_mk = R(
                                                    "vector",
                                                    lambda e, j=j, jr=jr:
                                                    e.tensor_tensor(
                                                        out=PT[j % 4][:],
                                                        in0=PT[j % 4][:],
                                                        in1=masks[:,
                                                                  jr * CH:
                                                                  (jr + 1) *
                                                                  CH],
                                                        op=ALU.mult),
                                                    waits=[(sa, t_e)],
                                                    inc=sv)
                                                t_ready[j] = (sv, t_mk)
                                t_rc = R("vector", lambda e, p_=p_:
                                         e.reciprocal(
                                             out=stat_a[:, 0:CH],
                                             in_=PS[6 + p_][0:1, :]),
                                         waits=[(sp, t_l_end)], inc=sv)
                                pr[6 + p_] = (sv, t_rc)
                                t_rb = R("tensor", lambda e, p_=p_: e.matmul(
                                    PS[6 + p_][:, :], ones_f[:, :],
                                    stat_a[0:1, 0:CH], start=True, stop=True),
                                    waits=[(sv, t_rc), pr[6 + p_]], inc=sp)
                                t_re = R("vector", lambda e, p_=p_:
                                         e.tensor_copy(out=rcbuf[p_][:],
                                                       in_=PS[6 + p_][:, :]),
                                         waits=[(sp, t_rb)], inc=sv)
                                pr[6 + p_] = (sv, t_re)
                                t_nm = R("vector", lambda e, h=h, p_=p_:
                                         e.tensor_tensor(
                                             out=aoutT[:, h * CH:
                                                       (h + 1) * CH],
                                             in0=PS[2 + p_][:, :],
                                             in1=rcbuf[p_][:], op=ALU.mult),
                                         waits=[(sv, t_re),
                                                (sp, tok_pv[(h, NKT - 1)])],
                                         inc=sv)
                                pr[2 + p_] = (sv, t_nm)
                                tok_norm[p_] = (sv, t_nm)
                                tok_bias_done[p_] = (sa, tok_exp[(h, NKT - 1)])
                            tok_kbuf_free[gb] = (sp, sp.n)
                            tok_vbuf_free[gb] = (sp, sp.n)
                            if g + 2 < NKVH:
                                issue_kv(g + 2)
                        tok_qT_lastread = (sp, sp.n)
                        tok_Bact = (sa, sa.n)
                        tok_attn_done = [tok_norm[0], tok_norm[1]]

                    if True:
                        ybf_free = [(None, 0), (None, 0)]
                        ysq_free = [(None, 0), (None, 0)]
                        t_sum = t_ssq = 0
                        t_ytt = {}
                        for et in range(NDM):
                            b = et % 2
                            w_wo = [tok_wqbuf_free[b],
                                    (sd_pre, t_pre_sync),
                                    (s_cst[0], t_cst0), (s_cst[1], t_cst1)]
                            if et < 2:
                                w_wo += [tok_qT_lastread, tok_attn_done[0],
                                         tok_attn_done[1], tok_bias_done[0],
                                         tok_bias_done[1]]
                            t_wl = R("sync", lambda e, et=et, b=b:
                                     e.dma_start(out=wobuf[b][:],
                                                 in_=wos[et, :, :]),
                                     waits=w_wo, inc=s_qo[b], incn=16)
                            t_g = 0
                            for j in range(NDM):
                                w_mm = [(s_qo[b], t_wl)]
                                if j == 0:
                                    w_mm += [pr[b], tok_attn_done[0],
                                             tok_attn_done[1]]
                                t_g = R("tensor", lambda e, et=et, j=j, b=b:
                                        e.matmul(
                                            PS[b][:, :],
                                            wobuf[b][:, j * 128:(j + 1) * 128],
                                            aoutT[:, j * CH:(j + 1) * CH],
                                            start=(j == 0),
                                            stop=(j == NDM - 1)),
                                        waits=w_mm, inc=sp)
                            tok_wqbuf_free[b] = (sp, t_g)
                            t_y = R("vector", lambda e, et=et, b=b:
                                    e.tensor_tensor(
                                        out=xc[:, et * CH:(et + 1) * CH],
                                        in0=PS[b][:, :],
                                        in1=xc[:, et * CH:(et + 1) * CH],
                                        op=ALU.add),
                                    waits=[(sp, t_g)], inc=sv)
                            pr[b] = (sv, t_y)
                            t_ytt[et] = t_y
                            t_cb = R("vector", lambda e, et=et, b=b:
                                     e.tensor_copy(
                                         out=ybf[b][:],
                                         in_=xc[:, et * CH:(et + 1) * CH]),
                                     waits=[(sv, t_y), ybf_free[b]], inc=sv)
                            t_sum = R("tensor", lambda e, et=et, b=b:
                                      e.matmul(PS[2][0:1, :], ones_bf[:, :],
                                               ybf[b][:], start=(et == 0),
                                               stop=(et == NDM - 1)),
                                      waits=[(sv, t_cb)] +
                                      ([pr[2]] if et == 0 else []), inc=sp)
                            ybf_free[b] = (sp, t_sum)
                            t_sq2 = R("vector", lambda e, et=et, b=b:
                                      e.tensor_tensor(
                                          out=ysq[b][:], in0=ybf[b][:],
                                          in1=ybf[b][:], op=ALU.mult),
                                      waits=[(sv, t_cb), ysq_free[b]],
                                      inc=sv)
                            t_ssq = R("tensor", lambda e, et=et, b=b:
                                      e.matmul(PS[3][0:1, :], ones_bf[:, :],
                                               ysq[b][:], start=(et == 0),
                                               stop=(et == NDM - 1)),
                                      waits=[(sv, t_sq2)] +
                                      ([pr[3]] if et == 0 else []), inc=sp)
                            ysq_free[b] = (sp, t_ssq)

                        t_m = R("vector", lambda e: e.tensor_scalar(
                            out=stat_m[:, :], in0=PS[2][0:1, :],
                            scalar1=1.0 / DM, scalar2=None, op0=ALU.mult),
                            waits=[(sp, t_sum)], inc=sv)
                        pr[2] = (sv, t_m)
                        t_v = R("vector", lambda e: e.tensor_scalar(
                            out=stat_a[:, :], in0=PS[3][0:1, :],
                            scalar1=1.0 / DM, scalar2=None, op0=ALU.mult),
                            waits=[(sp, t_ssq)], inc=sv)
                        pr[3] = (sv, t_v)
                        t_t1 = R("vector", lambda e: e.tensor_tensor(
                            out=rcbuf[0][0:1, :], in0=stat_m[:, :],
                            in1=stat_m[:, :], op=ALU.mult),
                            waits=[(sv, t_v), (sv, t_m)], inc=sv)
                        t_t2 = R("vector", lambda e: e.tensor_tensor(
                            out=stat_a[:, :], in0=stat_a[:, :],
                            in1=rcbuf[0][0:1, :], op=ALU.subtract),
                            waits=[(sv, t_t1)], inc=sv)
                        t_t3 = R("scalar", lambda e: e.activation(
                            out=stat_a[:, :], in_=stat_a[:, :], func=AF.Ln,
                            bias=epsl[0:1, :]), waits=[(sv, t_t2)], inc=sa)
                        t_t4 = R("scalar", lambda e: e.activation(
                            out=stat_a[:, :], in_=stat_a[:, :], func=AF.Exp,
                            scale=-0.5), waits=[(sa, t_t3)], inc=sa)
                        t_b1 = R("tensor", lambda e: e.matmul(
                            PS[4][:, :], ones_f[:, :], stat_m[0:1, :],
                            start=True, stop=True),
                            waits=[(sv, t_m), pr[4]], inc=sp)
                        t_b2 = R("tensor", lambda e: e.matmul(
                            PS[5][:, :], ones_f[:, :], stat_a[0:1, :],
                            start=True, stop=True),
                            waits=[(sa, t_t4), pr[5]], inc=sp)
                        t_ap = 0
                        tmp_free = [(None, 0), (None, 0)]
                        for et in range(NDM):
                            b = et % 2
                            t_a1 = R("vector", lambda e, et=et, b=b:
                                     e.tensor_tensor(
                                         out=rcbuf[b][:],
                                         in0=xc[:, et * CH:(et + 1) * CH],
                                         in1=PS[4][:, :], op=ALU.subtract),
                                     waits=[(sp, t_b1), (sv, t_ytt[et]),
                                            (sv, t_t2), tmp_free[b]], inc=sv)
                            t_a2 = R("vector", lambda e, et=et, b=b:
                                     e.tensor_tensor(
                                         out=xc[:, et * CH:(et + 1) * CH],
                                         in0=rcbuf[b][:], in1=PS[5][:, :],
                                         op=ALU.mult),
                                     waits=[(sv, t_a1), (sp, t_b2)], inc=sv)
                            tmp_free[b] = (sv, t_a2)
                            t_ap = R("vector", lambda e, et=et:
                                     e.tensor_copy(
                                         out=x1bf[:, et * CH:(et + 1) * CH],
                                         in_=xc[:, et * CH:(et + 1) * CH]),
                                     waits=[(sv, t_a2)], inc=sv)
                        pr[4] = (sv, t_ap)
                        pr[5] = (sv, t_ap)
                        t_sp = 0
                        for ii in range(4):
                            t_sp = R("scalar", lambda e, ii=ii, c=c:
                                     e.dma_start(
                                         out=x1scr[ii * 512:(ii + 1) * 512,
                                                   c * CH:(c + 1) * CH]
                                         .rearrange("(i p) t -> p i t",
                                                    p=128),
                                         in_=xc[:, ii * 4 * CH:
                                                (ii + 1) * 4 * CH]
                                         .rearrange("p (i t) -> p i t",
                                                    t=CH)),
                                     waits=[(sv, t_ap)], inc=s_spl, incn=16)
                        tok_x1spill[c] = (s_spl, t_sp)
                        tok_Cdone_pe = (sp, sp.n)
                        if c + 1 < NCH:
                            emit_Aprep(c + 1, [(s_spl, t_sp)])

                if True:
                    if True:
                        silu_free = [(None, 0), (None, 0)]
                        t_hid = 0
                        for r in range(NH):
                            b = r % 2
                            w_l = [tok_w13_free[b], (sd_pre, t_pre_sync),
                                   (s_cst[0], t_cst0), (s_cst[1], t_cst1)]
                            if r < 2:
                                w_l += [tok_qT_lastread, tok_Bact, tok_Elast,
                                        tok_attn_done[0], tok_attn_done[1]]
                            t_wh = [[0, 0], [0, 0]]
                            for (wi, wsrc_, wdst) in ((0, w1s, w1b),
                                                      (1, w3s, w3b)):
                                for hf in range(2):
                                    t_wh[wi][hf] = R(
                                        "sync",
                                        lambda e, r=r, hf=hf, wsrc_=wsrc_,
                                        wdst=wdst: e.dma_start(
                                            out=wdst[hf][:],
                                            in_=wsrc_[r, :,
                                                      hf * (DM // 2):
                                                      (hf + 1) * (DM // 2)]),
                                        waits=w_l + [tok_w13h_free[wi][hf]],
                                        inc=s_13h[wi][hf], incn=16)
                            t_gg = 0
                            t_ll = 0
                            for i in range(NDM):
                                hf = 0 if i < 8 else 1
                                w_mm = [(s_13h[0][hf], t_wh[0][hf])]
                                if i == 0:
                                    w_mm += [pr[0 + b], (sv, t_ap)]
                                t_gg = R("tensor",
                                         lambda e, r=r, i=i, b=b, hf=hf:
                                         e.matmul(
                                             PS[0 + b][:, :],
                                             w1b[hf][:, (i % 8) * 128:
                                                     (i % 8 + 1) * 128],
                                             x1bf[:, i * CH:(i + 1) * CH],
                                             start=(i == 0),
                                             stop=(i == NDM - 1)),
                                         waits=w_mm, inc=sp)
                                if i == 7 or i == NDM - 1:
                                    tok_w13h_free[0][hf] = (sp, t_gg)
                            for i in range(NDM):
                                hf = 0 if i < 8 else 1
                                w_mm = [(s_13h[1][hf], t_wh[1][hf])]
                                if i == 0:
                                    w_mm += [pr[2 + b], (sv, t_ap)]
                                t_ll = R("tensor",
                                         lambda e, r=r, i=i, b=b, hf=hf:
                                         e.matmul(
                                             PS[2 + b][:, :],
                                             w3b[hf][:, (i % 8) * 128:
                                                     (i % 8 + 1) * 128],
                                             x1bf[:, i * CH:(i + 1) * CH],
                                             start=(i == 0),
                                             stop=(i == NDM - 1)),
                                         waits=w_mm, inc=sp)
                                if i == 7 or i == NDM - 1:
                                    tok_w13h_free[1][hf] = (sp, t_ll)
                            tok_w13_free[b] = (sp, t_ll)
                            w_si = [(sp, t_gg), silu_free[b]]
                            if r < 2:
                                w_si += [tok_outst, tok_outst2, tok_Eapply,
                                         tok_attn_done[0], tok_attn_done[1]]
                            t_si = R("scalar", lambda e, b=b: e.activation(
                                out=silub[b][:], in_=PS[0 + b][:, :],
                                func=AF.Silu),
                                waits=w_si, inc=sa)
                            pr[0 + b] = (sa, t_si)
                            t_hid = R("vector", lambda e, r=r, b=b:
                                      e.tensor_tensor(
                                          out=hidden[:, r * CH:(r + 1) * CH],
                                          in0=silub[b][:],
                                          in1=PS[2 + b][:, :], op=ALU.mult),
                                      waits=[(sa, t_si), (sp, t_ll)], inc=sv)
                            pr[2 + b] = (sv, t_hid)
                            silu_free[b] = (sv, t_hid)
                        tok_hidden_all = t_hid

                    if True:
                        x1rd_free = [[tok_outst, tok_outst2],
                                     [tok_outst, tok_outst2]]
                        y2t_free = [(None, 0), (None, 0)]
                        y2bf_free = [(None, 0), (None, 0)]
                        y2sq_free = [(None, 0), (None, 0)]
                        t_sum = t_ssq = 0
                        t_y2 = {}
                        for jd in range(ND):
                            b = jd % 2
                            t_wq4 = [0] * 8

                            def issue_w2(q2, jd=jd):
                                bq2 = q2 % 2
                                w_lq = [tok_w2buf_free[bq2],
                                        (sd_pre, t_pre_sync)]
                                if jd < 2 and q2 < 2:
                                    w_lq += [(sv, tok_hidden_all)]
                                t_wq4[q2] = R(
                                    "sync", lambda e, jd=jd, q2=q2, bq2=bq2:
                                    e.dma_start(
                                        out=w2b[bq2][:],
                                        in_=w2p[jd, :, q2 * (HD // 8):
                                                (q2 + 1) * (HD // 8)]),
                                    waits=w_lq, inc=s_w2[bq2], incn=16)

                            issue_w2(0)
                            issue_w2(1)
                            w_xr = [tok_x1spill[c]] + \
                                (x1rd_free[b]
                                 if isinstance(x1rd_free[b], list)
                                 else [x1rd_free[b]])
                            if jd < 2:
                                w_xr += [(sv, tok_hidden_all)]
                            t_xr = R("sync", lambda e, jd=jd, c=c, b=b:
                                     e.dma_start(
                                         out=x1rd[b][:],
                                         in_=x1scr[jd * 128:(jd + 1) * 128,
                                                   c * CH:(c + 1) * CH]),
                                     waits=w_xr, inc=s_x1[b], incn=16)
                            t_g = 0
                            for r in range(NH):
                                q2 = r // 8
                                bq2 = q2 % 2
                                w_mm = [(s_w2[bq2], t_wq4[q2]),
                                        (sv, tok_hidden_all)]
                                if r == 0:
                                    w_mm += [pr[0 + b]]
                                rr = r % 8
                                t_g = R("tensor",
                                        lambda e, jd=jd, r=r, rr=rr, b=b,
                                        bq2=bq2: e.matmul(
                                            PS[0 + b][:, :],
                                            w2b[bq2][:, rr * 128:
                                                     (rr + 1) * 128],
                                            hidden[:, r * CH:(r + 1) * CH],
                                            start=(r == 0),
                                            stop=(r == NH - 1)),
                                        waits=w_mm, inc=sp)
                                if rr == 7:
                                    tok_w2buf_free[bq2] = (sp, t_g)
                                    if q2 + 2 < 8:
                                        issue_w2(q2 + 2)
                            w_yt = [(sp, t_g), (s_x1[b], t_xr),
                                    y2t_free[b]]
                            t_yt = R("vector", lambda e, jd=jd, b=b:
                                     e.tensor_tensor(
                                         out=y2t[b][:],
                                         in0=PS[0 + b][:, :], in1=x1rd[b][:],
                                         op=ALU.add),
                                     waits=w_yt, inc=sv)
                            pr[0 + b] = (sv, t_yt)
                            x1rd_free[b] = [(sv, t_yt)]
                            t_y2[jd] = t_yt
                            t_y2w = R("scalar", lambda e, jd=jd, c=c, b=b:
                                      e.dma_start(
                                          out=x1scr[jd * 128:(jd + 1) * 128,
                                                    c * CH:(c + 1) * CH],
                                          in_=y2t[b][:]),
                                      waits=[(sv, t_yt)], inc=s_y2w[b],
                                      incn=16)
                            y2t_free[b] = (s_y2w[b], t_y2w)
                            t_cb = R("vector", lambda e, jd=jd, b=b:
                                     e.tensor_copy(
                                         out=y2bf[b][:],
                                         in_=y2t[b][:]),
                                     waits=[(sv, t_yt), y2bf_free[b],
                                            y2sq_free[b]],
                                     inc=sv)
                            t_sum = R("tensor", lambda e, jd=jd, b=b:
                                      e.matmul(PS[2][0:1, :], ones_bf[:, :],
                                               y2bf[b][:], start=(jd == 0),
                                               stop=(jd == ND - 1)),
                                      waits=[(sv, t_cb)] +
                                      ([pr[2]] if jd == 0 else []), inc=sp)

                            t_sq2 = R("vector", lambda e, jd=jd, b=b:
                                      e.tensor_tensor(
                                          out=y2sq[b][:], in0=y2bf[b][:],
                                          in1=y2bf[b][:], op=ALU.mult),
                                      waits=[(sv, t_cb), y2sq_free[b],
                                             (sp, t_sum)],
                                      inc=sv)
                            t_ssq = R("tensor", lambda e, jd=jd, b=b:
                                      e.matmul(PS[3][0:1, :], ones_bf[:, :],
                                               y2sq[b][:], start=(jd == 0),
                                               stop=(jd == ND - 1)),
                                      waits=[(sv, t_sq2)] +
                                      ([pr[3]] if jd == 0 else []), inc=sp)
                            y2sq_free[b] = (sp, t_ssq)
                        tok_y2w_all = (s_y2w[0].n, s_y2w[1].n)

                        t_m = R("vector", lambda e: e.tensor_scalar(
                            out=stat_m[:, :], in0=PS[2][0:1, :],
                            scalar1=1.0 / DM, scalar2=None, op0=ALU.mult),
                            waits=[(sp, t_sum)], inc=sv)
                        pr[2] = (sv, t_m)
                        t_v = R("vector", lambda e: e.tensor_scalar(
                            out=stat_a[:, :], in0=PS[3][0:1, :],
                            scalar1=1.0 / DM, scalar2=None, op0=ALU.mult),
                            waits=[(sp, t_ssq)], inc=sv)
                        pr[3] = (sv, t_v)
                        t_t1 = R("vector", lambda e: e.tensor_tensor(
                            out=rcbuf[0][0:1, :], in0=stat_m[:, :],
                            in1=stat_m[:, :], op=ALU.mult),
                            waits=[(sv, t_m), tmp_free[0]], inc=sv)
                        t_t2 = R("vector", lambda e: e.tensor_tensor(
                            out=stat_a[:, :], in0=stat_a[:, :],
                            in1=rcbuf[0][0:1, :], op=ALU.subtract),
                            waits=[(sv, t_t1)], inc=sv)
                        t_t3 = R("scalar", lambda e: e.activation(
                            out=stat_a[:, :], in_=stat_a[:, :], func=AF.Ln,
                            bias=epsl[0:1, :]), waits=[(sv, t_t2)], inc=sa)
                        t_t4 = R("scalar", lambda e: e.activation(
                            out=stat_a[:, :], in_=stat_a[:, :], func=AF.Exp,
                            scale=-0.5), waits=[(sa, t_t3)], inc=sa)
                        t_b1 = R("tensor", lambda e: e.matmul(
                            PS[4][:, :], ones_f[:, :], stat_m[0:1, :],
                            start=True, stop=True),
                            waits=[(sv, t_m), pr[4]], inc=sp)
                        t_b2 = R("tensor", lambda e: e.matmul(
                            PS[5][:, :], ones_f[:, :], stat_a[0:1, :],
                            start=True, stop=True),
                            waits=[(sa, t_t4), pr[5]], inc=sp)
                        t_ap2 = 0
                        tmp2_free = [(None, 0), (None, 0)]
                        outb_free = [(None, 0), (None, 0)]
                        y2rd_free = [[(sv, t_y2[ND - 2])],
                                     [(sv, t_y2[ND - 1])]]
                        t_st = 0
                        for jd in range(ND):
                            b = jd % 2
                            t_rr = R("sync", lambda e, jd=jd, c=c, b=b:
                                     e.dma_start(
                                         out=x1rd[b][:],
                                         in_=x1scr[jd * 128:(jd + 1) * 128,
                                                   c * CH:(c + 1) * CH]),
                                     waits=[(s_y2w[0], tok_y2w_all[0]),
                                            (s_y2w[1], tok_y2w_all[1])] +
                                     y2rd_free[b], inc=s_y2r[b], incn=16)
                            t_a1 = R("vector", lambda e, jd=jd, b=b:
                                     e.tensor_tensor(
                                         out=rcbuf[b][:],
                                         in0=x1rd[b][:],
                                         in1=PS[4][:, :], op=ALU.subtract),
                                     waits=[(sp, t_b1), (s_y2r[b], t_rr),
                                            (sv, t_t2), tmp2_free[b]],
                                     inc=sv)
                            t_a2 = R("vector", lambda e, jd=jd, b=b:
                                     e.tensor_tensor(
                                         out=outb[b][:], in0=rcbuf[b][:],
                                         in1=PS[5][:, :], op=ALU.mult),
                                     waits=[(sv, t_a1), (sp, t_b2),
                                            outb_free[b], y2t_free[b]],
                                     inc=sv)
                            tmp2_free[b] = (sv, t_a2)
                            y2rd_free[b] = [(sv, t_a1)]
                            t_ap2 = t_a2
                            t_st = R("scalar", lambda e, jd=jd, c=c, b=b:
                                     e.dma_start(
                                         out=outT[jd * 128:(jd + 1) * 128,
                                                  c * CH:(c + 1) * CH],
                                         in_=outb[b][:]),
                                     waits=[(sv, t_a2)], inc=s_out[b],
                                     incn=16)
                            outb_free[b] = (s_out[b], t_st)
                            y2t_free[b] = (s_out[b], t_st)
                        pr[4] = (sv, t_ap2)
                        pr[5] = (sv, t_ap2)
                        tok_Elast = (sp, sp.n)
                        tok_Eapply = (sv, sv.n)
                        tok_outst = (s_out[0], s_out[0].n)
                        tok_outst2 = (s_out[1], s_out[1].n)

        def final_waits(e):
            for s in ([s_kvst, s_spl, sd_x, sd_pre, sg_res, sa, sv, sq,
                       s_wff] + s_w2st + s_y2w + sp_ring + s_y2r
                      + s_out + sg_pre + s_qo + s_13h[0] + s_13h[1] + s_w2
                      + s_kv + s_x1 + s_cst):
                if s.n > 0:
                    e.wait_ge(s.h, s.n)
            return e.nop()
        R("sync", final_waits)

        block = stk.enter_context(nc.Block())
        rec.emit(block)

    return nc


_NC_CACHE = {}


def _get_program():
    if "nc" not in _NC_CACHE:
        _NC_CACHE["nc"] = build_program()
    return _NC_CACHE["nc"]


def _prep_shared(inputs):
    f32 = np.float32
    qp = np.asarray(inputs["L0_attn_q_project"], f32)
    kp = np.asarray(inputs["L0_attn_k_project"], f32)
    vp = np.asarray(inputs["L0_attn_v_project"], f32)
    op = np.asarray(inputs["L0_attn_o_project"], f32)
    W1 = np.asarray(inputs["L0_ffn_W1"], f32)
    W2 = np.asarray(inputs["L0_ffn_W2"], f32)
    W3 = np.asarray(inputs["L0_ffn_W3"], f32)

    def lhsT_layout(w, nt):
        o, i = w.shape
        return np.ascontiguousarray(
            w.reshape(nt, 128, i // 128, 128).transpose(0, 3, 2, 1)
            .reshape(nt, 128, i))

    d = {
        "wq": lhsT_layout(qp, NQH),
        "wkh": lhsT_layout(kp, NKVH),
        "wvh": np.ascontiguousarray(
            vp.reshape(KV, NDM, 128).transpose(2, 1, 0)),
        "wo": lhsT_layout(op, NDM),
        "w1": lhsT_layout(W1, NH),
        "w3": lhsT_layout(W3, NH),
        "w2sl": np.ascontiguousarray(
            W2.reshape(ND, 128, NH, 128).transpose(0, 3, 2, 1)
            .reshape(ND, 128, NH * 128)),
        "bqh": np.ascontiguousarray(
            np.asarray(inputs["L0_attn_q_bias"], f32).reshape(NQH, 128).T),
        "bkh": np.ascontiguousarray(
            np.asarray(inputs["L0_attn_k_bias"], f32).reshape(NKVH, 128).T),
        "bvr": np.ascontiguousarray(
            np.asarray(inputs["L0_attn_v_bias"], f32).reshape(1, KV)),
        "lnwh": np.ascontiguousarray(
            np.asarray(inputs["L0_attn_ln_weight"], f32).reshape(NDM, 128).T),
    }
    return d


def kernel(**inputs):
    x = np.asarray(inputs["L0_attn_x"], np.float32)
    nb = x.shape[0]
    shared = _prep_shared(inputs)
    nc = _get_program()
    in_maps = []
    for b in range(nb):
        m = dict(shared)
        m["xT"] = np.ascontiguousarray(x[b].T)
        in_maps.append(m)
    res = run_bass_kernel_spmd(nc, in_maps, core_ids=list(range(nb)))
    out = np.stack([np.asarray(r["outT"]).T for r in res.results], axis=0)
    return out
